# version 08
# speedup vs baseline: 2.4645x; 1.0522x over previous
import numpy as np
from contextlib import ExitStack

import concourse.bass as bass
import concourse.mybir as mybir
from concourse.bass_utils import run_bass_kernel_spmd

F32 = mybir.dt.float32
BF16 = mybir.dt.bfloat16
ACTF = mybir.ActivationFunctionType
ALU = mybir.AluOpType
AX = mybir.AxisListType.X

B, S, D = 4, 4096, 1024
TOK = 2048
TOKF = 4096
NCH = 8
H, DH = 16, 64
NW, WSZ = 8, 256
NE, EHD = 16, 256
NT128 = TOK // 128
NT128F = TOKF // 128
EPS = 1e-5
MASKBIG = 1.0e6

ENGINES = ("tensor", "vector", "scalar", "gpsimd", "sync")


class Chan:
    __slots__ = ("sem", "n", "max_waited", "guarded")

    def __init__(self, sem):
        self.sem = sem
        self.n = 0
        self.max_waited = 0
        self.guarded = 0


class Sched:
    def __init__(self, nc, stack):
        self.nc = nc
        self.stack = stack
        self.ops = {e: [] for e in ENGINES}
        self._nsem = 0

    def chan(self, name=None):
        self._nsem += 1
        sem = self.stack.enter_context(self.nc.semaphore(name or f"ch{self._nsem}"))
        return Chan(sem)

    def emit(self, eng, f):
        self.ops[eng].append(f)

    def wait(self, eng, ch, val=None):
        tgt = ch.n if val is None else val
        if tgt > 0:
            ch.max_waited = max(ch.max_waited, tgt)
            self.ops[eng].append(lambda e, s=ch.sem, tv=tgt: e.wait_ge(s, tv))

    def inc(self, eng, f, ch, k=1):
        if k == 16 and ch.max_waited > ch.guarded:
            tgt = ch.max_waited
            self.ops[eng].append(lambda e, s=ch.sem, tv=tgt: e.wait_ge(s, tv))
            ch.guarded = tgt
        self.ops[eng].append(lambda e, ff=f, s=ch.sem, kk=k: ff(e).then_inc(s, kk))
        ch.n += k

    def bounce(self, cons_eng, prod_ch, via_eng, via_ch):
        self.wait(via_eng, prod_ch)
        self.inc(via_eng, lambda e: e.nop(), via_ch, 1)
        self.wait(cons_eng, via_ch)

    def barrier(self):
        ch = self.chan()
        for e in ENGINES:
            self.ops[e].append(lambda eng, s=ch.sem: eng.nop().then_inc(s, 1))
        ch.n += 5
        tgt = ch.n
        for e in ENGINES:
            self.ops[e].append(lambda eng, s=ch.sem, t=tgt: eng.wait_ge(s, t))

    def replay(self):
        with self.nc.Block() as block:
            for ename in ENGINES:
                deco = getattr(block, ename)
                ops = self.ops[ename]

                def body(eng, ops=ops):
                    for f in ops:
                        f(eng)

                deco(body)


def build_program(tap=None, race_check=False, upto=99, ph1_imp=3, ph1_topk=True):
    nc = bass.Bass("TRN2", target_bir_lowering=False,
                   detect_race_conditions=race_check)

    def din(name, shape):
        return nc.dram_tensor(name, list(shape), F32, kind="ExternalInput").ap()

    x_own = din("x_own", (TOK, D))
    x_sib = din("x_sib", (TOK, D))
    Wq = din("Wq", (D, D)); bq = din("bq", (D,))
    Wk = din("Wk", (D, D)); bk = din("bk", (D,))
    Wv = din("Wv", (D, D)); bv = din("bv", (D,))
    Wo = din("Wo", (D, D)); bo = din("bo", (D,))
    Ws1 = din("Ws1", (D, D // 2)); bs1 = din("bs1", (D // 2,))
    Ws2 = din("Ws2", (D // 2, 1))
    Wqkv = din("Wqkv", (3 * D, D)); bqkv = din("bqkv", (3 * D,))
    Wob = din("Wob", (D, D)); bob = din("bob", (D,))
    esel = din("esel", (NE, D))
    keys = din("keys", (NE, D, EHD))
    values = din("values", (NE, EHD, D))
    rsc = din("rsc", (1,))

    out_h = nc.dram_tensor("out", [TOK, D], F32, kind="ExternalOutput").ap()
    dbg_h = None
    if tap is not None:
        dbg_h = nc.dram_tensor("dbg", [128, 8192], F32, kind="ExternalOutput").ap()

    wq_c = nc.dram_tensor("wq_c", [H, D, DH], BF16, kind="Internal").ap()
    wk_c = nc.dram_tensor("wk_c", [H, D, DH], BF16, kind="Internal").ap()
    wv_c = nc.dram_tensor("wv_c", [H, D, DH], BF16, kind="Internal").ap()
    wo_c = nc.dram_tensor("wo_c", [D, D], BF16, kind="Internal").ap()
    wqkvT_c = nc.dram_tensor("wqkvT_c", [24, NCH, 128, 128], BF16, kind="Internal").ap()
    woTb_c = nc.dram_tensor("woTb_c", [NCH, 128, D], BF16, kind="Internal").ap()
    keys_c = nc.dram_tensor("keys_c", [NCH, 128, NE * EHD], BF16, kind="Internal").ap()
    vals_c = nc.dram_tensor("vals_c", [NE * EHD, D], BF16, kind="Internal").ap()
    scr512 = nc.dram_tensor("scr512", [512], F32, kind="Internal").ap()
    scr16 = nc.dram_tensor("scr16", [16], F32, kind="Internal").ap()

    stack = ExitStack()
    sc = Sched(nc, stack)

    def sb(name, shape, dt):
        return stack.enter_context(nc.sbuf_tensor(name, list(shape), dt))

    xres = sb("xres", (128, NT128, D), F32)
    xnF = sb("xnF", (128, NCH, TOK), BF16)
    es_sib = ExitStack()
    es_far = ExitStack()
    bar = sb("bar_", (128, TOKF), BF16)
    w_all = sb("w_all", (128, NT128, NE), F32)
    ones_bf = sb("ones_bf", (128, 128), BF16)
    ident_bf = sb("ident_bf", (128, 128), BF16)
    ident_f = sb("ident_f", (128, 128), F32)
    bqcol = sb("bqcol", (128, NCH), F32)
    bkcol = sb("bkcol", (128, NCH), F32)
    bqkvcol = sb("bqkvcol", (128, 24), F32)
    bvbc = sb("bvbc", (128, D), BF16)
    bs1bc = sb("bs1bc", (128, D // 2), F32)
    ws2bc = sb("ws2bc", (128, D // 2), F32)
    cs1bc = sb("cs1bc", (128, D // 2), F32)
    csebc = sb("csebc", (128, NE), F32)
    rsccol = sb("rsccol", (128, 1), F32)
    eselT = sb("eselT", (128, NCH, NE), F32)
    mucol = sb("mucol", (128, NT128F), F32)
    rscol = sb("rscol", (128, NT128F), F32)
    mrcol = sb("mrcol", (128, NT128F), F32)
    impc = sb("impc", (128, 32), F32)
    impc2 = sb("impc2", (128, 32), F32)
    m16 = sb("m16", (128, 16), F32)
    epscol = sb("epscol", (128, 1), F32)
    sumsq = sb("sumsq", (128, 2), F32)
    musum = sb("musum", (128, 2), F32)
    varcol = sb("varcol", (128, 2), F32)
    stdcol = sb("stdcol", (128, 2), F32)
    xntok = sb("xntok", (128, D), F32)
    xntok2 = sb("xntok2", (128, D), F32)
    mrow = sb("mrow", (1, 16), BF16)
    negKb = sb("negKb", (128, 1), F32)
    rrow = sb("rrow", (65, 512), BF16)

    LD = sc.chan("LD")
    GL = sc.chan("GL")
    GC = sc.chan("GC")
    PE = sc.chan("PE")
    DV = sc.chan("DV")
    SA = sc.chan("SA")

    def mmacc(out_ap, pairs):
        n = len(pairs)
        for i, (l, r) in enumerate(pairs):
            st, sp = (i == 0), (i == n - 1)
            if sp:
                sc.inc("tensor", lambda e, ll=l, rr=r, s=st: e.matmul(
                    out_ap, ll, rr, start=s, stop=True), PE, 1)
            else:
                sc.emit("tensor", lambda e, ll=l, rr=r, s=st: e.matmul(
                    out_ap, ll, rr, start=s, stop=False))

    def attn_stream(kmat, q_ap, vmat, ebuf_t, bA, bB, patt_t, EXPc, AVCc,
                    av0_waits=None, exp_kw=None, mid_emit=None):
        LAG = 4
        ekw = exp_kw or {}
        exp_base = EXPc.n
        avc_base = AVCc.n

        def emit_qk(kc):
            need = exp_base + kc - 6 + 1
            if need > 0:
                sc.wait("tensor", EXPc, need)
            rc = kc % 6
            sg = bA if rc < 3 else bB
            o = (rc % 3) * 512
            sc.inc("tensor", lambda e, k=kc, s=sg, oo=o, q=q_ap: e.matmul(
                s[:, oo:oo + 512], kmat[0:65, 128 * k:128 * (k + 1)], q,
                start=True, stop=True), PE, 1)
            sc.wait("scalar", PE)
            need2 = avc_base + kc - 6 + 1
            if need2 > 0:
                sc.wait("scalar", AVCc, need2)
            sl = 0 if rc < 3 else 1
            sc.inc("scalar", lambda e, s=sg, sl_=sl, oo=o: e.activation(
                ebuf_t[:, sl_, oo:oo + 512], s[:, oo:oo + 512], ACTF.Exp,
                scale=0.125, **ekw), EXPc, 1)

        def emit_av(kc):
            rc = kc % 6
            sl = 0 if rc < 3 else 1
            o = (rc % 3) * 512
            sc.wait("tensor", EXPc, exp_base + kc + 1)
            if kc == 0 and av0_waits is not None:
                av0_waits()
            sc.inc("tensor", lambda e, k=kc, sl_=sl, oo=o: e.matmul(
                patt_t[0:65, 0:512], vmat[:, k, :], ebuf_t[:, sl_, oo:oo + 512],
                start=(k == 0), stop=(k == 31)), AVCc, 1)

        for kc in range(32):
            emit_qk(kc)
            if kc == 6 and mid_emit is not None:
                mid_emit()
            if kc >= LAG:
                emit_av(kc - LAG)
        for kc in range(32 - LAG, 32):
            emit_av(kc)

    for j in range(NT128):
        src = bass.AP(x_own.tensor, j * 128 * D, [[D, 128], [1, D]])
        sc.inc("sync", lambda e, jj=j, s=src: e.dma_start(xres[:, jj, :], s), LD, 16)

    sc.inc("vector", lambda e: e.memset(ones_bf[:, :], 1.0), DV)
    sc.inc("vector", lambda e: e.memset(epscol[:, :], EPS), DV)
    sc.inc("vector", lambda e: e.memset(ident_bf[:, :], 1.0), DV)
    sc.inc("vector", lambda e: e.memset(ident_f[:, :], 1.0), DV)
    sc.wait("gpsimd", DV)
    sc.inc("gpsimd", lambda e: e.affine_select(ident_bf[:, :], ident_bf[:, :], [[1, 128]],
                                               ALU.is_equal, 0.0, channel_multiplier=-1), GC, 1)
    sc.inc("gpsimd", lambda e: e.affine_select(ident_f[:, :], ident_f[:, :], [[1, 128]],
                                               ALU.is_equal, 0.0, channel_multiplier=-1), GC, 1)

    def bcast_col(dst, src_ap, n):
        return lambda e: e.dma_start(dst[:, 0:n], bass.AP(src_ap.tensor, 0, [[1, 128], [128, n]]),
                                 allow_slow_non_contiguous=True)

    def bcast_row(dst, src_ap, n):
        return lambda e: e.dma_start(dst[:, 0:n], bass.AP(src_ap.tensor, 0, [[0, 128], [1, n]]))

    for dst, srct, n in ((bqcol, bq, NCH), (bkcol, bk, NCH), (bqkvcol, bqkv, 24)):
        sc.inc("sync", bcast_col(dst, srct, n), LD, 16)
    for dst, srct, n in ((bs1bc, bs1, D // 2), (ws2bc, Ws2, D // 2), (rsccol, rsc, 1)):
        sc.inc("sync", bcast_row(dst, srct, n), LD, 16)
    sc.inc("gpsimd", bcast_row(bvbc, bv, D), GL, 16)

    sc.inc("gpsimd", lambda e: e.dma_start(wo_c[:, :], Wo[:, :]), GL, 16)
    sc.inc("gpsimd", lambda e: e.dma_start(
        bass.AP(vals_c.tensor, 0, [[D, NE * EHD], [1, D]]),
        bass.AP(values.tensor, 0, [[D, NE * EHD], [1, D]])), GL, 16)
    for e_i in range(NE):
        src = bass.AP(keys.tensor, e_i * D * EHD, [[EHD * 128, NCH], [EHD, 128], [1, EHD]])
        dst = bass.AP(keys_c.tensor, EHD * e_i,
                      [[128 * NE * EHD, NCH], [NE * EHD, 128], [1, EHD]])
        sc.inc("gpsimd", lambda e, s=src, d=dst: e.dma_start(d, s), GL, 16)

    with (nc.sbuf_tensor("tstage", [128, NCH, 128], BF16) as tstage,
          nc.psum_tensor("pp0", [128, 512], F32) as pp0,
          nc.psum_tensor("pp1", [128, 512], F32) as pp1):
        with nc.sbuf_tensor("wstage", [128, NCH, D], BF16) as wstage:
            for (wsrc, wdst) in ((Wq, wq_c), (Wk, wk_c), (Wv, wv_c)):
                for c in range(NCH):
                    src = bass.AP(wsrc.tensor, c * 128 * D, [[D, 128], [1, D]])
                    sc.inc("gpsimd", lambda e, cc=c, s=src: e.dma_start(wstage[:, cc, :], s), GL, 16)
                sc.wait("gpsimd", GL)
                for h in range(H):
                    src = bass.AP(wstage, DH * h, [[NCH * D, 128], [D, NCH], [1, DH]])
                    dst = bass.AP(wdst.tensor, h * D * DH, [[DH, 128], [128 * DH, NCH], [1, DH]])
                    sc.inc("gpsimd", lambda e, s=src, d=dst: e.dma_start(d, s), GL, 16)
                sc.wait("gpsimd", GL)
        from contextlib import ExitStack as _ES
        _prep_es = _ES()
        wstage2 = _prep_es.enter_context(nc.sbuf_tensor("wstage2", [128, NCH, D], F32))

        def transpose_rows_to_cache(n_chunks, row_src_fn, dst_fn):
            sc.wait("tensor", GC)
            sc.wait("sync", GL)
            for jr in range(n_chunks):
                sc.wait("sync", LD)
                sc.inc("sync", lambda e, s=row_src_fn(jr): e.dma_start(
                    bass.AP(wstage2, 0, [[NCH * D, 128], [128, NCH], [1, 128]]), s), LD, 16)
                sc.wait("tensor", LD)
                base_dv = DV.n
                for c in range(NCH):
                    bank = pp0 if (c % 2 == 0) else pp1
                    if c >= 2:
                        sc.wait("tensor", DV, base_dv + c - 1)
                    sc.inc("tensor", lambda e, cc=c, b=bank: e.transpose(
                        b[:, 0:128],
                        bass.AP(wstage2, cc * 128, [[NCH * D, 128], [1, 128]]),
                        ident_f[:, :]), PE, 1)
                    sc.wait("vector", PE)
                    sc.inc("vector", lambda e, cc=c, b=bank: e.tensor_copy(
                        tstage[:, cc, :], b[:, 0:128]), DV, 1)
                sc.wait("sync", DV)
                sc.inc("sync", lambda e, d=dst_fn(jr): e.dma_start(
                    d, bass.AP(tstage, 0, [[NCH * 128, 128], [128, NCH], [1, 128]])), LD, 16)

        transpose_rows_to_cache(
            24,
            lambda jr: bass.AP(Wqkv.tensor, jr * 128 * D, [[D, 128], [1, D]]),
            lambda jr: bass.AP(wqkvT_c.tensor, jr * NCH * 128 * 128,
                               [[128, 128], [128 * 128, NCH], [1, 128]]))
        transpose_rows_to_cache(
            NCH,
            lambda jr: bass.AP(Wob.tensor, jr * 128 * D, [[D, 128], [1, D]]),
            lambda jr: bass.AP(woTb_c.tensor, jr * 128, [[D, 128], [128 * D, NCH], [1, 128]]))

        with nc.sbuf_tensor("estage", [16, D], F32) as estage, \
             nc.sbuf_tensor("onef", [128, 1], F32) as onef:
            sc.inc("sync", lambda e: e.dma_start(estage[:, :], esel[:, :]), LD, 16)
            sc.wait("tensor", LD)
            base_dv = DV.n
            for c in range(NCH):
                bank = pp0 if (c % 2 == 0) else pp1
                if c >= 2:
                    sc.wait("tensor", DV, base_dv + c - 1)
                sc.inc("tensor", lambda e, cc=c, b=bank: e.transpose(
                    b[0:128, 0:16], estage[:, cc * 128:(cc + 1) * 128],
                    ident_f[0:16, 0:16]), PE, 1)
                sc.wait("vector", PE)
                sc.inc("vector", lambda e, cc=c, b=bank: e.tensor_copy(
                    eselT[:, cc, :], b[0:128, 0:16]), DV, 1)
            sc.inc("vector", lambda e: e.memset(onef[:, :], 1.0), DV)
            sc.wait("tensor", DV)
            mmacc(pp0[0:1, 0:16], [(onef[:, :], eselT[:, c, :]) for c in range(NCH)])
            sc.wait("vector", PE)
            sc.inc("vector", lambda e: e.tensor_copy(estage[0:1, 0:16], pp0[0:1, 0:16]), DV, 1)
            sc.wait("sync", DV)
            sc.inc("sync", lambda e: e.dma_start(scr16[:], estage[0:1, 0:16]), LD, 16)
            sc.wait("sync", LD)
            sc.inc("sync", lambda e: e.dma_start(
                csebc[:, :], bass.AP(scr16.tensor, 0, [[0, 128], [1, NE]])), LD, 16)

        _prep_es.close()

    sibF = es_sib.enter_context(nc.sbuf_tensor("sibF", [128, NCH, TOK], BF16))
    far = es_far.enter_context(nc.sbuf_tensor("far", [128, TOKF + 64], F32))

    sc.barrier()

    def ln_pair(xts, jfs, pbanks, extra_fn=None):
        xnbuf = (xntok, xntok2)
        sc.wait("scalar", DV)
        for i, (xt, jf) in enumerate(zip(xts, jfs)):
            sc.inc("scalar", lambda e, x=xt, ii=i: e.activation(
                xnbuf[ii][:, :], x, ACTF.Square, accum_out=sumsq[:, ii:ii + 1]), SA, 1)
            sc.inc("vector", lambda e, x=xt, ii=i: e.reduce_sum(
                musum[:, ii:ii + 1], x, axis=AX), DV, 1)
        sc.wait("scalar", DV)
        for i, (xt, jf) in enumerate(zip(xts, jfs)):
            sc.inc("scalar", lambda e, f=jf, ii=i: e.activation(
                mucol[:, f:f + 1], musum[:, ii:ii + 1], ACTF.Copy, scale=1.0 / D), SA, 1)
            sc.inc("scalar", lambda e, ii=i: e.activation(
                varcol[:, ii:ii + 1], musum[:, ii:ii + 1], ACTF.Square,
                scale=1.0 / D), SA, 1)
        sc.wait("vector", SA)
        for i in range(len(xts)):
            sc.inc("vector", lambda e, ii=i: e.scalar_tensor_tensor(
                varcol[:, ii:ii + 1], sumsq[:, ii:ii + 1], 1.0 / D, varcol[:, ii:ii + 1],
                op0=ALU.mult, op1=ALU.subtract), DV, 1)
        sc.wait("scalar", DV)
        for i in range(len(xts)):
            sc.inc("scalar", lambda e, ii=i: e.activation(
                stdcol[:, ii:ii + 1], varcol[:, ii:ii + 1], ACTF.Sqrt,
                bias=epscol[:, :]), SA, 1)
        sc.wait("vector", SA)
        for i, (xt, jf) in enumerate(zip(xts, jfs)):
            sc.inc("vector", lambda e, f=jf, ii=i: e.reciprocal(
                rscol[:, f:f + 1], stdcol[:, ii:ii + 1]), DV, 1)
        sc.bounce("vector", DV, "scalar", SA)
        for i, (xt, jf) in enumerate(zip(xts, jfs)):
            sc.inc("vector", lambda e, f=jf: e.tensor_tensor(
                mrcol[:, f:f + 1], mucol[:, f:f + 1], rscol[:, f:f + 1],
                op=ALU.mult), DV, 1)
            sc.inc("vector", lambda e, x=xt, f=jf, ii=i: e.tensor_scalar(
                xnbuf[ii][:, :], x, mucol[:, f:f + 1], rscol[:, f:f + 1],
                op0=ALU.subtract, op1=ALU.mult), DV, 1)
        for i, (xt, jf) in enumerate(zip(xts, jfs)):
            sc.wait("tensor", DV)
            if jf < NT128:
                base_dv = DV.n
                nb = len(pbanks)
                for c in range(NCH):
                    bank = pbanks[c % nb]
                    if c >= nb:
                        sc.wait("tensor", DV, base_dv + c - nb + 1)
                    sc.inc("tensor", lambda e, cc=c, b=bank, ii=i: e.transpose(
                        b[:, 0:128], xnbuf[ii][:, cc * 128:(cc + 1) * 128],
                        ident_f[:, :]), PE, 1)
                    sc.wait("vector", PE)
                    sc.inc("vector", lambda e, cc=c, b=bank, f=jf: e.tensor_copy(
                        xnF[:, cc, 128 * f:128 * (f + 1)], b[:, 0:128]), DV, 1)
            if extra_fn is not None:
                extra_fn(jf, xt, i)

    def xf_stage_fp32(xt, xFst, pbanks):
        sc.wait("tensor", DV)
        base_dv = DV.n
        nb = len(pbanks)
        for c in range(NCH):
            bank = pbanks[c % nb]
            if c >= nb:
                sc.wait("tensor", DV, base_dv + c - nb + 1)
            src = bass.AP(xt.tensor, xt.offset + c * 128, [[xt.ap[0][0], 128], [1, 128]])
            sc.inc("tensor", lambda e, s=src, b=bank: e.transpose(
                b[:, 0:128], s, ident_f[:, :]), PE, 1)
            sc.wait("vector", PE)
            sc.inc("vector", lambda e, cc=c, b=bank: e.tensor_copy(
                xFst[:, cc, :], b[:, 0:128]), DV, 1)
        sc.wait("tensor", DV)

    if upto >= 1:
      ph1_es = ExitStack()
      ph1_es2 = ExitStack()
      ws1sb = ph1_es.enter_context(nc.sbuf_tensor("ws1sb", [128, NCH, D // 2], F32))
      xsib_stage = ph1_es.enter_context(nc.sbuf_tensor("xsib_stage", [128, 3, D], F32))
      xFst = ph1_es.enter_context(nc.sbuf_tensor("xFst", [128, NCH, 128], F32))
      z1buf = ph1_es.enter_context(nc.sbuf_tensor("z1buf", [128, D // 2], F32))
      with (nc.psum_tensor("pt0", [128, 512], F32) as pt0,
          nc.psum_tensor("pt1", [128, 512], F32) as pt1,
          nc.psum_tensor("pt2", [128, 512], F32) as pt2,
          nc.psum_tensor("pt3", [128, 512], F32) as pt3,
          nc.psum_tensor("pz", [128, 512], F32) as pz):

        for c in range(NCH):
            src = bass.AP(Ws1.tensor, c * 128 * (D // 2), [[D // 2, 128], [1, D // 2]])
            sc.inc("sync", lambda e, cc=c, s=src: e.dma_start(ws1sb[:, cc, :], s), LD, 16)
        with nc.sbuf_tensor("onef2", [128, 1], F32) as onef2:
            sc.inc("vector", lambda e: e.memset(onef2[:, :], 1.0), DV)
            sc.wait("tensor", LD)
            sc.wait("tensor", DV)
            mmacc(pz[0:1, 0:512], [(onef2[:, :], ws1sb[:, c, :]) for c in range(NCH)])
            sc.wait("vector", PE)
            sc.inc("vector", lambda e: e.tensor_copy(far[0:1, 0:512], pz[0:1, 0:512]), DV, 1)
            sc.wait("sync", DV)
            sc.inc("sync", lambda e: e.dma_start(scr512[:], far[0:1, 0:512]), LD, 16)
            sc.wait("sync", LD)
            sc.inc("sync", lambda e: e.dma_start(
                cs1bc[:, :], bass.AP(scr512.tensor, 0, [[0, 128], [1, 512]])), LD, 16)
            sc.wait("vector", LD)
            sc.wait("tensor", LD)

        ph1_banks = [pt0, pt1, pt2, pt3]

        def imp_extra(jf, xt, i=0):
            xf_stage_fp32(xt, xFst, ph1_banks)
            if ph1_imp < 2:
                return
            mmacc(pz[:, 0:512], [(xFst[:, c, :], ws1sb[:, c, :]) for c in range(NCH)])
            sc.wait("vector", PE)
            if ph1_imp < 3:
                sc.inc("vector", lambda e: e.tensor_copy(z1buf[:, :], pz[:, 0:512]), DV, 1)
                sc.wait("tensor", DV)
                return
            sc.inc("vector", lambda e, f=jf: e.scalar_tensor_tensor(
                z1buf[:, :], cs1bc[:, :], mrcol[:, f:f + 1], bs1bc[:, :],
                op0=ALU.mult, op1=ALU.subtract), DV, 1)
            sc.inc("vector", lambda e, f=jf: e.scalar_tensor_tensor(
                z1buf[:, :], pz[:, 0:512], rscol[:, f:f + 1], z1buf[:, :],
                op0=ALU.mult, op1=ALU.subtract), DV, 1)
            sc.inc("vector", lambda e: e.scalar_tensor_tensor(
                z1buf[:, :], z1buf[:, :], 0.0, ws2bc[:, :],
                op0=ALU.max, op1=ALU.mult), DV, 1)
            sc.inc("vector", lambda e, f=jf: e.reduce_sum(
                impc[:, f:f + 1], z1buf[:, :], axis=AX), DV, 1)
            sc.wait("tensor", DV)

        sc.wait("vector", LD)
        sc.wait("scalar", LD)
        sc.wait("tensor", LD)
        for j in range(0, NT128, 2):
            ln_pair((xres[:, j, :], xres[:, j + 1, :]), (j, j + 1), ph1_banks,
                    imp_extra if ph1_imp >= 1 else None)

        def sib_extra(jf, xt, i):
            js = jf - NT128
            xnb = xntok if i == 0 else xntok2
            base_dv = DV.n
            nb = len(ph1_banks)
            for c in range(NCH):
                bank = ph1_banks[c % nb]
                if c >= nb:
                    sc.wait("tensor", DV, base_dv + c - nb + 1)
                sc.inc("tensor", lambda e, cc=c, b=bank, xb=xnb: e.transpose(
                    b[:, 0:128], xb[:, cc * 128:(cc + 1) * 128], ident_f[:, :]), PE, 1)
                sc.wait("vector", PE)
                sc.inc("vector", lambda e, cc=c, b=bank, j=js: e.tensor_copy(
                    sibF[:, cc, 128 * j:128 * (j + 1)], b[:, 0:128]), DV, 1)
            imp_extra(jf, xt)

        sib_done = [(0, 0, 0)] * 3
        sib_ld_at = [0] * 3

        def sib_dma(j):
            s = j % 3
            dv0, pe0, sa0 = sib_done[s]
            sc.wait("sync", DV, dv0)
            sc.wait("sync", PE, pe0)
            sc.wait("sync", SA, sa0)
            src = bass.AP(x_sib.tensor, j * 128 * D, [[D, 128], [1, D]])
            sc.inc("sync", lambda e, sr=src, sl=s: e.dma_start(
                xsib_stage[:, sl, :], sr), LD, 16)
            sib_ld_at[s] = LD.n

        sib_dma(0)
        sib_dma(1)
        for j in range(0, NT128, 2):
            s0, s1 = j % 3, (j + 1) % 3
            if j + 2 < NT128:
                sib_dma(j + 2)
            for eng in ("vector", "scalar", "tensor"):
                sc.wait(eng, LD, max(sib_ld_at[s0], sib_ld_at[s1]))
            ln_pair((xsib_stage[:, s0, :], xsib_stage[:, s1, :]),
                    (NT128 + j, NT128 + j + 1), ph1_banks,
                    sib_extra if ph1_imp >= 1 else None)
            sib_done[s0] = (DV.n, PE.n, SA.n)
            sib_done[s1] = (DV.n, PE.n, SA.n)
            if j + 3 < NT128:
                sib_dma(j + 3)

        ph1_es.close()
        candbuf = ph1_es2.enter_context(nc.sbuf_tensor("candbuf", [128, 2304], F32))
        if not ph1_topk:
            sc.inc("vector", lambda e: e.memset(bar[0:1, 0:TOKF], 1.0), DV, 1)
        ph1_topk_gate = ph1_topk
        if ph1_topk_gate:
          sc.bounce("vector", DV, "scalar", SA)
          sc.inc("vector", lambda e: e.tensor_copy(impc2[:, :], impc[:, :]), DV, 1)
          sc.bounce("vector", DV, "scalar", SA)
          sc.inc("vector", lambda e: e.max(out=m16[:, 0:8], in_=impc2[:, :]), DV, 1)
          sc.bounce("vector", DV, "scalar", SA)
          sc.inc("vector", lambda e: e.match_replace(
            out=impc2[:, :], in_to_replace=m16[:, 0:8],
            in_values=impc2[:, :], imm_value=-1e30), DV, 1)
          sc.bounce("vector", DV, "scalar", SA)
          sc.inc("vector", lambda e: e.max(out=m16[:, 8:16], in_=impc2[:, :]), DV, 1)
          sc.wait("tensor", DV)
          sc.inc("tensor", lambda e: e.transpose(pt0[0:32, 0:128], impc[:, :],
                                               ident_f[:, :]), PE, 1)
          sc.wait("vector", PE)
          sc.inc("vector", lambda e: e.tensor_copy(candbuf[0:32, 2176:2304],
                                                 pt0[0:32, 0:128]), DV, 1)
          sc.wait("sync", DV)
          sc.inc("sync", lambda e: e.dma_start(
            far[0:1, 0:TOKF],
            bass.AP(candbuf, 2176, [[2304, 32], [1, 128]])), LD, 16)
          sc.inc("sync", lambda e: e.dma_start(
            candbuf[0:1, 0:2048],
            bass.AP(m16, 0, [[16, 128], [1, 16]])), LD, 16)
          sc.wait("vector", LD)
          sc.inc("vector", lambda e: e.max(out=candbuf[0:1, 2048:2056], in_=candbuf[0:1, 0:2048]), DV, 1)
          sc.bounce("vector", DV, "scalar", SA)
          sc.inc("vector", lambda e: e.match_replace(
            out=candbuf[0:1, 0:2048], in_to_replace=candbuf[0:1, 2048:2056],
            in_values=candbuf[0:1, 0:2048], imm_value=-1e30), DV, 1)
          sc.bounce("vector", DV, "scalar", SA)
          sc.inc("vector", lambda e: e.max(out=candbuf[0:1, 2056:2064], in_=candbuf[0:1, 0:2048]), DV, 1)
          sc.bounce("vector", DV, "scalar", SA)
          sc.inc("vector", lambda e: e.tensor_copy(
            far[0:1, TOKF + 8:TOKF + 9], candbuf[0:1, 2063:2064]), DV, 1)
          sc.bounce("vector", DV, "scalar", SA)
          sc.inc("vector", lambda e: e.tensor_scalar(
            bar[0:1, 0:TOKF], far[0:1, 0:TOKF], far[0:1, TOKF + 8:TOKF + 9], None,
            op0=ALU.is_lt), DV, 1)
        ph1_es2.close()

    sc.barrier()

    if tap == "ph1":
        TP = sc.chan("TP")
        if ph1_imp >= 3:
            sc.inc("sync", lambda e: e.dma_start(dbg_h[0:128, 0:32], impc[:, :]), TP, 16)
        if ph1_topk:
            sc.inc("sync", lambda e: e.dma_start(dbg_h[0:1, 100:101],
                                                 far[0:1, TOKF + 8:TOKF + 9]), TP, 16)
        sc.inc("sync", lambda e: e.dma_start(dbg_h[0:128, 256:288], mucol[:, 0:32]), TP, 16)
        sc.inc("sync", lambda e: e.dma_start(dbg_h[0:128, 288:320], rscol[:, 0:32]), TP, 16)
        if ph1_topk:
            TPg = sc.chan("TPg")
            sc.inc("gpsimd", lambda e: e.dma_start(dbg_h[0:1, 320:320 + 4096],
                                                   bar[0:1, 0:TOKF]), TPg, 16)
            sc.inc("sync", lambda e: e.dma_start(dbg_h[1:2, 320:320 + 4096],
                                                 far[0:1, 0:TOKF]), TP, 16)
            sc.wait("gpsimd", TPg)
        sc.wait("sync", TP)

    es_far.close()

    if upto >= 2:
      with (nc.sbuf_tensor("whq", [128, 2, NCH, DH], BF16) as whq,
          nc.sbuf_tensor("whk", [128, 2, NCH, DH], BF16) as whk,
          nc.sbuf_tensor("whv", [128, 2, NCH, DH], BF16) as whv,
          nc.sbuf_tensor("khat", [65, TOKF], BF16) as khat,
          nc.sbuf_tensor("qhat", [65, TOK], BF16) as qhat,
          nc.sbuf_tensor("vhat", [128, 32, 65], BF16) as vhat,
          nc.sbuf_tensor("attp", [128, TOK], BF16) as attp,
          nc.sbuf_tensor("ostage", [64, TOK], BF16) as ostage,
          nc.sbuf_tensor("ebuf", [128, 2, 1536], BF16) as ebuf,
          nc.sbuf_tensor("astage", [64, 512], BF16) as astage,
          nc.sbuf_tensor("woap", [128, D], BF16) as woap,
          nc.sbuf_tensor("boabc", [128, D], BF16) as boabc,
          nc.psum_tensor("sgA", [128, 1536], F32) as sgA,
          nc.psum_tensor("sgB", [128, 1536], F32) as sgB,
          nc.psum_tensor("patt", [128, 512], F32) as patt,
          nc.psum_tensor("prbc", [128, 512], F32) as prbc):

        WD = sc.chan("WD")
        SIBD = sc.chan("SIBD")
        UD = sc.chan("UD")
        MD = sc.chan("MD")
        OD = sc.chan("OD")
        EXP = sc.chan("EXP")
        AVC = sc.chan("AVC")
        kc_groups = [(0, 3), (3, 6), (6, 9), (9, 12), (12, 15), (15, 18), (18, 21),
                     (21, 24), (24, 27), (27, 30), (30, 32)]
        head_pe_done = 0
        pair_y_done = 0

        B2D = sc.chan("B2D")
        sc.inc("gpsimd", lambda e: e.dma_start(
            boabc[:, 0:D], bass.AP(bo.tensor, 0, [[0, 128], [1, D]])), B2D, 16)
        sc.wait("vector", B2D)
        hw_pe = [0, 0]
        hw_wd = [0, 0]

        def head_w_dma(h):
            s = h % 2
            sc.wait("sync", PE, hw_pe[s])
            for (cache, buf) in ((wq_c, whq), (wk_c, whk), (wv_c, whv)):
                src = bass.AP(cache.tensor, h * D * DH,
                              [[DH, 128], [128 * DH, NCH], [1, DH]])
                sc.inc("sync", lambda e, sr=src, b=buf, sl=s: e.dma_start(
                    bass.AP(b, sl * NCH * DH,
                            [[2 * NCH * DH, 128], [DH, NCH], [1, DH]]), sr), WD, 16)
            hw_wd[s] = WD.n

        head_w_dma(0)
        for h in range(H):
            hb = 64 * (h % 2)
            hc = h // 2
            hs = h % 2
            if h + 1 < H:
                head_w_dma(h + 1)
            sc.wait("tensor", WD, hw_wd[hs])
            if h % 2 == 1:
                sc.wait("sync", PE, pair_y_done)
                wsrc = bass.AP(wo_c.tensor, hc * 128 * D, [[D, 128], [1, D]])
                sc.inc("sync", lambda e, s=wsrc: e.dma_start(woap[:, :], s), WD, 16)
                woap_wd = WD.n

            sc.inc("vector", lambda e: e.memset(vhat[:, :, 64:65], 1.0), DV, 1)
            kbanks = [sgA[:, 0:512], sgA[:, 512:1024], sgA[:, 1024:1536],
                      sgB[:, 0:512]]
            vbanks = [sgB[:, 512:1024], sgB[:, 1024:1536]]
            k_ev = [0, 0, 0, 0]
            v_ev = [0, 0]
            for kt in range(8):
                def xn_sl(c, off, width, kt=kt):
                    if kt < 4:
                        return xnF[:, c, 512 * kt + off:512 * kt + off + width]
                    o = 512 * (kt - 4) + off
                    return sibF[:, c, o:o + width]

                ks = kt % 4
                if k_ev[ks]:
                    sc.wait("tensor", DV, k_ev[ks])
                if kt == 0:
                    sc.wait("tensor", DV)
                    if head_pe_done:
                        sc.wait("vector", PE, head_pe_done)
                mmacc(kbanks[ks][0:64, 0:512],
                      [(whk[:, hs, c, :], xn_sl(c, 0, 512)) for c in range(NCH)])
                sc.wait("vector", PE)
                sc.inc("vector", lambda e, k=kt, b=kbanks[ks]: e.tensor_scalar(
                    khat[0:64, 512 * k:512 * (k + 1)], b[0:64, 0:512],
                    bkcol[hb:hb + 64, hc:hc + 1], None, op0=ALU.add), DV, 1)
                k_ev[ks] = DV.n
                for q in range(4):
                    tt = kt * 4 + q
                    vsl = tt % 2
                    if v_ev[vsl]:
                        sc.wait("tensor", DV, v_ev[vsl])
                    mmacc(vbanks[vsl][:, 0:64],
                          [(xn_sl(c, 128 * q, 128), whv[:, hs, c, :]) for c in range(NCH)])
                    sc.wait("vector", PE)
                    sc.inc("vector", lambda e, t=tt, b=vbanks[vsl]: e.tensor_tensor(
                        vhat[:, t, 0:64], b[:, 0:64],
                        bvbc[:, DH * h:DH * h + DH], op=ALU.add), DV, 1)
                    v_ev[vsl] = DV.n
            sc.wait("gpsimd", DV)
            sc.inc("gpsimd", lambda e: e.dma_start(khat[64:65, :], bar[0:1, 0:TOKF]), UD, 16)
            sc.wait("vector", UD)
            sc.inc("vector", lambda e: e.tensor_scalar(
                khat[64:65, :], khat[64:65, :], -MASKBIG, None, op0=ALU.mult), DV, 1)

            for qt in range(4):
                ks = qt % 4
                if k_ev[ks]:
                    sc.wait("tensor", DV, k_ev[ks])
                mmacc(kbanks[ks][0:64, 0:512],
                      [(whq[:, hs, c, :], xnF[:, c, 512 * qt:512 * (qt + 1)])
                       for c in range(NCH)])
                sc.wait("vector", PE)
                sc.inc("vector", lambda e, q=qt, b=kbanks[ks]: e.tensor_scalar(
                    qhat[0:64, 512 * q:512 * (q + 1)], b[0:64, 0:512],
                    bqcol[hb:hb + 64, hc:hc + 1], None, op0=ALU.add), DV, 1)
                k_ev[ks] = DV.n
            hw_pe[hs] = PE.n
            sc.wait("gpsimd", DV)
            sc.inc("gpsimd", lambda e: e.dma_start(qhat[64:65, :], bar[0:1, 0:TOK]), UD, 16)

            sc.wait("tensor", UD)
            sc.wait("tensor", DV)
            sc.wait("tensor", SA)
            def mk_norm(qt):
                def emit():
                    sc.wait("tensor", DV)
                    sc.inc("tensor", lambda e: e.matmul(
                        prbc[0:64, 0:512], ones_bf[64:65, 0:64], rrow[64:65, 0:512],
                        start=True, stop=True), PE, 1)
                    sc.wait("vector", PE)
                    if h % 2 == 0:
                        dst = attp[0:64, 512 * qt:512 * (qt + 1)]
                    else:
                        dst = ostage[0:64, 512 * qt:512 * (qt + 1)]
                    if pair_y_done:
                        sc.wait("vector", PE, pair_y_done)
                    if h % 2 == 1 and qt == 0:
                        sc.wait("vector", OD)
                    sc.inc("vector", lambda e, d=dst: e.tensor_tensor(
                        d, astage[:, :], prbc[0:64, 0:512], op=ALU.mult), DV, 1)
                return emit

            pending = None
            for qt in range(4):
                q_ap = qhat[0:65, 512 * qt:512 * (qt + 1)]

                def av0w():
                    sc.wait("tensor", SA)
                    sc.wait("tensor", DV)
                attn_stream(khat, q_ap, vhat, ebuf, sgA, sgB, patt, EXP, AVC,
                            av0_waits=av0w, mid_emit=pending)
                sc.wait("vector", AVC)
                sc.inc("vector", lambda e: e.reciprocal(xntok[64:65, 0:512],
                                                        patt[64:65, 0:512]), DV, 1)
                sc.inc("vector", lambda e: e.tensor_copy(rrow[64:65, 0:512],
                                                         xntok[64:65, 0:512]), DV, 1)
                sc.inc("vector", lambda e: e.tensor_copy(astage[:, :],
                                                         patt[0:64, 0:512]), DV, 1)
                pending = mk_norm(qt)
            pending()
            head_pe_done = PE.n

            if h % 2 == 1:
                sc.wait("gpsimd", DV)
                sc.inc("gpsimd", lambda e: e.dma_start(
                    bass.AP(attp, 64 * TOK, [[TOK, 64], [1, TOK]]),
                    bass.AP(ostage, 0, [[TOK, 64], [1, TOK]])), OD, 16)
                sc.wait("tensor", OD)
                sc.wait("tensor", WD, woap_wd)
                sc.wait("tensor", DV)
                dv_base = DV.n
                i = 0
                for tt in range(NT128):
                    for jh in range(2):
                        bank = sgA if (i % 2 == 0) else sgB
                        if i >= 2:
                            sc.wait("tensor", DV, dv_base + i - 1)
                        sc.inc("tensor", lambda e, t=tt, j=jh, b=bank: e.matmul(
                            b[:, 0:512], attp[:, 128 * t:128 * (t + 1)],
                            woap[:, 512 * j:512 * (j + 1)], start=True, stop=True), PE, 1)
                        sc.wait("vector", PE)
                        sc.inc("vector", lambda e, t=tt, j=jh, b=bank: e.tensor_tensor(
                            xres[:, t, 512 * j:512 * (j + 1)],
                            xres[:, t, 512 * j:512 * (j + 1)], b[:, 0:512], op=ALU.add), DV, 1)
                        i += 1
                pair_y_done = PE.n

        for tt in range(NT128):
            sc.inc("vector", lambda e, t=tt: e.tensor_tensor(
                xres[:, t, :], xres[:, t, :], boabc[:, :], op=ALU.add), DV, 1)

    sc.barrier()
    es_sib.close()

    if tap == "ph2":
        TP = sc.chan("TP")
        for t in range(8):
            sc.inc("sync", lambda e, tt=t: e.dma_start(
                dbg_h[0:128, 1024 * tt:1024 * (tt + 1)], xres[:, tt, :]), TP, 16)
        sc.wait("sync", TP)

    def ffn_pass(tag):
        with (nc.sbuf_tensor("xFst2" + tag, [128, NCH, 128], F32) as xFst2,
              nc.sbuf_tensor("probs" + tag, [128, NE], F32) as probs,
              nc.sbuf_tensor("top8" + tag, [128, 8], F32) as top8,
              nc.sbuf_tensor("wsum" + tag, [128, NT128], F32) as wsum,
              nc.sbuf_tensor("wsr" + tag, [128, NT128], F32) as wsr,
              nc.sbuf_tensor("stok" + tag, [128, 2, 512], F32) as stok,
              nc.sbuf_tensor("sF" + tag, [128, 32, 512], BF16) as sF,
              nc.sbuf_tensor("kbuf" + tag, [128, 2, NCH, 512], BF16) as kbuf,
              nc.sbuf_tensor("vbuf" + tag, [128, 4, D], BF16) as vbuf,
              nc.psum_tensor("pfA" + tag, [128, 512], F32) as pfA,
              nc.psum_tensor("pfB" + tag, [128, 512], F32) as pfB,
              nc.psum_tensor("pfT0" + tag, [128, 512], F32) as pfT0,
              nc.psum_tensor("pfT1" + tag, [128, 512], F32) as pfT1,
              nc.psum_tensor("pval" + tag, [128, 4 * 512], F32) as pval):
            KD = sc.chan("KD" + tag)
            VD = sc.chan("VD" + tag)

            ffn_banks = [pfT0, pfT1, pval[:, 0:512], pval[:, 512:1024]]

            def route_extra(jf, xt, i=0):
                xf_stage_fp32(xt, xFst2, ffn_banks)
                mmacc(pfA[:, 0:16], [(xFst2[:, c, :], eselT[:, c, :]) for c in range(NCH)])
                sc.wait("scalar", DV)
                sc.inc("scalar", lambda e, f=jf: e.activation(
                    probs[:, :], csebc[:, :], ACTF.Copy, scale=mrcol[:, f:f + 1]), SA, 1)
                sc.wait("vector", PE)
                sc.wait("vector", SA)
                sc.inc("vector", lambda e, f=jf: e.scalar_tensor_tensor(
                    probs[:, :], pfA[:, 0:16], rscol[:, f:f + 1], probs[:, :],
                    op0=ALU.mult, op1=ALU.subtract), DV, 1)
                sc.wait("scalar", DV)
                sc.inc("scalar", lambda e: e.activation(probs[:, :], probs[:, :],
                                                        ACTF.Sigmoid), SA, 1)
                sc.wait("vector", SA)
                sc.inc("vector", lambda e: e.max(out=top8[:, :], in_=probs[:, :]), DV, 1)
                sc.bounce("vector", DV, "scalar", SA)
                sc.inc("vector", lambda e, j=jf: e.scalar_tensor_tensor(
                    w_all[:, j, :], probs[:, :], top8[:, 3:4], probs[:, :],
                    op0=ALU.is_ge, op1=ALU.mult), DV, 1)
                sc.inc("vector", lambda e, j=jf: e.reduce_sum(
                    wsum[:, j:j + 1], top8[:, 0:4], axis=AX), DV, 1)
                sc.wait("tensor", DV)

            sc.wait("tensor", DV)
            for j in range(0, NT128, 2):
                ln_pair((xres[:, j, :], xres[:, j + 1, :]), (j, j + 1),
                        ffn_banks, route_extra)

            sc.bounce("vector", DV, "scalar", SA)
            sc.inc("vector", lambda e: e.reciprocal(wsr[:, :], wsum[:, :]), DV, 1)
            sc.bounce("vector", DV, "scalar", SA)
            for j in range(NT128):
                sc.inc("vector", lambda e, jj=j: e.tensor_scalar(
                    w_all[:, jj, :], w_all[:, jj, :], wsr[:, jj:jj + 1], None,
                    op0=ALU.mult), DV, 1)

            kslot_pe = [0, 0]
            kd_at = [0, 0]

            def kbuf_dma(idx):
                t5x, eix = idx // 8, idx % 8
                s = idx % 2
                sc.wait("sync", PE, kslot_pe[s])
                for c in range(NCH):
                    src = bass.AP(keys_c.tensor, c * 128 * NE * EHD + eix * 512,
                                  [[NE * EHD, 128], [1, 512]])
                    sc.inc("sync", lambda e, cc=c, sr=src, sl=s: e.dma_start(
                        kbuf[:, sl, cc, :], sr), KD, 16)
                kd_at[s] = KD.n

            kbuf_dma(0)
            stok_ev = [0, 0]
            sF_ev = [0]
            pend_T = [None]
            for t5 in range(4):
                for ei in range(8):
                    kidx = t5 * 8 + ei
                    ks = kidx % 2
                    if kidx + 1 < 32:
                        kbuf_dma(kidx + 1)
                    sc.wait("tensor", KD, kd_at[ks])
                    for sub in range(4):
                        t1 = t5 * 4 + sub
                        bank = pfA if (sub % 2 == 0) else pfB
                        if stok_ev[sub % 2]:
                            sc.wait("tensor", DV, stok_ev[sub % 2])
                        mmacc(bank[:, 0:512],
                              [(xnF[:, c, 128 * t1:128 * (t1 + 1)], kbuf[:, ks, c, :])
                               for c in range(NCH)])
                        if pend_T[0] is not None:
                            pend_T[0]()
                            pend_T[0] = None
                        sc.wait("vector", PE)
                        e_lo = ei * 2
                        for eh in range(2):
                            sc.inc("vector", lambda e, b=bank, t=t1, el=e_lo, q=eh,
                                   ss=sub % 2: e.tensor_scalar(
                                       stok[:, ss, 256 * q:256 * (q + 1)],
                                       b[:, 256 * q:256 * (q + 1)],
                                       0.0, w_all[:, t, el + q:el + q + 1],
                                       op0=ALU.max, op1=ALU.mult), DV, 1)
                        stok_ev[sub % 2] = DV.n

                        def mk_T(sub=sub, ei=ei, need=DV.n, ss=sub % 2):
                            def emit():
                                sc.wait("tensor", DV, need)
                                if sF_ev[0]:
                                    sc.wait("tensor", DV, sF_ev[0])
                                for qq in range(4):
                                    g = ei * 4 + qq
                                    bank2 = ffn_banks[qq]
                                    sc.inc("tensor", lambda e, q=qq, b=bank2, s2=ss:
                                           e.transpose(
                                               b[:, 0:128],
                                               stok[:, s2, 128 * q:128 * (q + 1)],
                                               ident_f[:, :]), PE, 1)
                                    sc.wait("vector", PE)
                                    sc.inc("vector", lambda e, g_=g, b=bank2, s_=sub:
                                           e.tensor_copy(
                                               sF[:, g_, 128 * s_:128 * (s_ + 1)],
                                               b[:, 0:128]), DV, 1)
                                sF_ev[0] = DV.n
                            return emit
                        pend_T[0] = mk_T()
                    kslot_pe[ks] = PE.n
                if pend_T[0] is not None:
                    pend_T[0]()
                    pend_T[0] = None
                vslot_free = [0] * 4
                vd_at = [0] * 4

                def val_dma(g):
                    sl = g % 4
                    if vslot_free[sl]:
                        sc.wait("sync", PE, vslot_free[sl])
                    src = bass.AP(vals_c.tensor, g * 128 * D, [[D, 128], [1, D]])
                    sc.inc("sync", lambda e, sl_=sl, s=src: e.dma_start(
                        vbuf[:, sl_, :], s), VD, 16)
                    vd_at[sl] = VD.n

                val_dma(0)
                val_dma(1)
                for g in range(32):
                    slot = g % 4
                    if g + 2 < 32:
                        val_dma(g + 2)
                    sc.wait("tensor", VD, vd_at[slot])
                    if g == 0:
                        sc.wait("tensor", DV)
                    for sub in range(4):
                        for dh in range(2):
                            pi = sub if dh == 0 else None
                            st, sp = (g == 0), (g == 31)
                            if dh == 0:
                                outap = pval[:, 512 * sub:512 * (sub + 1)]
                            else:
                                outap = (pfA if sub == 0 else pfB)[:, 0:512] if sub < 2 else \
                                        (pfT0 if sub == 2 else pfT1)[:, 0:512]
                            lastf = (sub == 3 and dh == 1)
                            f = (lambda e, g_=g, sl=slot, s_=sub, d_=dh, o=outap,
                                 s0=st, s1=sp: e.matmul(
                                o, sF[:, g_, 128 * s_:128 * (s_ + 1)],
                                vbuf[:, sl, 512 * d_:512 * (d_ + 1)],
                                start=s0, stop=s1))
                            if lastf:
                                sc.inc("tensor", f, PE, 1)
                            else:
                                sc.emit("tensor", f)
                    vslot_free[slot] = PE.n
                sc.wait("vector", PE)
                for sub in range(4):
                    t1 = t5 * 4 + sub
                    for dh in range(2):
                        if dh == 0:
                            outap = pval[:, 512 * sub:512 * (sub + 1)]
                        else:
                            outap = (pfA if sub == 0 else pfB)[:, 0:512] if sub < 2 else \
                                    (pfT0 if sub == 2 else pfT1)[:, 0:512]
                        sc.inc("vector", lambda e, t=t1, d=dh, o=outap: e.scalar_tensor_tensor(
                            xres[:, t, 512 * d:512 * (d + 1)], o, rsccol[:, :],
                            xres[:, t, 512 * d:512 * (d + 1)],
                            op0=ALU.mult, op1=ALU.add), DV, 1)
                sc.wait("tensor", DV)

    if upto >= 3:
        ffn_pass("_a")
    sc.barrier()

    if tap == "ph3":
        TP = sc.chan("TP")
        for t in range(8):
            sc.inc("sync", lambda e, tt=t: e.dma_start(
                dbg_h[0:128, 1024 * tt:1024 * (tt + 1)], xres[:, tt, :]), TP, 16)
        sc.wait("sync", TP)

    if upto >= 4:
      with ExitStack() as ph4_es:
        _s = ph4_es.enter_context
        wqkvb = _s(nc.sbuf_tensor("wqkvb", [128, 2, NCH, 128], BF16))
        bqkvvbc = _s(nc.sbuf_tensor("bqkvvbc", [128, NE * DH], BF16))
        bobbc = _s(nc.sbuf_tensor("bobbc", [128, D], BF16))
        qkv_w = _s(nc.sbuf_tensor("qkv_w", [128, 24, WSZ], BF16))
        Qhb = _s(nc.sbuf_tensor("Qhb", [65, TOKF], BF16))
        Khb = _s(nc.sbuf_tensor("Khb", [65, TOKF], BF16))
        vhatb = _s(nc.sbuf_tensor("vhatb", [128, 32, 65], BF16))
        vals_w = _s(nc.sbuf_tensor("vals_w", [128, NCH, WSZ], BF16))
        ostb = _s(nc.sbuf_tensor("ostb", [64, TOK], BF16))
        ebufb = _s(nc.sbuf_tensor("ebufb", [128, 2, 1536], BF16))
        astb = _s(nc.sbuf_tensor("astb", [64, 512], BF16))
        wotb = _s(nc.sbuf_tensor("wotb", [128, NCH, 512], BF16))
        bgA = _s(nc.psum_tensor("bgA", [128, 1536], F32))
        bgB = _s(nc.psum_tensor("bgB", [128, 1536], F32))
        pattb = _s(nc.psum_tensor("pattb", [128, 512], F32))
        prbcb = _s(nc.psum_tensor("prbcb", [128, 512], F32))
        WDB = sc.chan("WDB")
        ODB = sc.chan("ODB")
        SDB = sc.chan("SDB")
        EXB = sc.chan("EXB")
        AVB = sc.chan("AVB")

        sc.wait("tensor", DV)
        for j in range(0, NT128, 2):
            ln_pair((xres[:, j, :], xres[:, j + 1, :]), (j, j + 1),
                    [bgA[:, 0:512], bgA[:, 512:1024],
                     bgB[:, 0:512], bgB[:, 512:1024]], None)

        B4D = sc.chan("B4D")
        sc.inc("gpsimd", lambda e: e.dma_start(
            bobbc[:, 0:D], bass.AP(bob.tensor, 0, [[0, 128], [1, D]])), B4D, 16)
        sc.inc("gpsimd", lambda e: e.dma_start(
            bqkvvbc[:, :], bass.AP(bqkv.tensor, 128, [[0, 128], [192, NE], [1, 64]])),
            B4D, 16)
        sc.wait("vector", B4D)
        sc.inc("vector", lambda e: e.memset(Khb[64:65, :], 1.0), DV, 1)
        sc.inc("vector", lambda e: e.memset(Qhb[64:65, :], 0.0), DV, 1)

        vslot = {(192 * c + 128) // 128: (c, (192 * c + 128) % 128) for c in range(NE)}
        kc_groups = [(0, 3), (3, 6), (6, 9), (9, 12), (12, 15), (15, 18), (18, 21),
                     (21, 24), (24, 27), (27, 30), (30, 32)]
        win_pe_done = 0
        slot_pe = [0, 0]
        wdb_at = [0, 0]

        def qkv_dma(jr):
            s = jr % 2
            if slot_pe[s]:
                sc.wait("sync", PE, slot_pe[s])
            src = bass.AP(wqkvT_c.tensor, jr * NCH * 128 * 128,
                          [[128, 128], [128 * 128, NCH], [1, 128]])
            sc.inc("sync", lambda e, sr=src, sl=s: e.dma_start(
                bass.AP(wqkvb, sl * NCH * 128,
                        [[2 * NCH * 128, 128], [128, NCH], [1, 128]]), sr), WDB, 16)
            wdb_at[s] = WDB.n

        qkv_dma(0)
        for w in range(NW):
            wc0 = WSZ * w
            sc.wait("tensor", SA)
            sc.wait("tensor", DV)
            sc.inc("vector", lambda e: e.memset(vhatb[:, :, 64:65], 1.0), DV, 1)
            qkv_ev = [0, 0]
            vh_ev = [0, 0]
            for jr in range(24):
                s = jr % 2
                if jr + 1 < 24:
                    qkv_dma(jr + 1)
                elif w + 1 < NW:
                    qkv_dma(0)
                sc.wait("tensor", WDB, wdb_at[s])
                bank = bgA[:, 512:1024] if (jr % 2 == 0) else bgB[:, 512:1024]
                if qkv_ev[s]:
                    sc.wait("tensor", DV, qkv_ev[s])
                if win_pe_done and jr == 0:
                    sc.wait("vector", PE, win_pe_done)
                mmacc(bank[:, 0:WSZ],
                      [(wqkvb[:, s, c, :], xnF[:, c, wc0:wc0 + WSZ]) for c in range(NCH)])
                sc.wait("vector", PE)
                sc.inc("vector", lambda e, j=jr, b=bank: e.tensor_scalar(
                    qkv_w[:, j, :], b[:, 0:WSZ], bqkvcol[:, j:j + 1], None,
                    op0=ALU.add), DV, 1)
                qkv_ev[s] = DV.n
                if jr in vslot:
                    cv, voff = vslot[jr]
                    for ah in range(2):
                        vb = pattb if ah == 0 else prbcb
                        if vh_ev[ah]:
                            sc.wait("tensor", DV, vh_ev[ah])
                        mmacc(vb[:, 0:64],
                              [(xnF[:, c, wc0 + 128 * ah:wc0 + 128 * ah + 128],
                                wqkvb[:, s, c, voff:voff + 64]) for c in range(NCH)])
                        sc.wait("vector", PE)
                        sc.inc("vector", lambda e, t=2 * cv + ah, b=vb, c_=cv: e.tensor_tensor(
                            vhatb[:, t, 0:64], b[:, 0:64],
                            bqkvvbc[:, 64 * c_:64 * c_ + 64], op=ALU.add), DV, 1)
                        vh_ev[ah] = DV.n
                slot_pe[s] = PE.n

            for kind, dstt in ((0, Qhb), (1, Khb)):
                for c in range(NE):
                    Bidx = 3 * c + kind
                    jc, base = Bidx // 2, 64 * (Bidx % 2)
                    if base == 0:
                        sc.inc("vector", lambda e, j=jc, c_=c, d=dstt: e.tensor_copy(
                            d[0:64, WSZ * c_:WSZ * (c_ + 1)], qkv_w[0:64, j, :]), DV, 1)
                if kind == 0:
                    jc0, cstart = 1, 1
                else:
                    jc0, cstart = 2, 0 if False else 2
                if kind == 0:
                    in_off = 64 * (24 * WSZ) + 1 * WSZ
                    out_off = WSZ * 1
                else:
                    in_off = 64 * (24 * WSZ) + 0 * WSZ
                    out_off = WSZ * 0
                sc.wait("gpsimd", DV)
                sc.inc("gpsimd", lambda e, io=in_off, oo=out_off, d=dstt: e.dma_start(
                    bass.AP(d, oo, [[TOKF, 64], [2 * WSZ, 8], [1, WSZ]]),
                    bass.AP(qkv_w, io, [[24 * WSZ, 64], [3 * WSZ, 8], [1, WSZ]])), SDB, 16)

            sc.wait("vector", SDB)
            nslots = [bgA[0:1, 0:1536], bgB[0:1, 0:1536],
                      pattb[0:1, 0:512], prbcb[0:1, 0:512]]
            for (mat, slot) in ((Khb, 2), (Qhb, 3)):
                sc.inc("vector", lambda e, m=mat: e.tensor_tensor(
                    bar[0:64, 0:TOKF], m[0:64, :], m[0:64, :], op=ALU.mult), DV, 1)
                sc.wait("tensor", DV)
                for kt in range(8):
                    if kt < 3:
                        dst = bgA[0:1, 512 * kt:512 * (kt + 1)]
                    elif kt < 6:
                        dst = bgB[0:1, 512 * (kt - 3):512 * (kt - 2)]
                    elif kt == 6:
                        dst = pattb[0:1, 0:512]
                    else:
                        dst = prbcb[0:1, 0:512]
                    sc.inc("tensor", lambda e, k=kt, d=dst: e.matmul(
                        d, ones_bf[0:64, 0:1],
                        bar[0:64, 512 * k:512 * (k + 1)], start=True, stop=True), PE, 1)
                sc.wait("vector", PE)
                for i, ns in enumerate(nslots):
                    sc.inc("vector", lambda e, s=ns, ii=i: e.reduce_max(
                        mrow[0:1, 8 + ii:9 + ii], s, axis=AX), DV, 1)
                sc.bounce("vector", DV, "scalar", SA)
                sc.inc("vector", lambda e, s=slot: e.reduce_max(
                    mrow[0:1, s:s + 1], mrow[0:1, 8:12], axis=AX), DV, 1)
                sc.wait("tensor", DV)
            sc.wait("scalar", DV)
            for slot in (2, 3):
                sc.inc("scalar", lambda e, s=slot: e.activation(
                    mrow[0:1, s:s + 1], mrow[0:1, s:s + 1], ACTF.Sqrt), SA, 1)
            sc.wait("vector", SA)
            sc.inc("vector", lambda e: e.scalar_tensor_tensor(
                mrow[0:1, 4:5], mrow[0:1, 2:3], -0.125,
                mrow[0:1, 3:4], op0=ALU.mult, op1=ALU.mult), DV, 1)
            sc.bounce("vector", DV, "scalar", SA)
            sc.wait("vector", EXB)
            sc.inc("vector", lambda e: e.tensor_scalar(
                mrow[0:1, 0:1], mrow[0:1, 4:5], 68.0, 0.0,
                op0=ALU.add, op1=ALU.min), DV, 1)
            sc.wait("tensor", DV)
            sc.inc("tensor", lambda e: e.matmul(
                prbcb[:, 0:1], ones_bf[0:1, 0:128], mrow[0:1, 0:1],
                start=True, stop=True), PE, 1)
            sc.wait("vector", PE)
            sc.inc("vector", lambda e: e.tensor_copy(negKb[:, :], prbcb[:, 0:1]), DV, 1)
            sc.wait("tensor", SDB)
            sc.wait("tensor", DV)
            sc.wait("tensor", SA)
            sc.wait("scalar", DV)
            def mk_normb(qt):
                def emit():
                    sc.wait("tensor", DV)
                    sc.inc("tensor", lambda e: e.matmul(
                        prbcb[0:64, 0:512], ones_bf[64:65, 0:64], rrow[64:65, 0:512],
                        start=True, stop=True), PE, 1)
                    sc.wait("vector", PE)
                    if win_pe_done:
                        sc.wait("vector", PE, win_pe_done)
                    sc.inc("vector", lambda e, j=qt: e.tensor_tensor(
                        vals_w[0:64, j, :], astb[:, 0:256], prbcb[0:64, 0:256],
                        op=ALU.mult), DV, 1)
                    if qt == 0:
                        sc.wait("vector", ODB)
                    sc.inc("vector", lambda e, j=qt: e.tensor_tensor(
                        ostb[0:64, WSZ * j:WSZ * (j + 1)], astb[:, 256:512],
                        prbcb[0:64, 256:512], op=ALU.mult), DV, 1)
                return emit

            pendingb = None
            for qt in range(8):
                q_ap = Qhb[0:65, 512 * qt:512 * (qt + 1)]

                def av0wb():
                    sc.wait("tensor", SA)
                    sc.wait("tensor", DV)
                attn_stream(Khb, q_ap, vhatb, ebufb, bgA, bgB, pattb, EXB, AVB,
                            av0_waits=av0wb, exp_kw={"bias": negKb[:, :]},
                            mid_emit=pendingb)
                sc.wait("vector", AVB)
                sc.inc("vector", lambda e: e.reciprocal(xntok[64:65, 0:512],
                                                        pattb[64:65, 0:512]), DV, 1)
                sc.inc("vector", lambda e: e.tensor_copy(rrow[64:65, 0:512],
                                                         xntok[64:65, 0:512]), DV, 1)
                sc.inc("vector", lambda e: e.tensor_copy(astb[:, :],
                                                         pattb[0:64, 0:512]), DV, 1)
                pendingb = mk_normb(qt)
            pendingb()
            sc.wait("gpsimd", DV)
            sc.inc("gpsimd", lambda e: e.dma_start(
                bass.AP(vals_w, 64 * NCH * WSZ, [[NCH * WSZ, 64], [WSZ, NCH], [1, WSZ]]),
                bass.AP(ostb, 0, [[TOK, 64], [WSZ, NCH], [1, WSZ]])), ODB, 16)
            sc.wait("tensor", ODB)
            sc.wait("tensor", DV)
            i = 0
            for jh in range(2):
                sc.wait("sync", PE)
                src = bass.AP(woTb_c.tensor, jh * 512, [[D, 128], [128 * D, NCH], [1, 512]])
                sc.inc("sync", lambda e, s=src: e.dma_start(
                    bass.AP(wotb, 0, [[NCH * 512, 128], [512, NCH], [1, 512]]), s), WDB, 16)
                sc.wait("tensor", WDB)
                for tt in range(2):
                    gt = 2 * w + tt
                    bank = bgA[:, 0:512] if (i % 2 == 0) else bgB[:, 0:512]
                    sc.wait("tensor", DV)
                    mmacc(bank,
                          [(vals_w[:, c, 128 * tt:128 * (tt + 1)],
                            wotb[:, c, :]) for c in range(NCH)])
                    sc.wait("vector", PE)
                    sc.inc("vector", lambda e, t=gt, j=jh, b=bank: e.tensor_tensor(
                        xres[:, t, 512 * j:512 * (j + 1)], xres[:, t, 512 * j:512 * (j + 1)],
                        b, op=ALU.add), DV, 1)
                    i += 1
            win_pe_done = PE.n
        for tt in range(NT128):
            sc.inc("vector", lambda e, t=tt: e.tensor_tensor(
                xres[:, t, :], xres[:, t, :], bobbc[:, :], op=ALU.add), DV, 1)

    sc.barrier()

    if tap == "ph4":
        TP = sc.chan("TP")
        for t in range(8):
            sc.inc("sync", lambda e, tt=t: e.dma_start(
                dbg_h[0:128, 1024 * tt:1024 * (tt + 1)], xres[:, tt, :]), TP, 16)
        sc.wait("sync", TP)

    if upto >= 5:
        ffn_pass("_b")
    sc.barrier()

    sc.wait("sync", DV)
    ST = sc.chan("ST")
    for j in range(NT128):
        dst = bass.AP(out_h.tensor, j * 128 * D, [[D, 128], [1, D]])
        sc.inc("sync", lambda e, jj=j, d=dst: e.dma_start(d, xres[:, jj, :]), ST, 16)
    sc.wait("sync", ST)

    sc.replay()
    stack.close()
    return nc


WEIGHT_MAP = [
    ("Wq", "L0_attn_a_Wq"), ("bq", "L0_attn_a_bq"), ("Wk", "L0_attn_a_Wk"),
    ("bk", "L0_attn_a_bk"), ("Wv", "L0_attn_a_Wv"), ("bv", "L0_attn_a_bv"),
    ("Wo", "L0_attn_a_Wo"), ("bo", "L0_attn_a_bo"), ("Ws1", "L0_attn_a_Ws1"),
    ("bs1", "L0_attn_a_bs1"), ("Ws2", "L0_attn_a_Ws2"),
    ("Wqkv", "L0_attn_b_Wqkv"), ("bqkv", "L0_attn_b_bqkv"),
    ("Wob", "L0_attn_b_Wo"), ("bob", "L0_attn_b_bo"),
    ("esel", "L0_ffn_expert_sel"), ("keys", "L0_ffn_keys"),
    ("values", "L0_ffn_values"), ("rsc", "L0_ffn_route_scale"),
]


def make_in_maps(inputs):
    x = np.ascontiguousarray(np.asarray(inputs["L0_attn_a_x"], dtype=np.float32))
    in_maps = []
    for c in range(8):
        b, h = c // 2, c % 2
        m = {
            "x_own": np.ascontiguousarray(x[b, 2048 * h:2048 * (h + 1)]),
            "x_sib": np.ascontiguousarray(x[b, 2048 * (1 - h):2048 * (2 - h)]),
        }
        for dn, kn in WEIGHT_MAP:
            m[dn] = np.ascontiguousarray(np.asarray(inputs[kn], dtype=np.float32))
        in_maps.append(m)
    return in_maps


def kernel(**inputs):
    nc = build_program()
    in_maps = make_in_maps(inputs)
    res = run_bass_kernel_spmd(nc, in_maps, core_ids=list(range(8)))
    out = np.empty((B, S, D), dtype=np.float32)
    for c in range(8):
        b, h = c // 2, c % 2
        out[b, 2048 * h:2048 * (h + 1)] = res.results[c]["out"]
    return out


# version 09
# speedup vs baseline: 2.5567x; 1.0374x over previous
import numpy as np
from contextlib import ExitStack

import concourse.bass as bass
import concourse.mybir as mybir
from concourse.bass_utils import run_bass_kernel_spmd

F32 = mybir.dt.float32
BF16 = mybir.dt.bfloat16
ACTF = mybir.ActivationFunctionType
ALU = mybir.AluOpType
AX = mybir.AxisListType.X

B, S, D = 4, 4096, 1024
TOK = 2048
TOKF = 4096
NCH = 8
H, DH = 16, 64
NW, WSZ = 8, 256
NE, EHD = 16, 256
NT128 = TOK // 128
NT128F = TOKF // 128
EPS = 1e-5
MASKBIG = 1.0e6

ENGINES = ("tensor", "vector", "scalar", "gpsimd", "sync")


class Chan:
    __slots__ = ("sem", "n", "max_waited", "guarded")

    def __init__(self, sem):
        self.sem = sem
        self.n = 0
        self.max_waited = 0
        self.guarded = 0


class Sched:
    def __init__(self, nc, stack):
        self.nc = nc
        self.stack = stack
        self.ops = {e: [] for e in ENGINES}
        self._nsem = 0

    def chan(self, name=None):
        self._nsem += 1
        sem = self.stack.enter_context(self.nc.semaphore(name or f"ch{self._nsem}"))
        return Chan(sem)

    def emit(self, eng, f):
        self.ops[eng].append(f)

    def wait(self, eng, ch, val=None):
        tgt = ch.n if val is None else val
        if tgt > 0:
            ch.max_waited = max(ch.max_waited, tgt)
            self.ops[eng].append(lambda e, s=ch.sem, tv=tgt: e.wait_ge(s, tv))

    def inc(self, eng, f, ch, k=1):
        if k == 16 and ch.max_waited > ch.guarded:
            tgt = ch.max_waited
            self.ops[eng].append(lambda e, s=ch.sem, tv=tgt: e.wait_ge(s, tv))
            ch.guarded = tgt
        self.ops[eng].append(lambda e, ff=f, s=ch.sem, kk=k: ff(e).then_inc(s, kk))
        ch.n += k

    def bounce(self, cons_eng, prod_ch, via_eng, via_ch):
        self.wait(via_eng, prod_ch)
        self.inc(via_eng, lambda e: e.nop(), via_ch, 1)
        self.wait(cons_eng, via_ch)

    def barrier(self):
        ch = self.chan()
        for e in ENGINES:
            self.ops[e].append(lambda eng, s=ch.sem: eng.nop().then_inc(s, 1))
        ch.n += 5
        tgt = ch.n
        for e in ENGINES:
            self.ops[e].append(lambda eng, s=ch.sem, t=tgt: eng.wait_ge(s, t))

    def replay(self):
        with self.nc.Block() as block:
            for ename in ENGINES:
                deco = getattr(block, ename)
                ops = self.ops[ename]

                def body(eng, ops=ops):
                    for f in ops:
                        f(eng)

                deco(body)


def build_program(tap=None, race_check=False, upto=99, ph1_imp=3, ph1_topk=True):
    nc = bass.Bass("TRN2", target_bir_lowering=False,
                   detect_race_conditions=race_check)

    def din(name, shape):
        return nc.dram_tensor(name, list(shape), F32, kind="ExternalInput").ap()

    x_own = din("x_own", (TOK, D))
    x_sib = din("x_sib", (TOK, D))
    Wq = din("Wq", (D, D)); bq = din("bq", (D,))
    Wk = din("Wk", (D, D)); bk = din("bk", (D,))
    Wv = din("Wv", (D, D)); bv = din("bv", (D,))
    Wo = din("Wo", (D, D)); bo = din("bo", (D,))
    Ws1 = din("Ws1", (D, D // 2)); bs1 = din("bs1", (D // 2,))
    Ws2 = din("Ws2", (D // 2, 1))
    Wqkv = din("Wqkv", (3 * D, D)); bqkv = din("bqkv", (3 * D,))
    Wob = din("Wob", (D, D)); bob = din("bob", (D,))
    esel = din("esel", (NE, D))
    keys = din("keys", (NE, D, EHD))
    values = din("values", (NE, EHD, D))
    rsc = din("rsc", (1,))

    out_h = nc.dram_tensor("out", [TOK, D], F32, kind="ExternalOutput").ap()
    dbg_h = None
    if tap is not None:
        dbg_h = nc.dram_tensor("dbg", [128, 8192], F32, kind="ExternalOutput").ap()

    wq_c = nc.dram_tensor("wq_c", [H, D, DH], BF16, kind="Internal").ap()
    wk_c = nc.dram_tensor("wk_c", [H, D, DH], BF16, kind="Internal").ap()
    wv_c = nc.dram_tensor("wv_c", [H, D, DH], BF16, kind="Internal").ap()
    wo_c = nc.dram_tensor("wo_c", [D, D], BF16, kind="Internal").ap()
    wqkvT_c = nc.dram_tensor("wqkvT_c", [24, NCH, 128, 128], BF16, kind="Internal").ap()
    woTb_c = nc.dram_tensor("woTb_c", [NCH, 128, D], BF16, kind="Internal").ap()
    keys_c = nc.dram_tensor("keys_c", [NCH, 128, NE * EHD], BF16, kind="Internal").ap()
    vals_c = nc.dram_tensor("vals_c", [NE * EHD, D], BF16, kind="Internal").ap()
    scr512 = nc.dram_tensor("scr512", [512], F32, kind="Internal").ap()
    scr16 = nc.dram_tensor("scr16", [16], F32, kind="Internal").ap()

    stack = ExitStack()
    sc = Sched(nc, stack)

    def sb(name, shape, dt):
        return stack.enter_context(nc.sbuf_tensor(name, list(shape), dt))

    xres = sb("xres", (128, NT128, D), F32)
    xnF = sb("xnF", (128, NCH, TOK), BF16)
    es_sib = ExitStack()
    es_far = ExitStack()
    bar = sb("bar_", (128, TOKF), BF16)
    w_all = sb("w_all", (128, NT128, NE), F32)
    ones_bf = sb("ones_bf", (128, 128), BF16)
    ident_bf = sb("ident_bf", (128, 128), BF16)
    ident_f = sb("ident_f", (128, 128), F32)
    bqcol = sb("bqcol", (128, NCH), F32)
    bkcol = sb("bkcol", (128, NCH), F32)
    bqkvcol = sb("bqkvcol", (128, 24), F32)
    bvbc = sb("bvbc", (128, D), BF16)
    bs1bc = sb("bs1bc", (128, D // 2), F32)
    ws2bc = sb("ws2bc", (128, D // 2), F32)
    cs1bc = sb("cs1bc", (128, D // 2), F32)
    csebc = sb("csebc", (128, NE), F32)
    rsccol = sb("rsccol", (128, 1), F32)
    eselT = sb("eselT", (128, NCH, NE), F32)
    mucol = sb("mucol", (128, NT128F), F32)
    rscol = sb("rscol", (128, NT128F), F32)
    mrcol = sb("mrcol", (128, NT128F), F32)
    impc = sb("impc", (128, 32), F32)
    impc2 = sb("impc2", (128, 32), F32)
    m16 = sb("m16", (128, 16), F32)
    epscol = sb("epscol", (128, 1), F32)
    sumsq = sb("sumsq", (128, 2), F32)
    musum = sb("musum", (128, 2), F32)
    varcol = sb("varcol", (128, 2), F32)
    stdcol = sb("stdcol", (128, 2), F32)
    xntok = sb("xntok", (128, D), F32)
    xntok2 = sb("xntok2", (128, D), F32)
    mrow = sb("mrow", (1, 16), BF16)
    negKb = sb("negKb", (128, 1), F32)
    rrow = sb("rrow", (65, 512), BF16)

    LD = sc.chan("LD")
    GL = sc.chan("GL")
    GC = sc.chan("GC")
    PE = sc.chan("PE")
    DV = sc.chan("DV")
    SA = sc.chan("SA")

    def mmacc(out_ap, pairs):
        n = len(pairs)
        for i, (l, r) in enumerate(pairs):
            st, sp = (i == 0), (i == n - 1)
            if sp:
                sc.inc("tensor", lambda e, ll=l, rr=r, s=st: e.matmul(
                    out_ap, ll, rr, start=s, stop=True), PE, 1)
            else:
                sc.emit("tensor", lambda e, ll=l, rr=r, s=st: e.matmul(
                    out_ap, ll, rr, start=s, stop=False))

    def attn_stream(kmat, q_ap, vmat, ebuf_t, bA, bB, patt_t, EXPc, AVCc,
                    av0_waits=None, exp_kw=None, mid_emit=None):
        LAG = 5
        ekw = exp_kw or {}
        exp_base = EXPc.n
        avc_base = AVCc.n

        def emit_qk(kc):
            need = exp_base + kc - 6 + 1
            if need > 0:
                sc.wait("tensor", EXPc, need)
            rc = kc % 6
            sg = bA if rc < 3 else bB
            o = (rc % 3) * 512
            sc.inc("tensor", lambda e, k=kc, s=sg, oo=o, q=q_ap: e.matmul(
                s[:, oo:oo + 512], kmat[0:65, 128 * k:128 * (k + 1)], q,
                start=True, stop=True), PE, 1)
            sc.wait("scalar", PE)
            need2 = avc_base + kc - 6 + 1
            if need2 > 0:
                sc.wait("scalar", AVCc, need2)
            sl = 0 if rc < 3 else 1
            sc.inc("scalar", lambda e, s=sg, sl_=sl, oo=o: e.activation(
                ebuf_t[:, sl_, oo:oo + 512], s[:, oo:oo + 512], ACTF.Exp,
                scale=0.125, **ekw), EXPc, 1)

        def emit_av(kc):
            rc = kc % 6
            sl = 0 if rc < 3 else 1
            o = (rc % 3) * 512
            sc.wait("tensor", EXPc, exp_base + kc + 1)
            if kc == 0 and av0_waits is not None:
                av0_waits()
            sc.inc("tensor", lambda e, k=kc, sl_=sl, oo=o: e.matmul(
                patt_t[0:65, 0:512], vmat[:, k, :], ebuf_t[:, sl_, oo:oo + 512],
                start=(k == 0), stop=(k == 31)), AVCc, 1)

        for kc in range(32):
            emit_qk(kc)
            if kc == 4 and mid_emit is not None:
                mid_emit()
            if kc >= LAG:
                emit_av(kc - LAG)
        for kc in range(32 - LAG, 32):
            emit_av(kc)

    for j in range(NT128):
        src = bass.AP(x_own.tensor, j * 128 * D, [[D, 128], [1, D]])
        sc.inc("sync", lambda e, jj=j, s=src: e.dma_start(xres[:, jj, :], s), LD, 16)

    sc.inc("vector", lambda e: e.memset(ones_bf[:, :], 1.0), DV)
    sc.inc("vector", lambda e: e.memset(epscol[:, :], EPS), DV)
    sc.inc("vector", lambda e: e.memset(ident_bf[:, :], 1.0), DV)
    sc.inc("vector", lambda e: e.memset(ident_f[:, :], 1.0), DV)
    sc.wait("gpsimd", DV)
    sc.inc("gpsimd", lambda e: e.affine_select(ident_bf[:, :], ident_bf[:, :], [[1, 128]],
                                               ALU.is_equal, 0.0, channel_multiplier=-1), GC, 1)
    sc.inc("gpsimd", lambda e: e.affine_select(ident_f[:, :], ident_f[:, :], [[1, 128]],
                                               ALU.is_equal, 0.0, channel_multiplier=-1), GC, 1)

    def bcast_col(dst, src_ap, n):
        return lambda e: e.dma_start(dst[:, 0:n], bass.AP(src_ap.tensor, 0, [[1, 128], [128, n]]),
                                 allow_slow_non_contiguous=True)

    def bcast_row(dst, src_ap, n):
        return lambda e: e.dma_start(dst[:, 0:n], bass.AP(src_ap.tensor, 0, [[0, 128], [1, n]]))

    for dst, srct, n in ((bqcol, bq, NCH), (bkcol, bk, NCH), (bqkvcol, bqkv, 24)):
        sc.inc("sync", bcast_col(dst, srct, n), LD, 16)
    for dst, srct, n in ((bs1bc, bs1, D // 2), (ws2bc, Ws2, D // 2), (rsccol, rsc, 1)):
        sc.inc("sync", bcast_row(dst, srct, n), LD, 16)
    sc.inc("gpsimd", bcast_row(bvbc, bv, D), GL, 16)

    sc.inc("gpsimd", lambda e: e.dma_start(wo_c[:, :], Wo[:, :]), GL, 16)
    sc.inc("gpsimd", lambda e: e.dma_start(
        bass.AP(vals_c.tensor, 0, [[D, NE * EHD], [1, D]]),
        bass.AP(values.tensor, 0, [[D, NE * EHD], [1, D]])), GL, 16)
    for e_i in range(NE):
        src = bass.AP(keys.tensor, e_i * D * EHD, [[EHD * 128, NCH], [EHD, 128], [1, EHD]])
        dst = bass.AP(keys_c.tensor, EHD * e_i,
                      [[128 * NE * EHD, NCH], [NE * EHD, 128], [1, EHD]])
        sc.inc("gpsimd", lambda e, s=src, d=dst: e.dma_start(d, s), GL, 16)

    with (nc.sbuf_tensor("tstage", [128, 2, NCH, 128], BF16) as tstage,
          nc.psum_tensor("pp0", [128, 512], F32) as pp0,
          nc.psum_tensor("pp1", [128, 512], F32) as pp1):
        with nc.sbuf_tensor("wstage", [128, NCH, D], BF16) as wstage:
            for (wsrc, wdst) in ((Wq, wq_c), (Wk, wk_c), (Wv, wv_c)):
                for c in range(NCH):
                    src = bass.AP(wsrc.tensor, c * 128 * D, [[D, 128], [1, D]])
                    sc.inc("gpsimd", lambda e, cc=c, s=src: e.dma_start(wstage[:, cc, :], s), GL, 16)
                sc.wait("gpsimd", GL)
                for h in range(H):
                    src = bass.AP(wstage, DH * h, [[NCH * D, 128], [D, NCH], [1, DH]])
                    dst = bass.AP(wdst.tensor, h * D * DH, [[DH, 128], [128 * DH, NCH], [1, DH]])
                    sc.inc("gpsimd", lambda e, s=src, d=dst: e.dma_start(d, s), GL, 16)
                sc.wait("gpsimd", GL)
        from contextlib import ExitStack as _ES
        _prep_es = _ES()
        wstage2 = _prep_es.enter_context(nc.sbuf_tensor("wstage2", [128, 2, NCH, D], F32))

        def transpose_rows_to_cache(n_chunks, row_src_fn, dst_fn):
            sc.wait("tensor", GC)
            sc.wait("sync", GL)
            w2_pe = [0, 0]
            w2_ld = [0, 0]
            ts_dv = [0, 0]

            def w2_dma(jr):
                s = jr % 2
                if w2_pe[s]:
                    sc.wait("sync", PE, w2_pe[s])
                sc.inc("sync", lambda e, sr=row_src_fn(jr), sl=s: e.dma_start(
                    bass.AP(wstage2, sl * NCH * D,
                            [[2 * NCH * D, 128], [128, NCH], [1, 128]]), sr), LD, 16)
                w2_ld[s] = LD.n

            w2_dma(0)
            ev_prev = 0
            st_ld = [0, 0]
            for jr in range(n_chunks):
                s = jr % 2
                if jr + 1 < n_chunks:
                    w2_dma(jr + 1)
                sc.wait("tensor", LD, w2_ld[s])
                if ev_prev:
                    sc.wait("tensor", DV, ev_prev)
                if st_ld[s]:
                    sc.wait("vector", LD, st_ld[s])
                base_dv = DV.n
                for c in range(NCH):
                    bank = pp0 if (c % 2 == 0) else pp1
                    if c >= 2:
                        sc.wait("tensor", DV, base_dv + c - 1)
                    sc.inc("tensor", lambda e, cc=c, b=bank, sl=s: e.transpose(
                        b[:, 0:128],
                        bass.AP(wstage2, sl * NCH * D + cc * 128,
                                [[2 * NCH * D, 128], [1, 128]]),
                        ident_f[:, :]), PE, 1)
                    sc.wait("vector", PE)
                    sc.inc("vector", lambda e, cc=c, b=bank, sl=s: e.tensor_copy(
                        tstage[:, sl, cc, :], b[:, 0:128]), DV, 1)
                w2_pe[s] = PE.n
                ts_dv[s] = DV.n
                ev_prev = DV.n
                sc.wait("sync", DV, ts_dv[s])
                sc.inc("sync", lambda e, d=dst_fn(jr), sl=s: e.dma_start(
                    d, bass.AP(tstage, sl * NCH * 128,
                               [[2 * NCH * 128, 128], [128, NCH], [1, 128]])), LD, 16)
                st_ld[s] = LD.n

        transpose_rows_to_cache(
            24,
            lambda jr: bass.AP(Wqkv.tensor, jr * 128 * D, [[D, 128], [1, D]]),
            lambda jr: bass.AP(wqkvT_c.tensor, jr * NCH * 128 * 128,
                               [[128, 128], [128 * 128, NCH], [1, 128]]))
        transpose_rows_to_cache(
            NCH,
            lambda jr: bass.AP(Wob.tensor, jr * 128 * D, [[D, 128], [1, D]]),
            lambda jr: bass.AP(woTb_c.tensor, jr * 128, [[D, 128], [128 * D, NCH], [1, 128]]))

        with nc.sbuf_tensor("estage", [16, D], F32) as estage, \
             nc.sbuf_tensor("onef", [128, 1], F32) as onef:
            sc.inc("sync", lambda e: e.dma_start(estage[:, :], esel[:, :]), LD, 16)
            sc.wait("tensor", LD)
            base_dv = DV.n
            for c in range(NCH):
                bank = pp0 if (c % 2 == 0) else pp1
                if c >= 2:
                    sc.wait("tensor", DV, base_dv + c - 1)
                sc.inc("tensor", lambda e, cc=c, b=bank: e.transpose(
                    b[0:128, 0:16], estage[:, cc * 128:(cc + 1) * 128],
                    ident_f[0:16, 0:16]), PE, 1)
                sc.wait("vector", PE)
                sc.inc("vector", lambda e, cc=c, b=bank: e.tensor_copy(
                    eselT[:, cc, :], b[0:128, 0:16]), DV, 1)
            sc.inc("vector", lambda e: e.memset(onef[:, :], 1.0), DV)
            sc.wait("tensor", DV)
            mmacc(pp0[0:1, 0:16], [(onef[:, :], eselT[:, c, :]) for c in range(NCH)])
            sc.wait("vector", PE)
            sc.inc("vector", lambda e: e.tensor_copy(estage[0:1, 0:16], pp0[0:1, 0:16]), DV, 1)
            sc.wait("sync", DV)
            sc.inc("sync", lambda e: e.dma_start(scr16[:], estage[0:1, 0:16]), LD, 16)
            sc.wait("sync", LD)
            sc.inc("sync", lambda e: e.dma_start(
                csebc[:, :], bass.AP(scr16.tensor, 0, [[0, 128], [1, NE]])), LD, 16)

        _prep_es.close()

    sibF = es_sib.enter_context(nc.sbuf_tensor("sibF", [128, NCH, TOK], BF16))
    far = es_far.enter_context(nc.sbuf_tensor("far", [128, TOKF + 64], F32))

    sc.barrier()

    def ln_pair(xts, jfs, pbanks, extra_fn=None):
        xnbuf = (xntok, xntok2)
        sc.wait("scalar", DV)
        for i, (xt, jf) in enumerate(zip(xts, jfs)):
            sc.inc("scalar", lambda e, x=xt, ii=i: e.activation(
                xnbuf[ii][:, :], x, ACTF.Square, accum_out=sumsq[:, ii:ii + 1]), SA, 1)
            sc.inc("vector", lambda e, x=xt, ii=i: e.reduce_sum(
                musum[:, ii:ii + 1], x, axis=AX), DV, 1)
        sc.wait("scalar", DV)
        for i, (xt, jf) in enumerate(zip(xts, jfs)):
            sc.inc("scalar", lambda e, f=jf, ii=i: e.activation(
                mucol[:, f:f + 1], musum[:, ii:ii + 1], ACTF.Copy, scale=1.0 / D), SA, 1)
            sc.inc("scalar", lambda e, ii=i: e.activation(
                varcol[:, ii:ii + 1], musum[:, ii:ii + 1], ACTF.Square,
                scale=1.0 / D), SA, 1)
        sc.wait("vector", SA)
        for i in range(len(xts)):
            sc.inc("vector", lambda e, ii=i: e.scalar_tensor_tensor(
                varcol[:, ii:ii + 1], sumsq[:, ii:ii + 1], 1.0 / D, varcol[:, ii:ii + 1],
                op0=ALU.mult, op1=ALU.subtract), DV, 1)
        sc.wait("scalar", DV)
        for i in range(len(xts)):
            sc.inc("scalar", lambda e, ii=i: e.activation(
                stdcol[:, ii:ii + 1], varcol[:, ii:ii + 1], ACTF.Sqrt,
                bias=epscol[:, :]), SA, 1)
        sc.wait("vector", SA)
        for i, (xt, jf) in enumerate(zip(xts, jfs)):
            sc.inc("vector", lambda e, f=jf, ii=i: e.reciprocal(
                rscol[:, f:f + 1], stdcol[:, ii:ii + 1]), DV, 1)
        sc.bounce("vector", DV, "scalar", SA)
        for i, (xt, jf) in enumerate(zip(xts, jfs)):
            sc.inc("vector", lambda e, f=jf: e.tensor_tensor(
                mrcol[:, f:f + 1], mucol[:, f:f + 1], rscol[:, f:f + 1],
                op=ALU.mult), DV, 1)
            sc.inc("vector", lambda e, x=xt, f=jf, ii=i: e.tensor_scalar(
                xnbuf[ii][:, :], x, mucol[:, f:f + 1], rscol[:, f:f + 1],
                op0=ALU.subtract, op1=ALU.mult), DV, 1)
        for i, (xt, jf) in enumerate(zip(xts, jfs)):
            sc.wait("tensor", DV)
            if jf < NT128:
                base_dv = DV.n
                nb = len(pbanks)
                for c in range(NCH):
                    bank = pbanks[c % nb]
                    if c >= nb:
                        sc.wait("tensor", DV, base_dv + c - nb + 1)
                    sc.inc("tensor", lambda e, cc=c, b=bank, ii=i: e.transpose(
                        b[:, 0:128], xnbuf[ii][:, cc * 128:(cc + 1) * 128],
                        ident_f[:, :]), PE, 1)
                    sc.wait("vector", PE)
                    sc.inc("vector", lambda e, cc=c, b=bank, f=jf: e.tensor_copy(
                        xnF[:, cc, 128 * f:128 * (f + 1)], b[:, 0:128]), DV, 1)
            if extra_fn is not None:
                extra_fn(jf, xt, i)

    def xf_stage_fp32(xt, xFst, pbanks):
        sc.wait("tensor", DV)
        base_dv = DV.n
        nb = len(pbanks)
        for c in range(NCH):
            bank = pbanks[c % nb]
            if c >= nb:
                sc.wait("tensor", DV, base_dv + c - nb + 1)
            src = bass.AP(xt.tensor, xt.offset + c * 128, [[xt.ap[0][0], 128], [1, 128]])
            sc.inc("tensor", lambda e, s=src, b=bank: e.transpose(
                b[:, 0:128], s, ident_f[:, :]), PE, 1)
            sc.wait("vector", PE)
            sc.inc("vector", lambda e, cc=c, b=bank: e.tensor_copy(
                xFst[:, cc, :], b[:, 0:128]), DV, 1)
        sc.wait("tensor", DV)

    if upto >= 1:
      ph1_es = ExitStack()
      ph1_es2 = ExitStack()
      ws1sb = ph1_es.enter_context(nc.sbuf_tensor("ws1sb", [128, NCH, D // 2], F32))
      xsib_stage = ph1_es.enter_context(nc.sbuf_tensor("xsib_stage", [128, 3, D], F32))
      xFst = ph1_es.enter_context(nc.sbuf_tensor("xFst", [128, NCH, 128], F32))
      z1buf = ph1_es.enter_context(nc.sbuf_tensor("z1buf", [128, D // 2], F32))
      with (nc.psum_tensor("pt0", [128, 512], F32) as pt0,
          nc.psum_tensor("pt1", [128, 512], F32) as pt1,
          nc.psum_tensor("pt2", [128, 512], F32) as pt2,
          nc.psum_tensor("pt3", [128, 512], F32) as pt3,
          nc.psum_tensor("pz", [128, 512], F32) as pz):

        for c in range(NCH):
            src = bass.AP(Ws1.tensor, c * 128 * (D // 2), [[D // 2, 128], [1, D // 2]])
            sc.inc("sync", lambda e, cc=c, s=src: e.dma_start(ws1sb[:, cc, :], s), LD, 16)
        with nc.sbuf_tensor("onef2", [128, 1], F32) as onef2:
            sc.inc("vector", lambda e: e.memset(onef2[:, :], 1.0), DV)
            sc.wait("tensor", LD)
            sc.wait("tensor", DV)
            mmacc(pz[0:1, 0:512], [(onef2[:, :], ws1sb[:, c, :]) for c in range(NCH)])
            sc.wait("vector", PE)
            sc.inc("vector", lambda e: e.tensor_copy(far[0:1, 0:512], pz[0:1, 0:512]), DV, 1)
            sc.wait("sync", DV)
            sc.inc("sync", lambda e: e.dma_start(scr512[:], far[0:1, 0:512]), LD, 16)
            sc.wait("sync", LD)
            sc.inc("sync", lambda e: e.dma_start(
                cs1bc[:, :], bass.AP(scr512.tensor, 0, [[0, 128], [1, 512]])), LD, 16)
            sc.wait("vector", LD)
            sc.wait("tensor", LD)

        ph1_banks = [pt0, pt1, pt2, pt3]

        def imp_extra(jf, xt, i=0):
            xf_stage_fp32(xt, xFst, ph1_banks)
            if ph1_imp < 2:
                return
            mmacc(pz[:, 0:512], [(xFst[:, c, :], ws1sb[:, c, :]) for c in range(NCH)])
            sc.wait("vector", PE)
            if ph1_imp < 3:
                sc.inc("vector", lambda e: e.tensor_copy(z1buf[:, :], pz[:, 0:512]), DV, 1)
                sc.wait("tensor", DV)
                return
            sc.inc("vector", lambda e, f=jf: e.scalar_tensor_tensor(
                z1buf[:, :], cs1bc[:, :], mrcol[:, f:f + 1], bs1bc[:, :],
                op0=ALU.mult, op1=ALU.subtract), DV, 1)
            sc.inc("vector", lambda e, f=jf: e.scalar_tensor_tensor(
                z1buf[:, :], pz[:, 0:512], rscol[:, f:f + 1], z1buf[:, :],
                op0=ALU.mult, op1=ALU.subtract), DV, 1)
            sc.inc("vector", lambda e: e.scalar_tensor_tensor(
                z1buf[:, :], z1buf[:, :], 0.0, ws2bc[:, :],
                op0=ALU.max, op1=ALU.mult), DV, 1)
            sc.inc("vector", lambda e, f=jf: e.reduce_sum(
                impc[:, f:f + 1], z1buf[:, :], axis=AX), DV, 1)
            sc.wait("tensor", DV)

        sc.wait("vector", LD)
        sc.wait("scalar", LD)
        sc.wait("tensor", LD)
        for j in range(0, NT128, 2):
            ln_pair((xres[:, j, :], xres[:, j + 1, :]), (j, j + 1), ph1_banks,
                    imp_extra if ph1_imp >= 1 else None)

        def sib_extra(jf, xt, i):
            js = jf - NT128
            xnb = xntok if i == 0 else xntok2
            base_dv = DV.n
            nb = len(ph1_banks)
            for c in range(NCH):
                bank = ph1_banks[c % nb]
                if c >= nb:
                    sc.wait("tensor", DV, base_dv + c - nb + 1)
                sc.inc("tensor", lambda e, cc=c, b=bank, xb=xnb: e.transpose(
                    b[:, 0:128], xb[:, cc * 128:(cc + 1) * 128], ident_f[:, :]), PE, 1)
                sc.wait("vector", PE)
                sc.inc("vector", lambda e, cc=c, b=bank, j=js: e.tensor_copy(
                    sibF[:, cc, 128 * j:128 * (j + 1)], b[:, 0:128]), DV, 1)
            imp_extra(jf, xt)

        sib_done = [(0, 0, 0)] * 3
        sib_ld_at = [0] * 3

        def sib_dma(j):
            s = j % 3
            dv0, pe0, sa0 = sib_done[s]
            sc.wait("sync", DV, dv0)
            sc.wait("sync", PE, pe0)
            sc.wait("sync", SA, sa0)
            src = bass.AP(x_sib.tensor, j * 128 * D, [[D, 128], [1, D]])
            sc.inc("sync", lambda e, sr=src, sl=s: e.dma_start(
                xsib_stage[:, sl, :], sr), LD, 16)
            sib_ld_at[s] = LD.n

        sib_dma(0)
        sib_dma(1)
        for j in range(0, NT128, 2):
            s0, s1 = j % 3, (j + 1) % 3
            if j + 2 < NT128:
                sib_dma(j + 2)
            for eng in ("vector", "scalar", "tensor"):
                sc.wait(eng, LD, max(sib_ld_at[s0], sib_ld_at[s1]))
            ln_pair((xsib_stage[:, s0, :], xsib_stage[:, s1, :]),
                    (NT128 + j, NT128 + j + 1), ph1_banks,
                    sib_extra if ph1_imp >= 1 else None)
            sib_done[s0] = (DV.n, PE.n, SA.n)
            sib_done[s1] = (DV.n, PE.n, SA.n)
            if j + 3 < NT128:
                sib_dma(j + 3)

        ph1_es.close()
        candbuf = ph1_es2.enter_context(nc.sbuf_tensor("candbuf", [128, 2304], F32))
        if not ph1_topk:
            sc.inc("vector", lambda e: e.memset(bar[0:1, 0:TOKF], 1.0), DV, 1)
        ph1_topk_gate = ph1_topk
        if ph1_topk_gate:
          sc.bounce("vector", DV, "scalar", SA)
          sc.inc("vector", lambda e: e.tensor_copy(impc2[:, :], impc[:, :]), DV, 1)
          sc.bounce("vector", DV, "scalar", SA)
          sc.inc("vector", lambda e: e.max(out=m16[:, 0:8], in_=impc2[:, :]), DV, 1)
          sc.bounce("vector", DV, "scalar", SA)
          sc.inc("vector", lambda e: e.match_replace(
            out=impc2[:, :], in_to_replace=m16[:, 0:8],
            in_values=impc2[:, :], imm_value=-1e30), DV, 1)
          sc.bounce("vector", DV, "scalar", SA)
          sc.inc("vector", lambda e: e.max(out=m16[:, 8:16], in_=impc2[:, :]), DV, 1)
          sc.wait("tensor", DV)
          sc.inc("tensor", lambda e: e.transpose(pt0[0:32, 0:128], impc[:, :],
                                               ident_f[:, :]), PE, 1)
          sc.wait("vector", PE)
          sc.inc("vector", lambda e: e.tensor_copy(candbuf[0:32, 2176:2304],
                                                 pt0[0:32, 0:128]), DV, 1)
          sc.wait("sync", DV)
          sc.inc("sync", lambda e: e.dma_start(
            far[0:1, 0:TOKF],
            bass.AP(candbuf, 2176, [[2304, 32], [1, 128]])), LD, 16)
          sc.inc("sync", lambda e: e.dma_start(
            candbuf[0:1, 0:2048],
            bass.AP(m16, 0, [[16, 128], [1, 16]])), LD, 16)
          sc.wait("vector", LD)
          sc.inc("vector", lambda e: e.max(out=candbuf[0:1, 2048:2056], in_=candbuf[0:1, 0:2048]), DV, 1)
          sc.bounce("vector", DV, "scalar", SA)
          sc.inc("vector", lambda e: e.match_replace(
            out=candbuf[0:1, 0:2048], in_to_replace=candbuf[0:1, 2048:2056],
            in_values=candbuf[0:1, 0:2048], imm_value=-1e30), DV, 1)
          sc.bounce("vector", DV, "scalar", SA)
          sc.inc("vector", lambda e: e.max(out=candbuf[0:1, 2056:2064], in_=candbuf[0:1, 0:2048]), DV, 1)
          sc.bounce("vector", DV, "scalar", SA)
          sc.inc("vector", lambda e: e.tensor_copy(
            far[0:1, TOKF + 8:TOKF + 9], candbuf[0:1, 2063:2064]), DV, 1)
          sc.bounce("vector", DV, "scalar", SA)
          sc.inc("vector", lambda e: e.tensor_scalar(
            bar[0:1, 0:TOKF], far[0:1, 0:TOKF], far[0:1, TOKF + 8:TOKF + 9], None,
            op0=ALU.is_lt), DV, 1)
        ph1_es2.close()

    sc.barrier()

    if tap == "ph1":
        TP = sc.chan("TP")
        if ph1_imp >= 3:
            sc.inc("sync", lambda e: e.dma_start(dbg_h[0:128, 0:32], impc[:, :]), TP, 16)
        if ph1_topk:
            sc.inc("sync", lambda e: e.dma_start(dbg_h[0:1, 100:101],
                                                 far[0:1, TOKF + 8:TOKF + 9]), TP, 16)
        sc.inc("sync", lambda e: e.dma_start(dbg_h[0:128, 256:288], mucol[:, 0:32]), TP, 16)
        sc.inc("sync", lambda e: e.dma_start(dbg_h[0:128, 288:320], rscol[:, 0:32]), TP, 16)
        if ph1_topk:
            TPg = sc.chan("TPg")
            sc.inc("gpsimd", lambda e: e.dma_start(dbg_h[0:1, 320:320 + 4096],
                                                   bar[0:1, 0:TOKF]), TPg, 16)
            sc.inc("sync", lambda e: e.dma_start(dbg_h[1:2, 320:320 + 4096],
                                                 far[0:1, 0:TOKF]), TP, 16)
            sc.wait("gpsimd", TPg)
        sc.wait("sync", TP)

    es_far.close()

    if upto >= 2:
      with (nc.sbuf_tensor("whq", [128, 2, NCH, DH], BF16) as whq,
          nc.sbuf_tensor("whk", [128, 2, NCH, DH], BF16) as whk,
          nc.sbuf_tensor("whv", [128, 2, NCH, DH], BF16) as whv,
          nc.sbuf_tensor("khat", [65, TOKF], BF16) as khat,
          nc.sbuf_tensor("qhat", [65, TOK], BF16) as qhat,
          nc.sbuf_tensor("vhat", [128, 32, 65], BF16) as vhat,
          nc.sbuf_tensor("attp", [128, TOK], BF16) as attp,
          nc.sbuf_tensor("ostage", [64, TOK], BF16) as ostage,
          nc.sbuf_tensor("ebuf", [128, 2, 1536], BF16) as ebuf,
          nc.sbuf_tensor("astage", [64, 512], BF16) as astage,
          nc.sbuf_tensor("woap", [128, D], BF16) as woap,
          nc.sbuf_tensor("boabc", [128, D], BF16) as boabc,
          nc.psum_tensor("sgA", [128, 1536], F32) as sgA,
          nc.psum_tensor("sgB", [128, 1536], F32) as sgB,
          nc.psum_tensor("patt", [128, 512], F32) as patt,
          nc.psum_tensor("prbc", [128, 512], F32) as prbc):

        WD = sc.chan("WD")
        SIBD = sc.chan("SIBD")
        UD = sc.chan("UD")
        MD = sc.chan("MD")
        OD = sc.chan("OD")
        EXP = sc.chan("EXP")
        AVC = sc.chan("AVC")
        kc_groups = [(0, 3), (3, 6), (6, 9), (9, 12), (12, 15), (15, 18), (18, 21),
                     (21, 24), (24, 27), (27, 30), (30, 32)]
        head_pe_done = 0
        pair_y_done = 0

        B2D = sc.chan("B2D")
        sc.inc("gpsimd", lambda e: e.dma_start(
            boabc[:, 0:D], bass.AP(bo.tensor, 0, [[0, 128], [1, D]])), B2D, 16)
        sc.wait("vector", B2D)
        hw_pe = [0, 0]
        hw_wd = [0, 0]

        def head_w_dma(h):
            s = h % 2
            sc.wait("sync", PE, hw_pe[s])
            for (cache, buf) in ((wq_c, whq), (wk_c, whk), (wv_c, whv)):
                src = bass.AP(cache.tensor, h * D * DH,
                              [[DH, 128], [128 * DH, NCH], [1, DH]])
                sc.inc("sync", lambda e, sr=src, b=buf, sl=s: e.dma_start(
                    bass.AP(b, sl * NCH * DH,
                            [[2 * NCH * DH, 128], [DH, NCH], [1, DH]]), sr), WD, 16)
            hw_wd[s] = WD.n

        head_w_dma(0)
        for h in range(H):
            hb = 64 * (h % 2)
            hc = h // 2
            hs = h % 2
            if h + 1 < H:
                head_w_dma(h + 1)
            sc.wait("tensor", WD, hw_wd[hs])
            if h % 2 == 1:
                sc.wait("sync", PE, pair_y_done)
                wsrc = bass.AP(wo_c.tensor, hc * 128 * D, [[D, 128], [1, D]])
                sc.inc("sync", lambda e, s=wsrc: e.dma_start(woap[:, :], s), WD, 16)
                woap_wd = WD.n

            sc.inc("vector", lambda e: e.memset(vhat[:, :, 64:65], 1.0), DV, 1)
            kbanks = [sgA[:, 0:512], sgA[:, 512:1024], sgA[:, 1024:1536],
                      sgB[:, 0:512]]
            vbanks = [sgB[:, 512:1024], sgB[:, 1024:1536]]
            k_ev = [0, 0, 0, 0]
            v_ev = [0, 0]
            for kt in range(8):
                def xn_sl(c, off, width, kt=kt):
                    if kt < 4:
                        return xnF[:, c, 512 * kt + off:512 * kt + off + width]
                    o = 512 * (kt - 4) + off
                    return sibF[:, c, o:o + width]

                ks = kt % 4
                if k_ev[ks]:
                    sc.wait("tensor", DV, k_ev[ks])
                if kt == 0:
                    sc.wait("tensor", DV)
                    if head_pe_done:
                        sc.wait("vector", PE, head_pe_done)
                mmacc(kbanks[ks][0:64, 0:512],
                      [(whk[:, hs, c, :], xn_sl(c, 0, 512)) for c in range(NCH)])
                sc.wait("vector", PE)
                sc.inc("vector", lambda e, k=kt, b=kbanks[ks]: e.tensor_scalar(
                    khat[0:64, 512 * k:512 * (k + 1)], b[0:64, 0:512],
                    bkcol[hb:hb + 64, hc:hc + 1], None, op0=ALU.add), DV, 1)
                k_ev[ks] = DV.n
                for q in range(4):
                    tt = kt * 4 + q
                    vsl = tt % 2
                    if v_ev[vsl]:
                        sc.wait("tensor", DV, v_ev[vsl])
                    mmacc(vbanks[vsl][:, 0:64],
                          [(xn_sl(c, 128 * q, 128), whv[:, hs, c, :]) for c in range(NCH)])
                    sc.wait("vector", PE)
                    sc.inc("vector", lambda e, t=tt, b=vbanks[vsl]: e.tensor_tensor(
                        vhat[:, t, 0:64], b[:, 0:64],
                        bvbc[:, DH * h:DH * h + DH], op=ALU.add), DV, 1)
                    v_ev[vsl] = DV.n
            sc.wait("gpsimd", DV)
            sc.inc("gpsimd", lambda e: e.dma_start(khat[64:65, :], bar[0:1, 0:TOKF]), UD, 16)
            sc.wait("vector", UD)
            sc.inc("vector", lambda e: e.tensor_scalar(
                khat[64:65, :], khat[64:65, :], -MASKBIG, None, op0=ALU.mult), DV, 1)

            for qt in range(4):
                ks = qt % 4
                if k_ev[ks]:
                    sc.wait("tensor", DV, k_ev[ks])
                mmacc(kbanks[ks][0:64, 0:512],
                      [(whq[:, hs, c, :], xnF[:, c, 512 * qt:512 * (qt + 1)])
                       for c in range(NCH)])
                sc.wait("vector", PE)
                sc.inc("vector", lambda e, q=qt, b=kbanks[ks]: e.tensor_scalar(
                    qhat[0:64, 512 * q:512 * (q + 1)], b[0:64, 0:512],
                    bqcol[hb:hb + 64, hc:hc + 1], None, op0=ALU.add), DV, 1)
                k_ev[ks] = DV.n
            hw_pe[hs] = PE.n
            sc.wait("gpsimd", DV)
            sc.inc("gpsimd", lambda e: e.dma_start(qhat[64:65, :], bar[0:1, 0:TOK]), UD, 16)

            sc.wait("tensor", UD)
            sc.wait("tensor", DV)
            sc.wait("tensor", SA)
            def mk_norm(qt):
                def emit():
                    sc.wait("tensor", DV)
                    sc.inc("tensor", lambda e: e.matmul(
                        prbc[0:64, 0:512], ones_bf[64:65, 0:64], rrow[64:65, 0:512],
                        start=True, stop=True), PE, 1)
                    sc.wait("vector", PE)
                    if h % 2 == 0:
                        dst = attp[0:64, 512 * qt:512 * (qt + 1)]
                    else:
                        dst = ostage[0:64, 512 * qt:512 * (qt + 1)]
                    if pair_y_done:
                        sc.wait("vector", PE, pair_y_done)
                    if h % 2 == 1 and qt == 0:
                        sc.wait("vector", OD)
                    sc.inc("vector", lambda e, d=dst: e.tensor_tensor(
                        d, astage[:, :], prbc[0:64, 0:512], op=ALU.mult), DV, 1)
                return emit

            pending = None
            for qt in range(4):
                q_ap = qhat[0:65, 512 * qt:512 * (qt + 1)]

                def av0w():
                    sc.wait("tensor", SA)
                    sc.wait("tensor", DV)
                attn_stream(khat, q_ap, vhat, ebuf, sgA, sgB, patt, EXP, AVC,
                            av0_waits=av0w, mid_emit=pending)
                sc.wait("vector", AVC)
                sc.inc("vector", lambda e: e.reciprocal(xntok[64:65, 0:512],
                                                        patt[64:65, 0:512]), DV, 1)
                sc.inc("vector", lambda e: e.tensor_copy(rrow[64:65, 0:512],
                                                         xntok[64:65, 0:512]), DV, 1)
                sc.inc("vector", lambda e: e.tensor_copy(astage[:, :],
                                                         patt[0:64, 0:512]), DV, 1)
                pending = mk_norm(qt)
            pending()
            head_pe_done = PE.n

            if h % 2 == 1:
                sc.wait("gpsimd", DV)
                sc.inc("gpsimd", lambda e: e.dma_start(
                    bass.AP(attp, 64 * TOK, [[TOK, 64], [1, TOK]]),
                    bass.AP(ostage, 0, [[TOK, 64], [1, TOK]])), OD, 16)
                sc.wait("tensor", OD)
                sc.wait("tensor", WD, woap_wd)
                sc.wait("tensor", DV)
                dv_base = DV.n
                i = 0
                for tt in range(NT128):
                    for jh in range(2):
                        bank = sgA if (i % 2 == 0) else sgB
                        if i >= 2:
                            sc.wait("tensor", DV, dv_base + i - 1)
                        sc.inc("tensor", lambda e, t=tt, j=jh, b=bank: e.matmul(
                            b[:, 0:512], attp[:, 128 * t:128 * (t + 1)],
                            woap[:, 512 * j:512 * (j + 1)], start=True, stop=True), PE, 1)
                        sc.wait("vector", PE)
                        sc.inc("vector", lambda e, t=tt, j=jh, b=bank: e.tensor_tensor(
                            xres[:, t, 512 * j:512 * (j + 1)],
                            xres[:, t, 512 * j:512 * (j + 1)], b[:, 0:512], op=ALU.add), DV, 1)
                        i += 1
                pair_y_done = PE.n

        for tt in range(NT128):
            sc.inc("vector", lambda e, t=tt: e.tensor_tensor(
                xres[:, t, :], xres[:, t, :], boabc[:, :], op=ALU.add), DV, 1)

    sc.barrier()
    es_sib.close()

    if tap == "ph2":
        TP = sc.chan("TP")
        for t in range(8):
            sc.inc("sync", lambda e, tt=t: e.dma_start(
                dbg_h[0:128, 1024 * tt:1024 * (tt + 1)], xres[:, tt, :]), TP, 16)
        sc.wait("sync", TP)

    def ffn_pass(tag):
        with (nc.sbuf_tensor("xFst2" + tag, [128, NCH, 128], F32) as xFst2,
              nc.sbuf_tensor("probs" + tag, [128, NE], F32) as probs,
              nc.sbuf_tensor("top8" + tag, [128, 8], F32) as top8,
              nc.sbuf_tensor("wsum" + tag, [128, NT128], F32) as wsum,
              nc.sbuf_tensor("wsr" + tag, [128, NT128], F32) as wsr,
              nc.sbuf_tensor("stok" + tag, [128, 2, 512], F32) as stok,
              nc.sbuf_tensor("sF" + tag, [128, 32, 512], BF16) as sF,
              nc.sbuf_tensor("kbuf" + tag, [128, 3, NCH, 512], BF16) as kbuf,
              nc.sbuf_tensor("vbuf" + tag, [128, 4, D], BF16) as vbuf,
              nc.psum_tensor("pfA" + tag, [128, 512], F32) as pfA,
              nc.psum_tensor("pfB" + tag, [128, 512], F32) as pfB,
              nc.psum_tensor("pfT0" + tag, [128, 512], F32) as pfT0,
              nc.psum_tensor("pfT1" + tag, [128, 512], F32) as pfT1,
              nc.psum_tensor("pval" + tag, [128, 4 * 512], F32) as pval):
            KD = sc.chan("KD" + tag)
            VD = sc.chan("VD" + tag)

            ffn_banks = [pfT0, pfT1, pval[:, 0:512], pval[:, 512:1024]]

            def route_extra(jf, xt, i=0):
                xf_stage_fp32(xt, xFst2, ffn_banks)
                mmacc(pfA[:, 0:16], [(xFst2[:, c, :], eselT[:, c, :]) for c in range(NCH)])
                sc.wait("scalar", DV)
                sc.inc("scalar", lambda e, f=jf: e.activation(
                    probs[:, :], csebc[:, :], ACTF.Copy, scale=mrcol[:, f:f + 1]), SA, 1)
                sc.wait("vector", PE)
                sc.wait("vector", SA)
                sc.inc("vector", lambda e, f=jf: e.scalar_tensor_tensor(
                    probs[:, :], pfA[:, 0:16], rscol[:, f:f + 1], probs[:, :],
                    op0=ALU.mult, op1=ALU.subtract), DV, 1)
                sc.wait("scalar", DV)
                sc.inc("scalar", lambda e: e.activation(probs[:, :], probs[:, :],
                                                        ACTF.Sigmoid), SA, 1)
                sc.wait("vector", SA)
                sc.inc("vector", lambda e: e.max(out=top8[:, :], in_=probs[:, :]), DV, 1)
                sc.bounce("vector", DV, "scalar", SA)
                sc.inc("vector", lambda e, j=jf: e.scalar_tensor_tensor(
                    w_all[:, j, :], probs[:, :], top8[:, 3:4], probs[:, :],
                    op0=ALU.is_ge, op1=ALU.mult), DV, 1)
                sc.inc("vector", lambda e, j=jf: e.reduce_sum(
                    wsum[:, j:j + 1], top8[:, 0:4], axis=AX), DV, 1)
                sc.wait("tensor", DV)

            sc.wait("tensor", DV)
            for j in range(0, NT128, 2):
                ln_pair((xres[:, j, :], xres[:, j + 1, :]), (j, j + 1),
                        ffn_banks, route_extra)

            sc.bounce("vector", DV, "scalar", SA)
            sc.inc("vector", lambda e: e.reciprocal(wsr[:, :], wsum[:, :]), DV, 1)
            sc.bounce("vector", DV, "scalar", SA)
            for j in range(NT128):
                sc.inc("vector", lambda e, jj=j: e.tensor_scalar(
                    w_all[:, jj, :], w_all[:, jj, :], wsr[:, jj:jj + 1], None,
                    op0=ALU.mult), DV, 1)

            kslot_pe = [0, 0, 0]
            kd_at = [0, 0, 0]

            def kbuf_dma(idx):
                t5x, eix = idx // 8, idx % 8
                s = idx % 3
                if kslot_pe[s]:
                    sc.wait("sync", PE, kslot_pe[s])
                for c in range(NCH):
                    src = bass.AP(keys_c.tensor, c * 128 * NE * EHD + eix * 512,
                                  [[NE * EHD, 128], [1, 512]])
                    sc.inc("sync", lambda e, cc=c, sr=src, sl=s: e.dma_start(
                        kbuf[:, sl, cc, :], sr), KD, 16)
                kd_at[s] = KD.n

            kbuf_dma(0)
            kbuf_dma(1)
            stok_ev = [0, 0]
            sF_ev = [0]
            pend_T = [None]
            for t5 in range(4):
                for ei in range(8):
                    kidx = t5 * 8 + ei
                    ks = kidx % 3
                    if kidx + 2 < 32:
                        kbuf_dma(kidx + 2)
                    sc.wait("tensor", KD, kd_at[ks])
                    for sub in range(4):
                        t1 = t5 * 4 + sub
                        bank = pfA if (sub % 2 == 0) else pfB
                        if stok_ev[sub % 2]:
                            sc.wait("tensor", DV, stok_ev[sub % 2])
                        mmacc(bank[:, 0:512],
                              [(xnF[:, c, 128 * t1:128 * (t1 + 1)], kbuf[:, ks, c, :])
                               for c in range(NCH)])
                        if pend_T[0] is not None:
                            pend_T[0]()
                            pend_T[0] = None
                        sc.wait("vector", PE)
                        e_lo = ei * 2
                        for eh in range(2):
                            sc.inc("vector", lambda e, b=bank, t=t1, el=e_lo, q=eh,
                                   ss=sub % 2: e.tensor_scalar(
                                       stok[:, ss, 256 * q:256 * (q + 1)],
                                       b[:, 256 * q:256 * (q + 1)],
                                       0.0, w_all[:, t, el + q:el + q + 1],
                                       op0=ALU.max, op1=ALU.mult), DV, 1)
                        stok_ev[sub % 2] = DV.n

                        def mk_T(sub=sub, ei=ei, need=DV.n, ss=sub % 2):
                            def emit():
                                sc.wait("tensor", DV, need)
                                if sF_ev[0]:
                                    sc.wait("tensor", DV, sF_ev[0])
                                for qq in range(4):
                                    g = ei * 4 + qq
                                    bank2 = ffn_banks[qq]
                                    sc.inc("tensor", lambda e, q=qq, b=bank2, s2=ss:
                                           e.transpose(
                                               b[:, 0:128],
                                               stok[:, s2, 128 * q:128 * (q + 1)],
                                               ident_f[:, :]), PE, 1)
                                    sc.wait("vector", PE)
                                    sc.inc("vector", lambda e, g_=g, b=bank2, s_=sub:
                                           e.tensor_copy(
                                               sF[:, g_, 128 * s_:128 * (s_ + 1)],
                                               b[:, 0:128]), DV, 1)
                                sF_ev[0] = DV.n
                            return emit
                        pend_T[0] = mk_T()
                    kslot_pe[ks] = PE.n
                if pend_T[0] is not None:
                    pend_T[0]()
                    pend_T[0] = None
                vslot_free = [0] * 4
                vd_at = [0] * 4

                def val_dma(g):
                    sl = g % 4
                    if vslot_free[sl]:
                        sc.wait("sync", PE, vslot_free[sl])
                    src = bass.AP(vals_c.tensor, g * 128 * D, [[D, 128], [1, D]])
                    sc.inc("sync", lambda e, sl_=sl, s=src: e.dma_start(
                        vbuf[:, sl_, :], s), VD, 16)
                    vd_at[sl] = VD.n

                val_dma(0)
                val_dma(1)
                for g in range(32):
                    slot = g % 4
                    if g + 2 < 32:
                        val_dma(g + 2)
                    sc.wait("tensor", VD, vd_at[slot])
                    if g == 0:
                        sc.wait("tensor", DV)
                    for sub in range(4):
                        for dh in range(2):
                            pi = sub if dh == 0 else None
                            st, sp = (g == 0), (g == 31)
                            if dh == 0:
                                outap = pval[:, 512 * sub:512 * (sub + 1)]
                            else:
                                outap = (pfA if sub == 0 else pfB)[:, 0:512] if sub < 2 else \
                                        (pfT0 if sub == 2 else pfT1)[:, 0:512]
                            lastf = (sub == 3 and dh == 1)
                            f = (lambda e, g_=g, sl=slot, s_=sub, d_=dh, o=outap,
                                 s0=st, s1=sp: e.matmul(
                                o, sF[:, g_, 128 * s_:128 * (s_ + 1)],
                                vbuf[:, sl, 512 * d_:512 * (d_ + 1)],
                                start=s0, stop=s1))
                            if lastf:
                                sc.inc("tensor", f, PE, 1)
                            else:
                                sc.emit("tensor", f)
                    vslot_free[slot] = PE.n
                sc.wait("vector", PE)
                for sub in range(4):
                    t1 = t5 * 4 + sub
                    for dh in range(2):
                        if dh == 0:
                            outap = pval[:, 512 * sub:512 * (sub + 1)]
                        else:
                            outap = (pfA if sub == 0 else pfB)[:, 0:512] if sub < 2 else \
                                    (pfT0 if sub == 2 else pfT1)[:, 0:512]
                        sc.inc("vector", lambda e, t=t1, d=dh, o=outap: e.scalar_tensor_tensor(
                            xres[:, t, 512 * d:512 * (d + 1)], o, rsccol[:, :],
                            xres[:, t, 512 * d:512 * (d + 1)],
                            op0=ALU.mult, op1=ALU.add), DV, 1)
                sc.wait("tensor", DV)

    if upto >= 3:
        ffn_pass("_a")
    sc.barrier()

    if tap == "ph3":
        TP = sc.chan("TP")
        for t in range(8):
            sc.inc("sync", lambda e, tt=t: e.dma_start(
                dbg_h[0:128, 1024 * tt:1024 * (tt + 1)], xres[:, tt, :]), TP, 16)
        sc.wait("sync", TP)

    if upto >= 4:
      with ExitStack() as ph4_es:
        _s = ph4_es.enter_context
        wqkvb = _s(nc.sbuf_tensor("wqkvb", [128, 2, NCH, 128], BF16))
        bqkvvbc = _s(nc.sbuf_tensor("bqkvvbc", [128, NE * DH], BF16))
        bobbc = _s(nc.sbuf_tensor("bobbc", [128, D], BF16))
        qkv_w = _s(nc.sbuf_tensor("qkv_w", [128, 24, WSZ], BF16))
        Qhb = _s(nc.sbuf_tensor("Qhb", [65, TOKF], BF16))
        Khb = _s(nc.sbuf_tensor("Khb", [65, TOKF], BF16))
        vhatb = _s(nc.sbuf_tensor("vhatb", [128, 32, 65], BF16))
        vals_w = _s(nc.sbuf_tensor("vals_w", [128, NCH, WSZ], BF16))
        ostb = _s(nc.sbuf_tensor("ostb", [64, TOK], BF16))
        ebufb = _s(nc.sbuf_tensor("ebufb", [128, 2, 1536], BF16))
        astb = _s(nc.sbuf_tensor("astb", [64, 512], BF16))
        wotb = _s(nc.sbuf_tensor("wotb", [128, NCH, 512], BF16))
        bgA = _s(nc.psum_tensor("bgA", [128, 1536], F32))
        bgB = _s(nc.psum_tensor("bgB", [128, 1536], F32))
        pattb = _s(nc.psum_tensor("pattb", [128, 512], F32))
        prbcb = _s(nc.psum_tensor("prbcb", [128, 512], F32))
        WDB = sc.chan("WDB")
        ODB = sc.chan("ODB")
        SDB = sc.chan("SDB")
        EXB = sc.chan("EXB")
        AVB = sc.chan("AVB")

        sc.wait("tensor", DV)
        for j in range(0, NT128, 2):
            ln_pair((xres[:, j, :], xres[:, j + 1, :]), (j, j + 1),
                    [bgA[:, 0:512], bgA[:, 512:1024],
                     bgB[:, 0:512], bgB[:, 512:1024]], None)

        B4D = sc.chan("B4D")
        sc.inc("gpsimd", lambda e: e.dma_start(
            bobbc[:, 0:D], bass.AP(bob.tensor, 0, [[0, 128], [1, D]])), B4D, 16)
        sc.inc("gpsimd", lambda e: e.dma_start(
            bqkvvbc[:, :], bass.AP(bqkv.tensor, 128, [[0, 128], [192, NE], [1, 64]])),
            B4D, 16)
        sc.wait("vector", B4D)
        sc.inc("vector", lambda e: e.memset(Khb[64:65, :], 1.0), DV, 1)
        sc.inc("vector", lambda e: e.memset(Qhb[64:65, :], 0.0), DV, 1)

        vslot = {(192 * c + 128) // 128: (c, (192 * c + 128) % 128) for c in range(NE)}
        kc_groups = [(0, 3), (3, 6), (6, 9), (9, 12), (12, 15), (15, 18), (18, 21),
                     (21, 24), (24, 27), (27, 30), (30, 32)]
        win_pe_done = 0
        slot_pe = [0, 0]
        wdb_at = [0, 0]

        def qkv_dma(jr):
            s = jr % 2
            if slot_pe[s]:
                sc.wait("sync", PE, slot_pe[s])
            src = bass.AP(wqkvT_c.tensor, jr * NCH * 128 * 128,
                          [[128, 128], [128 * 128, NCH], [1, 128]])
            sc.inc("sync", lambda e, sr=src, sl=s: e.dma_start(
                bass.AP(wqkvb, sl * NCH * 128,
                        [[2 * NCH * 128, 128], [128, NCH], [1, 128]]), sr), WDB, 16)
            wdb_at[s] = WDB.n

        qkv_dma(0)
        for w in range(NW):
            wc0 = WSZ * w
            sc.wait("tensor", SA)
            sc.wait("tensor", DV)
            sc.inc("vector", lambda e: e.memset(vhatb[:, :, 64:65], 1.0), DV, 1)
            qkv_ev = [0, 0]
            vh_ev = [0, 0]
            for jr in range(24):
                s = jr % 2
                if jr + 1 < 24:
                    qkv_dma(jr + 1)
                elif w + 1 < NW:
                    qkv_dma(0)
                sc.wait("tensor", WDB, wdb_at[s])
                bank = bgA[:, 512:1024] if (jr % 2 == 0) else bgB[:, 512:1024]
                if qkv_ev[s]:
                    sc.wait("tensor", DV, qkv_ev[s])
                if win_pe_done and jr == 0:
                    sc.wait("vector", PE, win_pe_done)
                mmacc(bank[:, 0:WSZ],
                      [(wqkvb[:, s, c, :], xnF[:, c, wc0:wc0 + WSZ]) for c in range(NCH)])
                sc.wait("vector", PE)
                sc.inc("vector", lambda e, j=jr, b=bank: e.tensor_scalar(
                    qkv_w[:, j, :], b[:, 0:WSZ], bqkvcol[:, j:j + 1], None,
                    op0=ALU.add), DV, 1)
                qkv_ev[s] = DV.n
                if jr in vslot:
                    cv, voff = vslot[jr]
                    for ah in range(2):
                        vb = pattb if ah == 0 else prbcb
                        if vh_ev[ah]:
                            sc.wait("tensor", DV, vh_ev[ah])
                        mmacc(vb[:, 0:64],
                              [(xnF[:, c, wc0 + 128 * ah:wc0 + 128 * ah + 128],
                                wqkvb[:, s, c, voff:voff + 64]) for c in range(NCH)])
                        sc.wait("vector", PE)
                        sc.inc("vector", lambda e, t=2 * cv + ah, b=vb, c_=cv: e.tensor_tensor(
                            vhatb[:, t, 0:64], b[:, 0:64],
                            bqkvvbc[:, 64 * c_:64 * c_ + 64], op=ALU.add), DV, 1)
                        vh_ev[ah] = DV.n
                slot_pe[s] = PE.n

            for kind, dstt in ((0, Qhb), (1, Khb)):
                for c in range(NE):
                    Bidx = 3 * c + kind
                    jc, base = Bidx // 2, 64 * (Bidx % 2)
                    if base == 0:
                        sc.inc("vector", lambda e, j=jc, c_=c, d=dstt: e.tensor_copy(
                            d[0:64, WSZ * c_:WSZ * (c_ + 1)], qkv_w[0:64, j, :]), DV, 1)
                if kind == 0:
                    jc0, cstart = 1, 1
                else:
                    jc0, cstart = 2, 0 if False else 2
                if kind == 0:
                    in_off = 64 * (24 * WSZ) + 1 * WSZ
                    out_off = WSZ * 1
                else:
                    in_off = 64 * (24 * WSZ) + 0 * WSZ
                    out_off = WSZ * 0
                sc.wait("gpsimd", DV)
                sc.inc("gpsimd", lambda e, io=in_off, oo=out_off, d=dstt: e.dma_start(
                    bass.AP(d, oo, [[TOKF, 64], [2 * WSZ, 8], [1, WSZ]]),
                    bass.AP(qkv_w, io, [[24 * WSZ, 64], [3 * WSZ, 8], [1, WSZ]])), SDB, 16)

            sc.wait("vector", SDB)
            nslots = [bgA[0:1, 0:1536], bgB[0:1, 0:1536],
                      pattb[0:1, 0:512], prbcb[0:1, 0:512]]
            for (mat, slot) in ((Khb, 2), (Qhb, 3)):
                sc.inc("vector", lambda e, m=mat: e.tensor_tensor(
                    bar[0:64, 0:TOKF], m[0:64, :], m[0:64, :], op=ALU.mult), DV, 1)
                sc.wait("tensor", DV)
                for kt in range(8):
                    if kt < 3:
                        dst = bgA[0:1, 512 * kt:512 * (kt + 1)]
                    elif kt < 6:
                        dst = bgB[0:1, 512 * (kt - 3):512 * (kt - 2)]
                    elif kt == 6:
                        dst = pattb[0:1, 0:512]
                    else:
                        dst = prbcb[0:1, 0:512]
                    sc.inc("tensor", lambda e, k=kt, d=dst: e.matmul(
                        d, ones_bf[0:64, 0:1],
                        bar[0:64, 512 * k:512 * (k + 1)], start=True, stop=True), PE, 1)
                sc.wait("vector", PE)
                for i, ns in enumerate(nslots):
                    sc.inc("vector", lambda e, s=ns, ii=i: e.reduce_max(
                        mrow[0:1, 8 + ii:9 + ii], s, axis=AX), DV, 1)
                sc.bounce("vector", DV, "scalar", SA)
                sc.inc("vector", lambda e, s=slot: e.reduce_max(
                    mrow[0:1, s:s + 1], mrow[0:1, 8:12], axis=AX), DV, 1)
                sc.wait("tensor", DV)
            sc.wait("scalar", DV)
            for slot in (2, 3):
                sc.inc("scalar", lambda e, s=slot: e.activation(
                    mrow[0:1, s:s + 1], mrow[0:1, s:s + 1], ACTF.Sqrt), SA, 1)
            sc.wait("vector", SA)
            sc.inc("vector", lambda e: e.scalar_tensor_tensor(
                mrow[0:1, 4:5], mrow[0:1, 2:3], -0.125,
                mrow[0:1, 3:4], op0=ALU.mult, op1=ALU.mult), DV, 1)
            sc.bounce("vector", DV, "scalar", SA)
            sc.wait("vector", EXB)
            sc.inc("vector", lambda e: e.tensor_scalar(
                mrow[0:1, 0:1], mrow[0:1, 4:5], 68.0, 0.0,
                op0=ALU.add, op1=ALU.min), DV, 1)
            sc.wait("tensor", DV)
            sc.inc("tensor", lambda e: e.matmul(
                prbcb[:, 0:1], ones_bf[0:1, 0:128], mrow[0:1, 0:1],
                start=True, stop=True), PE, 1)
            sc.wait("vector", PE)
            sc.inc("vector", lambda e: e.tensor_copy(negKb[:, :], prbcb[:, 0:1]), DV, 1)
            sc.wait("tensor", SDB)
            sc.wait("tensor", DV)
            sc.wait("tensor", SA)
            sc.wait("scalar", DV)
            def mk_normb(qt):
                def emit():
                    sc.wait("tensor", DV)
                    sc.inc("tensor", lambda e: e.matmul(
                        prbcb[0:64, 0:512], ones_bf[64:65, 0:64], rrow[64:65, 0:512],
                        start=True, stop=True), PE, 1)
                    sc.wait("vector", PE)
                    if win_pe_done:
                        sc.wait("vector", PE, win_pe_done)
                    sc.inc("vector", lambda e, j=qt: e.tensor_tensor(
                        vals_w[0:64, j, :], astb[:, 0:256], prbcb[0:64, 0:256],
                        op=ALU.mult), DV, 1)
                    if qt == 0:
                        sc.wait("vector", ODB)
                    sc.inc("vector", lambda e, j=qt: e.tensor_tensor(
                        ostb[0:64, WSZ * j:WSZ * (j + 1)], astb[:, 256:512],
                        prbcb[0:64, 256:512], op=ALU.mult), DV, 1)
                return emit

            pendingb = None
            for qt in range(8):
                q_ap = Qhb[0:65, 512 * qt:512 * (qt + 1)]

                def av0wb():
                    sc.wait("tensor", SA)
                    sc.wait("tensor", DV)
                attn_stream(Khb, q_ap, vhatb, ebufb, bgA, bgB, pattb, EXB, AVB,
                            av0_waits=av0wb, exp_kw={"bias": negKb[:, :]},
                            mid_emit=pendingb)
                sc.wait("vector", AVB)
                sc.inc("vector", lambda e: e.reciprocal(xntok[64:65, 0:512],
                                                        pattb[64:65, 0:512]), DV, 1)
                sc.inc("vector", lambda e: e.tensor_copy(rrow[64:65, 0:512],
                                                         xntok[64:65, 0:512]), DV, 1)
                sc.inc("vector", lambda e: e.tensor_copy(astb[:, :],
                                                         pattb[0:64, 0:512]), DV, 1)
                pendingb = mk_normb(qt)
            pendingb()
            sc.wait("gpsimd", DV)
            sc.inc("gpsimd", lambda e: e.dma_start(
                bass.AP(vals_w, 64 * NCH * WSZ, [[NCH * WSZ, 64], [WSZ, NCH], [1, WSZ]]),
                bass.AP(ostb, 0, [[TOK, 64], [WSZ, NCH], [1, WSZ]])), ODB, 16)
            sc.wait("tensor", ODB)
            sc.wait("tensor", DV)
            i = 0
            for jh in range(2):
                sc.wait("sync", PE)
                src = bass.AP(woTb_c.tensor, jh * 512, [[D, 128], [128 * D, NCH], [1, 512]])
                sc.inc("sync", lambda e, s=src: e.dma_start(
                    bass.AP(wotb, 0, [[NCH * 512, 128], [512, NCH], [1, 512]]), s), WDB, 16)
                sc.wait("tensor", WDB)
                for tt in range(2):
                    gt = 2 * w + tt
                    bank = bgA[:, 0:512] if (i % 2 == 0) else bgB[:, 0:512]
                    sc.wait("tensor", DV)
                    mmacc(bank,
                          [(vals_w[:, c, 128 * tt:128 * (tt + 1)],
                            wotb[:, c, :]) for c in range(NCH)])
                    sc.wait("vector", PE)
                    sc.inc("vector", lambda e, t=gt, j=jh, b=bank: e.tensor_tensor(
                        xres[:, t, 512 * j:512 * (j + 1)], xres[:, t, 512 * j:512 * (j + 1)],
                        b, op=ALU.add), DV, 1)
                    i += 1
            win_pe_done = PE.n
        for tt in range(NT128):
            sc.inc("vector", lambda e, t=tt: e.tensor_tensor(
                xres[:, t, :], xres[:, t, :], bobbc[:, :], op=ALU.add), DV, 1)

    sc.barrier()

    if tap == "ph4":
        TP = sc.chan("TP")
        for t in range(8):
            sc.inc("sync", lambda e, tt=t: e.dma_start(
                dbg_h[0:128, 1024 * tt:1024 * (tt + 1)], xres[:, tt, :]), TP, 16)
        sc.wait("sync", TP)

    if upto >= 5:
        ffn_pass("_b")
    sc.barrier()

    sc.wait("sync", DV)
    ST = sc.chan("ST")
    for j in range(NT128):
        dst = bass.AP(out_h.tensor, j * 128 * D, [[D, 128], [1, D]])
        sc.inc("sync", lambda e, jj=j, d=dst: e.dma_start(d, xres[:, jj, :]), ST, 16)
    sc.wait("sync", ST)

    sc.replay()
    stack.close()
    return nc


WEIGHT_MAP = [
    ("Wq", "L0_attn_a_Wq"), ("bq", "L0_attn_a_bq"), ("Wk", "L0_attn_a_Wk"),
    ("bk", "L0_attn_a_bk"), ("Wv", "L0_attn_a_Wv"), ("bv", "L0_attn_a_bv"),
    ("Wo", "L0_attn_a_Wo"), ("bo", "L0_attn_a_bo"), ("Ws1", "L0_attn_a_Ws1"),
    ("bs1", "L0_attn_a_bs1"), ("Ws2", "L0_attn_a_Ws2"),
    ("Wqkv", "L0_attn_b_Wqkv"), ("bqkv", "L0_attn_b_bqkv"),
    ("Wob", "L0_attn_b_Wo"), ("bob", "L0_attn_b_bo"),
    ("esel", "L0_ffn_expert_sel"), ("keys", "L0_ffn_keys"),
    ("values", "L0_ffn_values"), ("rsc", "L0_ffn_route_scale"),
]


def make_in_maps(inputs):
    x = np.ascontiguousarray(np.asarray(inputs["L0_attn_a_x"], dtype=np.float32))
    in_maps = []
    for c in range(8):
        b, h = c // 2, c % 2
        m = {
            "x_own": np.ascontiguousarray(x[b, 2048 * h:2048 * (h + 1)]),
            "x_sib": np.ascontiguousarray(x[b, 2048 * (1 - h):2048 * (2 - h)]),
        }
        for dn, kn in WEIGHT_MAP:
            m[dn] = np.ascontiguousarray(np.asarray(inputs[kn], dtype=np.float32))
        in_maps.append(m)
    return in_maps


def kernel(**inputs):
    nc = build_program()
    in_maps = make_in_maps(inputs)
    res = run_bass_kernel_spmd(nc, in_maps, core_ids=list(range(8)))
    out = np.empty((B, S, D), dtype=np.float32)
    for c in range(8):
        b, h = c // 2, c % 2
        out[b, 2048 * h:2048 * (h + 1)] = res.results[c]["out"]
    return out


# version 10
# speedup vs baseline: 2.5614x; 1.0019x over previous
import numpy as np
from contextlib import ExitStack

import concourse.bass as bass
import concourse.mybir as mybir
from concourse.bass_utils import run_bass_kernel_spmd

F32 = mybir.dt.float32
BF16 = mybir.dt.bfloat16
ACTF = mybir.ActivationFunctionType
ALU = mybir.AluOpType
AX = mybir.AxisListType.X

B, S, D = 4, 4096, 1024
TOK = 2048
TOKF = 4096
NCH = 8
H, DH = 16, 64
NW, WSZ = 8, 256
NE, EHD = 16, 256
NT128 = TOK // 128
NT128F = TOKF // 128
EPS = 1e-5
MASKBIG = 1.0e6

ENGINES = ("tensor", "vector", "scalar", "gpsimd", "sync")


class Chan:
    __slots__ = ("sem", "n", "max_waited", "guarded")

    def __init__(self, sem):
        self.sem = sem
        self.n = 0
        self.max_waited = 0
        self.guarded = 0


class Sched:
    def __init__(self, nc, stack):
        self.nc = nc
        self.stack = stack
        self.ops = {e: [] for e in ENGINES}
        self._nsem = 0

    def chan(self, name=None):
        self._nsem += 1
        sem = self.stack.enter_context(self.nc.semaphore(name or f"ch{self._nsem}"))
        return Chan(sem)

    def emit(self, eng, f):
        self.ops[eng].append(f)

    def wait(self, eng, ch, val=None):
        tgt = ch.n if val is None else val
        if tgt > 0:
            ch.max_waited = max(ch.max_waited, tgt)
            self.ops[eng].append(lambda e, s=ch.sem, tv=tgt: e.wait_ge(s, tv))

    def inc(self, eng, f, ch, k=1):
        if k == 16 and ch.max_waited > ch.guarded:
            tgt = ch.max_waited
            self.ops[eng].append(lambda e, s=ch.sem, tv=tgt: e.wait_ge(s, tv))
            ch.guarded = tgt
        self.ops[eng].append(lambda e, ff=f, s=ch.sem, kk=k: ff(e).then_inc(s, kk))
        ch.n += k

    def bounce(self, cons_eng, prod_ch, via_eng, via_ch):
        self.wait(via_eng, prod_ch)
        self.inc(via_eng, lambda e: e.nop(), via_ch, 1)
        self.wait(cons_eng, via_ch)

    def barrier(self):
        ch = self.chan()
        for e in ENGINES:
            self.ops[e].append(lambda eng, s=ch.sem: eng.nop().then_inc(s, 1))
        ch.n += 5
        tgt = ch.n
        for e in ENGINES:
            self.ops[e].append(lambda eng, s=ch.sem, t=tgt: eng.wait_ge(s, t))

    def replay(self):
        with self.nc.Block() as block:
            for ename in ENGINES:
                deco = getattr(block, ename)
                ops = self.ops[ename]

                def body(eng, ops=ops):
                    for f in ops:
                        f(eng)

                deco(body)


def build_program(tap=None, race_check=False, upto=99, ph1_imp=3, ph1_topk=True):
    nc = bass.Bass("TRN2", target_bir_lowering=False,
                   detect_race_conditions=race_check)

    def din(name, shape):
        return nc.dram_tensor(name, list(shape), F32, kind="ExternalInput").ap()

    x_own = din("x_own", (TOK, D))
    x_sib = din("x_sib", (TOK, D))
    Wq = din("Wq", (D, D)); bq = din("bq", (D,))
    Wk = din("Wk", (D, D)); bk = din("bk", (D,))
    Wv = din("Wv", (D, D)); bv = din("bv", (D,))
    Wo = din("Wo", (D, D)); bo = din("bo", (D,))
    Ws1 = din("Ws1", (D, D // 2)); bs1 = din("bs1", (D // 2,))
    Ws2 = din("Ws2", (D // 2, 1))
    Wqkv = din("Wqkv", (3 * D, D)); bqkv = din("bqkv", (3 * D,))
    Wob = din("Wob", (D, D)); bob = din("bob", (D,))
    esel = din("esel", (NE, D))
    keys = din("keys", (NE, D, EHD))
    values = din("values", (NE, EHD, D))
    rsc = din("rsc", (1,))

    out_h = nc.dram_tensor("out", [TOK, D], F32, kind="ExternalOutput").ap()
    dbg_h = None
    if tap is not None:
        dbg_h = nc.dram_tensor("dbg", [128, 8192], F32, kind="ExternalOutput").ap()

    wq_c = nc.dram_tensor("wq_c", [H, D, DH], BF16, kind="Internal").ap()
    wk_c = nc.dram_tensor("wk_c", [H, D, DH], BF16, kind="Internal").ap()
    wv_c = nc.dram_tensor("wv_c", [H, D, DH], BF16, kind="Internal").ap()
    wo_c = nc.dram_tensor("wo_c", [D, D], BF16, kind="Internal").ap()
    wqkvT_c = nc.dram_tensor("wqkvT_c", [24, NCH, 128, 128], BF16, kind="Internal").ap()
    woTb_c = nc.dram_tensor("woTb_c", [NCH, 128, D], BF16, kind="Internal").ap()
    keys_c = nc.dram_tensor("keys_c", [NCH, 128, NE * EHD], BF16, kind="Internal").ap()
    vals_c = nc.dram_tensor("vals_c", [NE * EHD, D], BF16, kind="Internal").ap()
    scr512 = nc.dram_tensor("scr512", [512], F32, kind="Internal").ap()
    scr16 = nc.dram_tensor("scr16", [16], F32, kind="Internal").ap()

    stack = ExitStack()
    sc = Sched(nc, stack)

    def sb(name, shape, dt):
        return stack.enter_context(nc.sbuf_tensor(name, list(shape), dt))

    xres = sb("xres", (128, NT128, D), F32)
    xnF = sb("xnF", (128, NCH, TOK), BF16)
    es_sib = ExitStack()
    es_far = ExitStack()
    bar = sb("bar_", (128, TOKF), BF16)
    w_all = sb("w_all", (128, NT128, NE), F32)
    ones_bf = sb("ones_bf", (128, 128), BF16)
    ident_bf = sb("ident_bf", (128, 128), BF16)
    ident_f = sb("ident_f", (128, 128), F32)
    bqcol = sb("bqcol", (128, NCH), F32)
    bkcol = sb("bkcol", (128, NCH), F32)
    bqkvcol = sb("bqkvcol", (128, 24), F32)
    bvbc = sb("bvbc", (128, D), BF16)
    bs1bc = sb("bs1bc", (128, D // 2), F32)
    ws2bc = sb("ws2bc", (128, D // 2), F32)
    cs1bc = sb("cs1bc", (128, D // 2), F32)
    csebc = sb("csebc", (128, NE), F32)
    rsccol = sb("rsccol", (128, 1), F32)
    eselT = sb("eselT", (128, NCH, NE), F32)
    mucol = sb("mucol", (128, NT128F), F32)
    rscol = sb("rscol", (128, NT128F), F32)
    mrcol = sb("mrcol", (128, NT128F), F32)
    impc = sb("impc", (128, 32), F32)
    impc2 = sb("impc2", (128, 32), F32)
    m16 = sb("m16", (128, 16), F32)
    epscol = sb("epscol", (128, 1), F32)
    sumsq = sb("sumsq", (128, 4), F32)
    musum = sb("musum", (128, 4), F32)
    varcol = sb("varcol", (128, 4), F32)
    stdcol = sb("stdcol", (128, 4), F32)
    xntok = sb("xntok", (128, D), F32)
    xntok2 = sb("xntok2", (128, D), F32)
    mrow = sb("mrow", (1, 16), BF16)
    negKb = sb("negKb", (128, 1), F32)
    rrow = sb("rrow", (65, 512), BF16)

    LD = sc.chan("LD")
    GL = sc.chan("GL")
    GC = sc.chan("GC")
    PE = sc.chan("PE")
    DV = sc.chan("DV")
    SA = sc.chan("SA")

    def mmacc(out_ap, pairs):
        n = len(pairs)
        for i, (l, r) in enumerate(pairs):
            st, sp = (i == 0), (i == n - 1)
            if sp:
                sc.inc("tensor", lambda e, ll=l, rr=r, s=st: e.matmul(
                    out_ap, ll, rr, start=s, stop=True), PE, 1)
            else:
                sc.emit("tensor", lambda e, ll=l, rr=r, s=st: e.matmul(
                    out_ap, ll, rr, start=s, stop=False))

    def attn_stream(kmat, q_ap, vmat, ebuf_t, bA, bB, patt_t, EXPc, AVCc,
                    av0_waits=None, exp_kw=None, mid_emit=None):
        LAG = 5
        ekw = exp_kw or {}
        exp_base = EXPc.n
        avc_base = AVCc.n

        def emit_qk(kc):
            need = exp_base + kc - 6 + 1
            if need > 0:
                sc.wait("tensor", EXPc, need)
            rc = kc % 6
            sg = bA if rc < 3 else bB
            o = (rc % 3) * 512
            sc.inc("tensor", lambda e, k=kc, s=sg, oo=o, q=q_ap: e.matmul(
                s[:, oo:oo + 512], kmat[0:65, 128 * k:128 * (k + 1)], q,
                start=True, stop=True), PE, 1)
            sc.wait("scalar", PE)
            need2 = avc_base + kc - 6 + 1
            if need2 > 0:
                sc.wait("scalar", AVCc, need2)
            sl = 0 if rc < 3 else 1
            sc.inc("scalar", lambda e, s=sg, sl_=sl, oo=o: e.activation(
                ebuf_t[:, sl_, oo:oo + 512], s[:, oo:oo + 512], ACTF.Exp,
                scale=0.125, **ekw), EXPc, 1)

        def emit_av(kc):
            rc = kc % 6
            sl = 0 if rc < 3 else 1
            o = (rc % 3) * 512
            sc.wait("tensor", EXPc, exp_base + kc + 1)
            if kc == 0 and av0_waits is not None:
                av0_waits()
            sc.inc("tensor", lambda e, k=kc, sl_=sl, oo=o: e.matmul(
                patt_t[0:65, 0:512], vmat[:, k, :], ebuf_t[:, sl_, oo:oo + 512],
                start=(k == 0), stop=(k == 31)), AVCc, 1)

        for kc in range(32):
            emit_qk(kc)
            if kc == 4 and mid_emit is not None:
                mid_emit()
            if kc >= LAG:
                emit_av(kc - LAG)
        for kc in range(32 - LAG, 32):
            emit_av(kc)

    for j in range(NT128):
        src = bass.AP(x_own.tensor, j * 128 * D, [[D, 128], [1, D]])
        sc.inc("sync", lambda e, jj=j, s=src: e.dma_start(xres[:, jj, :], s), LD, 16)

    sc.inc("vector", lambda e: e.memset(ones_bf[:, :], 1.0), DV)
    sc.inc("vector", lambda e: e.memset(epscol[:, :], EPS), DV)
    sc.inc("vector", lambda e: e.memset(ident_bf[:, :], 1.0), DV)
    sc.inc("vector", lambda e: e.memset(ident_f[:, :], 1.0), DV)
    sc.wait("gpsimd", DV)
    sc.inc("gpsimd", lambda e: e.affine_select(ident_bf[:, :], ident_bf[:, :], [[1, 128]],
                                               ALU.is_equal, 0.0, channel_multiplier=-1), GC, 1)
    sc.inc("gpsimd", lambda e: e.affine_select(ident_f[:, :], ident_f[:, :], [[1, 128]],
                                               ALU.is_equal, 0.0, channel_multiplier=-1), GC, 1)

    def bcast_col(dst, src_ap, n):
        return lambda e: e.dma_start(dst[:, 0:n], bass.AP(src_ap.tensor, 0, [[1, 128], [128, n]]),
                                 allow_slow_non_contiguous=True)

    def bcast_row(dst, src_ap, n):
        return lambda e: e.dma_start(dst[:, 0:n], bass.AP(src_ap.tensor, 0, [[0, 128], [1, n]]))

    for dst, srct, n in ((bqcol, bq, NCH), (bkcol, bk, NCH), (bqkvcol, bqkv, 24)):
        sc.inc("sync", bcast_col(dst, srct, n), LD, 16)
    for dst, srct, n in ((bs1bc, bs1, D // 2), (ws2bc, Ws2, D // 2), (rsccol, rsc, 1)):
        sc.inc("sync", bcast_row(dst, srct, n), LD, 16)
    sc.inc("gpsimd", bcast_row(bvbc, bv, D), GL, 16)

    sc.inc("gpsimd", lambda e: e.dma_start(wo_c[:, :], Wo[:, :]), GL, 16)
    sc.inc("gpsimd", lambda e: e.dma_start(
        bass.AP(vals_c.tensor, 0, [[D, NE * EHD], [1, D]]),
        bass.AP(values.tensor, 0, [[D, NE * EHD], [1, D]])), GL, 16)
    for e_i in range(NE):
        src = bass.AP(keys.tensor, e_i * D * EHD, [[EHD * 128, NCH], [EHD, 128], [1, EHD]])
        dst = bass.AP(keys_c.tensor, EHD * e_i,
                      [[128 * NE * EHD, NCH], [NE * EHD, 128], [1, EHD]])
        sc.inc("gpsimd", lambda e, s=src, d=dst: e.dma_start(d, s), GL, 16)

    with (nc.sbuf_tensor("tstage", [128, 2, NCH, 128], BF16) as tstage,
          nc.psum_tensor("pp0", [128, 512], F32) as pp0,
          nc.psum_tensor("pp1", [128, 512], F32) as pp1):
        with nc.sbuf_tensor("wstage", [128, NCH, D], BF16) as wstage:
            for (wsrc, wdst) in ((Wq, wq_c), (Wk, wk_c), (Wv, wv_c)):
                for c in range(NCH):
                    src = bass.AP(wsrc.tensor, c * 128 * D, [[D, 128], [1, D]])
                    sc.inc("gpsimd", lambda e, cc=c, s=src: e.dma_start(wstage[:, cc, :], s), GL, 16)
                sc.wait("gpsimd", GL)
                for h in range(H):
                    src = bass.AP(wstage, DH * h, [[NCH * D, 128], [D, NCH], [1, DH]])
                    dst = bass.AP(wdst.tensor, h * D * DH, [[DH, 128], [128 * DH, NCH], [1, DH]])
                    sc.inc("gpsimd", lambda e, s=src, d=dst: e.dma_start(d, s), GL, 16)
                sc.wait("gpsimd", GL)
        from contextlib import ExitStack as _ES
        _prep_es = _ES()
        wstage2 = _prep_es.enter_context(nc.sbuf_tensor("wstage2", [128, 2, NCH, D], F32))

        def transpose_rows_to_cache(n_chunks, row_src_fn, dst_fn):
            sc.wait("tensor", GC)
            sc.wait("sync", GL)
            w2_pe = [0, 0]
            w2_ld = [0, 0]
            ts_dv = [0, 0]

            def w2_dma(jr):
                s = jr % 2
                if w2_pe[s]:
                    sc.wait("sync", PE, w2_pe[s])
                sc.inc("sync", lambda e, sr=row_src_fn(jr), sl=s: e.dma_start(
                    bass.AP(wstage2, sl * NCH * D,
                            [[2 * NCH * D, 128], [128, NCH], [1, 128]]), sr), LD, 16)
                w2_ld[s] = LD.n

            w2_dma(0)
            ev_prev = 0
            st_ld = [0, 0]
            for jr in range(n_chunks):
                s = jr % 2
                if jr + 1 < n_chunks:
                    w2_dma(jr + 1)
                sc.wait("tensor", LD, w2_ld[s])
                if ev_prev:
                    sc.wait("tensor", DV, ev_prev)
                if st_ld[s]:
                    sc.wait("vector", LD, st_ld[s])
                base_dv = DV.n
                for c in range(NCH):
                    bank = pp0 if (c % 2 == 0) else pp1
                    if c >= 2:
                        sc.wait("tensor", DV, base_dv + c - 1)
                    sc.inc("tensor", lambda e, cc=c, b=bank, sl=s: e.transpose(
                        b[:, 0:128],
                        bass.AP(wstage2, sl * NCH * D + cc * 128,
                                [[2 * NCH * D, 128], [1, 128]]),
                        ident_f[:, :]), PE, 1)
                    sc.wait("vector", PE)
                    sc.inc("vector", lambda e, cc=c, b=bank, sl=s: e.tensor_copy(
                        tstage[:, sl, cc, :], b[:, 0:128]), DV, 1)
                w2_pe[s] = PE.n
                ts_dv[s] = DV.n
                ev_prev = DV.n
                sc.wait("sync", DV, ts_dv[s])
                sc.inc("sync", lambda e, d=dst_fn(jr), sl=s: e.dma_start(
                    d, bass.AP(tstage, sl * NCH * 128,
                               [[2 * NCH * 128, 128], [128, NCH], [1, 128]])), LD, 16)
                st_ld[s] = LD.n

        transpose_rows_to_cache(
            24,
            lambda jr: bass.AP(Wqkv.tensor, jr * 128 * D, [[D, 128], [1, D]]),
            lambda jr: bass.AP(wqkvT_c.tensor, jr * NCH * 128 * 128,
                               [[128, 128], [128 * 128, NCH], [1, 128]]))
        transpose_rows_to_cache(
            NCH,
            lambda jr: bass.AP(Wob.tensor, jr * 128 * D, [[D, 128], [1, D]]),
            lambda jr: bass.AP(woTb_c.tensor, jr * 128, [[D, 128], [128 * D, NCH], [1, 128]]))

        with nc.sbuf_tensor("estage", [16, D], F32) as estage, \
             nc.sbuf_tensor("onef", [128, 1], F32) as onef:
            sc.inc("sync", lambda e: e.dma_start(estage[:, :], esel[:, :]), LD, 16)
            sc.wait("tensor", LD)
            base_dv = DV.n
            for c in range(NCH):
                bank = pp0 if (c % 2 == 0) else pp1
                if c >= 2:
                    sc.wait("tensor", DV, base_dv + c - 1)
                sc.inc("tensor", lambda e, cc=c, b=bank: e.transpose(
                    b[0:128, 0:16], estage[:, cc * 128:(cc + 1) * 128],
                    ident_f[0:16, 0:16]), PE, 1)
                sc.wait("vector", PE)
                sc.inc("vector", lambda e, cc=c, b=bank: e.tensor_copy(
                    eselT[:, cc, :], b[0:128, 0:16]), DV, 1)
            sc.inc("vector", lambda e: e.memset(onef[:, :], 1.0), DV)
            sc.wait("tensor", DV)
            mmacc(pp0[0:1, 0:16], [(onef[:, :], eselT[:, c, :]) for c in range(NCH)])
            sc.wait("vector", PE)
            sc.inc("vector", lambda e: e.tensor_copy(estage[0:1, 0:16], pp0[0:1, 0:16]), DV, 1)
            sc.wait("sync", DV)
            sc.inc("sync", lambda e: e.dma_start(scr16[:], estage[0:1, 0:16]), LD, 16)
            sc.wait("sync", LD)
            sc.inc("sync", lambda e: e.dma_start(
                csebc[:, :], bass.AP(scr16.tensor, 0, [[0, 128], [1, NE]])), LD, 16)

        _prep_es.close()

    sibF = es_sib.enter_context(nc.sbuf_tensor("sibF", [128, NCH, TOK], BF16))
    far = es_far.enter_context(nc.sbuf_tensor("far", [128, TOKF + 64], F32))

    sc.barrier()

    def ln_pair(xts, jfs, pbanks, extra_fn=None, xnbufs=None):
        xnbuf = xnbufs or (xntok, xntok2)
        sc.wait("scalar", DV)
        for i, (xt, jf) in enumerate(zip(xts, jfs)):
            sc.inc("scalar", lambda e, x=xt, ii=i: e.activation(
                xnbuf[ii][:, :], x, ACTF.Square, accum_out=sumsq[:, ii:ii + 1]), SA, 1)
            sc.inc("vector", lambda e, x=xt, ii=i: e.reduce_sum(
                musum[:, ii:ii + 1], x, axis=AX), DV, 1)
        sc.wait("scalar", DV)
        for i, (xt, jf) in enumerate(zip(xts, jfs)):
            sc.inc("scalar", lambda e, f=jf, ii=i: e.activation(
                mucol[:, f:f + 1], musum[:, ii:ii + 1], ACTF.Copy, scale=1.0 / D), SA, 1)
            sc.inc("scalar", lambda e, ii=i: e.activation(
                varcol[:, ii:ii + 1], musum[:, ii:ii + 1], ACTF.Square,
                scale=1.0 / D), SA, 1)
        sc.wait("vector", SA)
        for i in range(len(xts)):
            sc.inc("vector", lambda e, ii=i: e.scalar_tensor_tensor(
                varcol[:, ii:ii + 1], sumsq[:, ii:ii + 1], 1.0 / D, varcol[:, ii:ii + 1],
                op0=ALU.mult, op1=ALU.subtract), DV, 1)
        sc.wait("scalar", DV)
        for i in range(len(xts)):
            sc.inc("scalar", lambda e, ii=i: e.activation(
                stdcol[:, ii:ii + 1], varcol[:, ii:ii + 1], ACTF.Sqrt,
                bias=epscol[:, :]), SA, 1)
        sc.wait("vector", SA)
        for i, (xt, jf) in enumerate(zip(xts, jfs)):
            sc.inc("vector", lambda e, f=jf, ii=i: e.reciprocal(
                rscol[:, f:f + 1], stdcol[:, ii:ii + 1]), DV, 1)
        sc.bounce("vector", DV, "scalar", SA)
        for i, (xt, jf) in enumerate(zip(xts, jfs)):
            sc.inc("vector", lambda e, f=jf: e.tensor_tensor(
                mrcol[:, f:f + 1], mucol[:, f:f + 1], rscol[:, f:f + 1],
                op=ALU.mult), DV, 1)
            sc.inc("vector", lambda e, x=xt, f=jf, ii=i: e.tensor_scalar(
                xnbuf[ii][:, :], x, mucol[:, f:f + 1], rscol[:, f:f + 1],
                op0=ALU.subtract, op1=ALU.mult), DV, 1)
        for i, (xt, jf) in enumerate(zip(xts, jfs)):
            sc.wait("tensor", DV)
            if jf < NT128:
                base_dv = DV.n
                nb = len(pbanks)
                for c in range(NCH):
                    bank = pbanks[c % nb]
                    if c >= nb:
                        sc.wait("tensor", DV, base_dv + c - nb + 1)
                    sc.inc("tensor", lambda e, cc=c, b=bank, ii=i: e.transpose(
                        b[:, 0:128], xnbuf[ii][:, cc * 128:(cc + 1) * 128],
                        ident_f[:, :]), PE, 1)
                    sc.wait("vector", PE)
                    sc.inc("vector", lambda e, cc=c, b=bank, f=jf: e.tensor_copy(
                        xnF[:, cc, 128 * f:128 * (f + 1)], b[:, 0:128]), DV, 1)
            if extra_fn is not None:
                extra_fn(jf, xt, i)

    def xf_stage_fp32(xt, xFst, pbanks):
        sc.wait("tensor", DV)
        base_dv = DV.n
        nb = len(pbanks)
        for c in range(NCH):
            bank = pbanks[c % nb]
            if c >= nb:
                sc.wait("tensor", DV, base_dv + c - nb + 1)
            src = bass.AP(xt.tensor, xt.offset + c * 128, [[xt.ap[0][0], 128], [1, 128]])
            sc.inc("tensor", lambda e, s=src, b=bank: e.transpose(
                b[:, 0:128], s, ident_f[:, :]), PE, 1)
            sc.wait("vector", PE)
            sc.inc("vector", lambda e, cc=c, b=bank: e.tensor_copy(
                xFst[:, cc, :], b[:, 0:128]), DV, 1)
        sc.wait("tensor", DV)

    if upto >= 1:
      ph1_es = ExitStack()
      ph1_es2 = ExitStack()
      ws1sb = ph1_es.enter_context(nc.sbuf_tensor("ws1sb", [128, NCH, D // 2], F32))
      xsib_stage = ph1_es.enter_context(nc.sbuf_tensor("xsib_stage", [128, 3, D], F32))
      xFst = ph1_es.enter_context(nc.sbuf_tensor("xFst", [128, NCH, 128], F32))
      z1buf = ph1_es.enter_context(nc.sbuf_tensor("z1buf", [128, D // 2], F32))
      with (nc.psum_tensor("pt0", [128, 512], F32) as pt0,
          nc.psum_tensor("pt1", [128, 512], F32) as pt1,
          nc.psum_tensor("pt2", [128, 512], F32) as pt2,
          nc.psum_tensor("pt3", [128, 512], F32) as pt3,
          nc.psum_tensor("pz", [128, 512], F32) as pz):

        for c in range(NCH):
            src = bass.AP(Ws1.tensor, c * 128 * (D // 2), [[D // 2, 128], [1, D // 2]])
            sc.inc("sync", lambda e, cc=c, s=src: e.dma_start(ws1sb[:, cc, :], s), LD, 16)
        with nc.sbuf_tensor("onef2", [128, 1], F32) as onef2:
            sc.inc("vector", lambda e: e.memset(onef2[:, :], 1.0), DV)
            sc.wait("tensor", LD)
            sc.wait("tensor", DV)
            mmacc(pz[0:1, 0:512], [(onef2[:, :], ws1sb[:, c, :]) for c in range(NCH)])
            sc.wait("vector", PE)
            sc.inc("vector", lambda e: e.tensor_copy(far[0:1, 0:512], pz[0:1, 0:512]), DV, 1)
            sc.wait("sync", DV)
            sc.inc("sync", lambda e: e.dma_start(scr512[:], far[0:1, 0:512]), LD, 16)
            sc.wait("sync", LD)
            sc.inc("sync", lambda e: e.dma_start(
                cs1bc[:, :], bass.AP(scr512.tensor, 0, [[0, 128], [1, 512]])), LD, 16)
            sc.wait("vector", LD)
            sc.wait("tensor", LD)

        ph1_banks = [pt0, pt1, pt2, pt3]

        def imp_extra(jf, xt, i=0):
            xf_stage_fp32(xt, xFst, ph1_banks)
            if ph1_imp < 2:
                return
            mmacc(pz[:, 0:512], [(xFst[:, c, :], ws1sb[:, c, :]) for c in range(NCH)])
            sc.wait("vector", PE)
            if ph1_imp < 3:
                sc.inc("vector", lambda e: e.tensor_copy(z1buf[:, :], pz[:, 0:512]), DV, 1)
                sc.wait("tensor", DV)
                return
            sc.inc("vector", lambda e, f=jf: e.scalar_tensor_tensor(
                z1buf[:, :], cs1bc[:, :], mrcol[:, f:f + 1], bs1bc[:, :],
                op0=ALU.mult, op1=ALU.subtract), DV, 1)
            sc.inc("vector", lambda e, f=jf: e.scalar_tensor_tensor(
                z1buf[:, :], pz[:, 0:512], rscol[:, f:f + 1], z1buf[:, :],
                op0=ALU.mult, op1=ALU.subtract), DV, 1)
            sc.inc("vector", lambda e: e.scalar_tensor_tensor(
                z1buf[:, :], z1buf[:, :], 0.0, ws2bc[:, :],
                op0=ALU.max, op1=ALU.mult), DV, 1)
            sc.inc("vector", lambda e, f=jf: e.reduce_sum(
                impc[:, f:f + 1], z1buf[:, :], axis=AX), DV, 1)
            sc.wait("tensor", DV)

        sc.wait("vector", LD)
        sc.wait("scalar", LD)
        sc.wait("tensor", LD)
        for j in range(0, NT128, 2):
            ln_pair((xres[:, j, :], xres[:, j + 1, :]), (j, j + 1), ph1_banks,
                    imp_extra if ph1_imp >= 1 else None)

        def sib_extra(jf, xt, i):
            js = jf - NT128
            xnb = xntok if i == 0 else xntok2
            base_dv = DV.n
            nb = len(ph1_banks)
            for c in range(NCH):
                bank = ph1_banks[c % nb]
                if c >= nb:
                    sc.wait("tensor", DV, base_dv + c - nb + 1)
                sc.inc("tensor", lambda e, cc=c, b=bank, xb=xnb: e.transpose(
                    b[:, 0:128], xb[:, cc * 128:(cc + 1) * 128], ident_f[:, :]), PE, 1)
                sc.wait("vector", PE)
                sc.inc("vector", lambda e, cc=c, b=bank, j=js: e.tensor_copy(
                    sibF[:, cc, 128 * j:128 * (j + 1)], b[:, 0:128]), DV, 1)
            imp_extra(jf, xt)

        sib_done = [(0, 0, 0)] * 3
        sib_ld_at = [0] * 3

        def sib_dma(j):
            s = j % 3
            dv0, pe0, sa0 = sib_done[s]
            sc.wait("sync", DV, dv0)
            sc.wait("sync", PE, pe0)
            sc.wait("sync", SA, sa0)
            src = bass.AP(x_sib.tensor, j * 128 * D, [[D, 128], [1, D]])
            sc.inc("sync", lambda e, sr=src, sl=s: e.dma_start(
                xsib_stage[:, sl, :], sr), LD, 16)
            sib_ld_at[s] = LD.n

        sib_dma(0)
        sib_dma(1)
        for j in range(0, NT128, 2):
            s0, s1 = j % 3, (j + 1) % 3
            if j + 2 < NT128:
                sib_dma(j + 2)
            for eng in ("vector", "scalar", "tensor"):
                sc.wait(eng, LD, max(sib_ld_at[s0], sib_ld_at[s1]))
            ln_pair((xsib_stage[:, s0, :], xsib_stage[:, s1, :]),
                    (NT128 + j, NT128 + j + 1), ph1_banks,
                    sib_extra if ph1_imp >= 1 else None)
            sib_done[s0] = (DV.n, PE.n, SA.n)
            sib_done[s1] = (DV.n, PE.n, SA.n)
            if j + 3 < NT128:
                sib_dma(j + 3)

        ph1_es.close()
        candbuf = ph1_es2.enter_context(nc.sbuf_tensor("candbuf", [128, 2304], F32))
        if not ph1_topk:
            sc.inc("vector", lambda e: e.memset(bar[0:1, 0:TOKF], 1.0), DV, 1)
        ph1_topk_gate = ph1_topk
        if ph1_topk_gate:
          sc.bounce("vector", DV, "scalar", SA)
          sc.inc("vector", lambda e: e.tensor_copy(impc2[:, :], impc[:, :]), DV, 1)
          sc.bounce("vector", DV, "scalar", SA)
          sc.inc("vector", lambda e: e.max(out=m16[:, 0:8], in_=impc2[:, :]), DV, 1)
          sc.bounce("vector", DV, "scalar", SA)
          sc.inc("vector", lambda e: e.match_replace(
            out=impc2[:, :], in_to_replace=m16[:, 0:8],
            in_values=impc2[:, :], imm_value=-1e30), DV, 1)
          sc.bounce("vector", DV, "scalar", SA)
          sc.inc("vector", lambda e: e.max(out=m16[:, 8:16], in_=impc2[:, :]), DV, 1)
          sc.wait("tensor", DV)
          sc.inc("tensor", lambda e: e.transpose(pt0[0:32, 0:128], impc[:, :],
                                               ident_f[:, :]), PE, 1)
          sc.wait("vector", PE)
          sc.inc("vector", lambda e: e.tensor_copy(candbuf[0:32, 2176:2304],
                                                 pt0[0:32, 0:128]), DV, 1)
          sc.wait("sync", DV)
          sc.inc("sync", lambda e: e.dma_start(
            far[0:1, 0:TOKF],
            bass.AP(candbuf, 2176, [[2304, 32], [1, 128]])), LD, 16)
          sc.inc("sync", lambda e: e.dma_start(
            candbuf[0:1, 0:2048],
            bass.AP(m16, 0, [[16, 128], [1, 16]])), LD, 16)
          sc.wait("vector", LD)
          sc.inc("vector", lambda e: e.max(out=candbuf[0:1, 2048:2056], in_=candbuf[0:1, 0:2048]), DV, 1)
          sc.bounce("vector", DV, "scalar", SA)
          sc.inc("vector", lambda e: e.match_replace(
            out=candbuf[0:1, 0:2048], in_to_replace=candbuf[0:1, 2048:2056],
            in_values=candbuf[0:1, 0:2048], imm_value=-1e30), DV, 1)
          sc.bounce("vector", DV, "scalar", SA)
          sc.inc("vector", lambda e: e.max(out=candbuf[0:1, 2056:2064], in_=candbuf[0:1, 0:2048]), DV, 1)
          sc.bounce("vector", DV, "scalar", SA)
          sc.inc("vector", lambda e: e.tensor_copy(
            far[0:1, TOKF + 8:TOKF + 9], candbuf[0:1, 2063:2064]), DV, 1)
          sc.bounce("vector", DV, "scalar", SA)
          sc.inc("vector", lambda e: e.tensor_scalar(
            bar[0:1, 0:TOKF], far[0:1, 0:TOKF], far[0:1, TOKF + 8:TOKF + 9], None,
            op0=ALU.is_lt), DV, 1)
        ph1_es2.close()

    sc.barrier()

    if tap == "ph1":
        TP = sc.chan("TP")
        if ph1_imp >= 3:
            sc.inc("sync", lambda e: e.dma_start(dbg_h[0:128, 0:32], impc[:, :]), TP, 16)
        if ph1_topk:
            sc.inc("sync", lambda e: e.dma_start(dbg_h[0:1, 100:101],
                                                 far[0:1, TOKF + 8:TOKF + 9]), TP, 16)
        sc.inc("sync", lambda e: e.dma_start(dbg_h[0:128, 256:288], mucol[:, 0:32]), TP, 16)
        sc.inc("sync", lambda e: e.dma_start(dbg_h[0:128, 288:320], rscol[:, 0:32]), TP, 16)
        if ph1_topk:
            TPg = sc.chan("TPg")
            sc.inc("gpsimd", lambda e: e.dma_start(dbg_h[0:1, 320:320 + 4096],
                                                   bar[0:1, 0:TOKF]), TPg, 16)
            sc.inc("sync", lambda e: e.dma_start(dbg_h[1:2, 320:320 + 4096],
                                                 far[0:1, 0:TOKF]), TP, 16)
            sc.wait("gpsimd", TPg)
        sc.wait("sync", TP)

    es_far.close()

    if upto >= 2:
      with (nc.sbuf_tensor("whq", [128, 2, NCH, DH], BF16) as whq,
          nc.sbuf_tensor("whk", [128, 2, NCH, DH], BF16) as whk,
          nc.sbuf_tensor("whv", [128, 2, NCH, DH], BF16) as whv,
          nc.sbuf_tensor("khat", [65, TOKF], BF16) as khat,
          nc.sbuf_tensor("qhat", [65, TOK], BF16) as qhat,
          nc.sbuf_tensor("vhat", [128, 32, 65], BF16) as vhat,
          nc.sbuf_tensor("attp", [128, TOK], BF16) as attp,
          nc.sbuf_tensor("ostage", [64, TOK], BF16) as ostage,
          nc.sbuf_tensor("ebuf", [128, 2, 1536], BF16) as ebuf,
          nc.sbuf_tensor("astage", [64, 512], BF16) as astage,
          nc.sbuf_tensor("woap", [128, D], BF16) as woap,
          nc.sbuf_tensor("boabc", [128, D], BF16) as boabc,
          nc.psum_tensor("sgA", [128, 1536], F32) as sgA,
          nc.psum_tensor("sgB", [128, 1536], F32) as sgB,
          nc.psum_tensor("patt", [128, 512], F32) as patt,
          nc.psum_tensor("prbc", [128, 512], F32) as prbc):

        WD = sc.chan("WD")
        SIBD = sc.chan("SIBD")
        UD = sc.chan("UD")
        MD = sc.chan("MD")
        OD = sc.chan("OD")
        EXP = sc.chan("EXP")
        AVC = sc.chan("AVC")
        kc_groups = [(0, 3), (3, 6), (6, 9), (9, 12), (12, 15), (15, 18), (18, 21),
                     (21, 24), (24, 27), (27, 30), (30, 32)]
        head_pe_done = 0
        pair_y_done = 0

        B2D = sc.chan("B2D")
        sc.inc("gpsimd", lambda e: e.dma_start(
            boabc[:, 0:D], bass.AP(bo.tensor, 0, [[0, 128], [1, D]])), B2D, 16)
        sc.wait("vector", B2D)
        hw_pe = [0, 0]
        hw_wd = [0, 0]

        def head_w_dma(h):
            s = h % 2
            sc.wait("sync", PE, hw_pe[s])
            for (cache, buf) in ((wq_c, whq), (wk_c, whk), (wv_c, whv)):
                src = bass.AP(cache.tensor, h * D * DH,
                              [[DH, 128], [128 * DH, NCH], [1, DH]])
                sc.inc("sync", lambda e, sr=src, b=buf, sl=s: e.dma_start(
                    bass.AP(b, sl * NCH * DH,
                            [[2 * NCH * DH, 128], [DH, NCH], [1, DH]]), sr), WD, 16)
            hw_wd[s] = WD.n

        head_w_dma(0)
        for h in range(H):
            hb = 64 * (h % 2)
            hc = h // 2
            hs = h % 2
            if h + 1 < H:
                head_w_dma(h + 1)
            sc.wait("tensor", WD, hw_wd[hs])
            if h % 2 == 1:
                sc.wait("sync", PE, pair_y_done)
                wsrc = bass.AP(wo_c.tensor, hc * 128 * D, [[D, 128], [1, D]])
                sc.inc("sync", lambda e, s=wsrc: e.dma_start(woap[:, :], s), WD, 16)
                woap_wd = WD.n

            sc.inc("vector", lambda e: e.memset(vhat[:, :, 64:65], 1.0), DV, 1)
            kbanks = [sgA[:, 0:512], sgA[:, 512:1024], sgA[:, 1024:1536],
                      sgB[:, 0:512]]
            vbanks = [sgB[:, 512:1024], sgB[:, 1024:1536]]
            k_ev = [0, 0, 0, 0]
            v_ev = [0, 0]
            for kt in range(8):
                def xn_sl(c, off, width, kt=kt):
                    if kt < 4:
                        return xnF[:, c, 512 * kt + off:512 * kt + off + width]
                    o = 512 * (kt - 4) + off
                    return sibF[:, c, o:o + width]

                ks = kt % 4
                if k_ev[ks]:
                    sc.wait("tensor", DV, k_ev[ks])
                if kt == 0:
                    sc.wait("tensor", DV)
                    if head_pe_done:
                        sc.wait("vector", PE, head_pe_done)
                mmacc(kbanks[ks][0:64, 0:512],
                      [(whk[:, hs, c, :], xn_sl(c, 0, 512)) for c in range(NCH)])
                sc.wait("vector", PE)
                sc.inc("vector", lambda e, k=kt, b=kbanks[ks]: e.tensor_scalar(
                    khat[0:64, 512 * k:512 * (k + 1)], b[0:64, 0:512],
                    bkcol[hb:hb + 64, hc:hc + 1], None, op0=ALU.add), DV, 1)
                k_ev[ks] = DV.n
                for q in range(4):
                    tt = kt * 4 + q
                    vsl = tt % 2
                    if v_ev[vsl]:
                        sc.wait("tensor", DV, v_ev[vsl])
                    mmacc(vbanks[vsl][:, 0:64],
                          [(xn_sl(c, 128 * q, 128), whv[:, hs, c, :]) for c in range(NCH)])
                    sc.wait("vector", PE)
                    sc.inc("vector", lambda e, t=tt, b=vbanks[vsl]: e.tensor_tensor(
                        vhat[:, t, 0:64], b[:, 0:64],
                        bvbc[:, DH * h:DH * h + DH], op=ALU.add), DV, 1)
                    v_ev[vsl] = DV.n
            sc.wait("gpsimd", DV)
            sc.inc("gpsimd", lambda e: e.dma_start(khat[64:65, :], bar[0:1, 0:TOKF]), UD, 16)
            sc.wait("vector", UD)
            sc.inc("vector", lambda e: e.tensor_scalar(
                khat[64:65, :], khat[64:65, :], -MASKBIG, None, op0=ALU.mult), DV, 1)

            for qt in range(4):
                ks = qt % 4
                if k_ev[ks]:
                    sc.wait("tensor", DV, k_ev[ks])
                mmacc(kbanks[ks][0:64, 0:512],
                      [(whq[:, hs, c, :], xnF[:, c, 512 * qt:512 * (qt + 1)])
                       for c in range(NCH)])
                sc.wait("vector", PE)
                sc.inc("vector", lambda e, q=qt, b=kbanks[ks]: e.tensor_scalar(
                    qhat[0:64, 512 * q:512 * (q + 1)], b[0:64, 0:512],
                    bqcol[hb:hb + 64, hc:hc + 1], None, op0=ALU.add), DV, 1)
                k_ev[ks] = DV.n
            hw_pe[hs] = PE.n
            sc.wait("gpsimd", DV)
            sc.inc("gpsimd", lambda e: e.dma_start(qhat[64:65, :], bar[0:1, 0:TOK]), UD, 16)

            sc.wait("tensor", UD)
            sc.wait("tensor", DV)
            sc.wait("tensor", SA)
            def mk_norm(qt):
                def emit():
                    sc.wait("tensor", DV)
                    sc.inc("tensor", lambda e: e.matmul(
                        prbc[0:64, 0:512], ones_bf[64:65, 0:64], rrow[64:65, 0:512],
                        start=True, stop=True), PE, 1)
                    sc.wait("vector", PE)
                    if h % 2 == 0:
                        dst = attp[0:64, 512 * qt:512 * (qt + 1)]
                    else:
                        dst = ostage[0:64, 512 * qt:512 * (qt + 1)]
                    if pair_y_done:
                        sc.wait("vector", PE, pair_y_done)
                    if h % 2 == 1 and qt == 0:
                        sc.wait("vector", OD)
                    sc.inc("vector", lambda e, d=dst: e.tensor_tensor(
                        d, astage[:, :], prbc[0:64, 0:512], op=ALU.mult), DV, 1)
                return emit

            pending = None
            for qt in range(4):
                q_ap = qhat[0:65, 512 * qt:512 * (qt + 1)]

                def av0w():
                    sc.wait("tensor", SA)
                    sc.wait("tensor", DV)
                attn_stream(khat, q_ap, vhat, ebuf, sgA, sgB, patt, EXP, AVC,
                            av0_waits=av0w, mid_emit=pending)
                sc.wait("vector", AVC)
                sc.inc("vector", lambda e: e.reciprocal(xntok[64:65, 0:512],
                                                        patt[64:65, 0:512]), DV, 1)
                sc.inc("vector", lambda e: e.tensor_copy(rrow[64:65, 0:512],
                                                         xntok[64:65, 0:512]), DV, 1)
                sc.inc("vector", lambda e: e.tensor_copy(astage[:, :],
                                                         patt[0:64, 0:512]), DV, 1)
                pending = mk_norm(qt)
            pending()
            head_pe_done = PE.n

            if h % 2 == 1:
                sc.wait("gpsimd", DV)
                sc.inc("gpsimd", lambda e: e.dma_start(
                    bass.AP(attp, 64 * TOK, [[TOK, 64], [1, TOK]]),
                    bass.AP(ostage, 0, [[TOK, 64], [1, TOK]])), OD, 16)
                sc.wait("tensor", OD)
                sc.wait("tensor", WD, woap_wd)
                sc.wait("tensor", DV)
                dv_base = DV.n
                i = 0
                for tt in range(NT128):
                    for jh in range(2):
                        bank = sgA if (i % 2 == 0) else sgB
                        if i >= 2:
                            sc.wait("tensor", DV, dv_base + i - 1)
                        sc.inc("tensor", lambda e, t=tt, j=jh, b=bank: e.matmul(
                            b[:, 0:512], attp[:, 128 * t:128 * (t + 1)],
                            woap[:, 512 * j:512 * (j + 1)], start=True, stop=True), PE, 1)
                        sc.wait("vector", PE)
                        sc.inc("vector", lambda e, t=tt, j=jh, b=bank: e.tensor_tensor(
                            xres[:, t, 512 * j:512 * (j + 1)],
                            xres[:, t, 512 * j:512 * (j + 1)], b[:, 0:512], op=ALU.add), DV, 1)
                        i += 1
                pair_y_done = PE.n

        for tt in range(NT128):
            sc.inc("vector", lambda e, t=tt: e.tensor_tensor(
                xres[:, t, :], xres[:, t, :], boabc[:, :], op=ALU.add), DV, 1)

    sc.barrier()
    es_sib.close()

    if tap == "ph2":
        TP = sc.chan("TP")
        for t in range(8):
            sc.inc("sync", lambda e, tt=t: e.dma_start(
                dbg_h[0:128, 1024 * tt:1024 * (tt + 1)], xres[:, tt, :]), TP, 16)
        sc.wait("sync", TP)

    def ffn_pass(tag):
        with (nc.sbuf_tensor("xnt3" + tag, [128, D], F32) as xnt3,
              nc.sbuf_tensor("xnt4" + tag, [128, D], F32) as xnt4,
              nc.sbuf_tensor("xFst2" + tag, [128, NCH, 128], F32) as xFst2,
              nc.sbuf_tensor("probs" + tag, [128, NE], F32) as probs,
              nc.sbuf_tensor("top8" + tag, [128, 8], F32) as top8,
              nc.sbuf_tensor("wsum" + tag, [128, NT128], F32) as wsum,
              nc.sbuf_tensor("wsr" + tag, [128, NT128], F32) as wsr,
              nc.sbuf_tensor("stok" + tag, [128, 2, 512], F32) as stok,
              nc.sbuf_tensor("sF" + tag, [128, 32, 512], BF16) as sF,
              nc.sbuf_tensor("kbuf" + tag, [128, 3, NCH, 512], BF16) as kbuf,
              nc.sbuf_tensor("vbuf" + tag, [128, 4, D], BF16) as vbuf,
              nc.psum_tensor("pfA" + tag, [128, 512], F32) as pfA,
              nc.psum_tensor("pfB" + tag, [128, 512], F32) as pfB,
              nc.psum_tensor("pfT0" + tag, [128, 512], F32) as pfT0,
              nc.psum_tensor("pfT1" + tag, [128, 512], F32) as pfT1,
              nc.psum_tensor("pval" + tag, [128, 4 * 512], F32) as pval):
            KD = sc.chan("KD" + tag)
            VD = sc.chan("VD" + tag)

            ffn_banks = [pfT0, pfT1, pval[:, 0:512], pval[:, 512:1024]]
            xnquad = (xntok, xntok2, xnt3, xnt4)

            def route_extra(jf, xt, i=0):
                xf_stage_fp32(xt, xFst2, ffn_banks)
                mmacc(pfA[:, 0:16], [(xFst2[:, c, :], eselT[:, c, :]) for c in range(NCH)])
                sc.wait("scalar", DV)
                sc.inc("scalar", lambda e, f=jf: e.activation(
                    probs[:, :], csebc[:, :], ACTF.Copy, scale=mrcol[:, f:f + 1]), SA, 1)
                sc.wait("vector", PE)
                sc.wait("vector", SA)
                sc.inc("vector", lambda e, f=jf: e.scalar_tensor_tensor(
                    probs[:, :], pfA[:, 0:16], rscol[:, f:f + 1], probs[:, :],
                    op0=ALU.mult, op1=ALU.subtract), DV, 1)
                sc.wait("scalar", DV)
                sc.inc("scalar", lambda e: e.activation(probs[:, :], probs[:, :],
                                                        ACTF.Sigmoid), SA, 1)
                sc.wait("vector", SA)
                sc.inc("vector", lambda e: e.max(out=top8[:, :], in_=probs[:, :]), DV, 1)
                sc.bounce("vector", DV, "scalar", SA)
                sc.inc("vector", lambda e, j=jf: e.scalar_tensor_tensor(
                    w_all[:, j, :], probs[:, :], top8[:, 3:4], probs[:, :],
                    op0=ALU.is_ge, op1=ALU.mult), DV, 1)
                sc.inc("vector", lambda e, j=jf: e.reduce_sum(
                    wsum[:, j:j + 1], top8[:, 0:4], axis=AX), DV, 1)
                sc.wait("tensor", DV)

            sc.wait("tensor", DV)
            for j in range(0, NT128, 4):
                ln_pair(tuple(xres[:, j + i, :] for i in range(4)),
                        tuple(j + i for i in range(4)),
                        ffn_banks, route_extra, xnbufs=xnquad)

            sc.bounce("vector", DV, "scalar", SA)
            sc.inc("vector", lambda e: e.reciprocal(wsr[:, :], wsum[:, :]), DV, 1)
            sc.bounce("vector", DV, "scalar", SA)
            for j in range(NT128):
                sc.inc("vector", lambda e, jj=j: e.tensor_scalar(
                    w_all[:, jj, :], w_all[:, jj, :], wsr[:, jj:jj + 1], None,
                    op0=ALU.mult), DV, 1)

            kslot_pe = [0, 0, 0]
            kd_at = [0, 0, 0]

            def kbuf_dma(idx):
                t5x, eix = idx // 8, idx % 8
                s = idx % 3
                if kslot_pe[s]:
                    sc.wait("sync", PE, kslot_pe[s])
                for c in range(NCH):
                    src = bass.AP(keys_c.tensor, c * 128 * NE * EHD + eix * 512,
                                  [[NE * EHD, 128], [1, 512]])
                    sc.inc("sync", lambda e, cc=c, sr=src, sl=s: e.dma_start(
                        kbuf[:, sl, cc, :], sr), KD, 16)
                kd_at[s] = KD.n

            kbuf_dma(0)
            kbuf_dma(1)
            stok_ev = [0, 0]
            sF_ev = [0]
            pend_T = [None]
            for t5 in range(4):
                for ei in range(8):
                    kidx = t5 * 8 + ei
                    ks = kidx % 3
                    if kidx + 2 < 32:
                        kbuf_dma(kidx + 2)
                    sc.wait("tensor", KD, kd_at[ks])
                    for sub in range(4):
                        t1 = t5 * 4 + sub
                        bank = pfA if (sub % 2 == 0) else pfB
                        if stok_ev[sub % 2]:
                            sc.wait("tensor", DV, stok_ev[sub % 2])
                        mmacc(bank[:, 0:512],
                              [(xnF[:, c, 128 * t1:128 * (t1 + 1)], kbuf[:, ks, c, :])
                               for c in range(NCH)])
                        if pend_T[0] is not None:
                            pend_T[0]()
                            pend_T[0] = None
                        sc.wait("vector", PE)
                        e_lo = ei * 2
                        for eh in range(2):
                            sc.inc("vector", lambda e, b=bank, t=t1, el=e_lo, q=eh,
                                   ss=sub % 2: e.tensor_scalar(
                                       stok[:, ss, 256 * q:256 * (q + 1)],
                                       b[:, 256 * q:256 * (q + 1)],
                                       0.0, w_all[:, t, el + q:el + q + 1],
                                       op0=ALU.max, op1=ALU.mult), DV, 1)
                        stok_ev[sub % 2] = DV.n

                        def mk_T(sub=sub, ei=ei, need=DV.n, ss=sub % 2):
                            def emit():
                                sc.wait("tensor", DV, need)
                                if sF_ev[0]:
                                    sc.wait("tensor", DV, sF_ev[0])
                                for qq in range(4):
                                    g = ei * 4 + qq
                                    bank2 = ffn_banks[qq]
                                    sc.inc("tensor", lambda e, q=qq, b=bank2, s2=ss:
                                           e.transpose(
                                               b[:, 0:128],
                                               stok[:, s2, 128 * q:128 * (q + 1)],
                                               ident_f[:, :]), PE, 1)
                                    sc.wait("vector", PE)
                                    sc.inc("vector", lambda e, g_=g, b=bank2, s_=sub:
                                           e.tensor_copy(
                                               sF[:, g_, 128 * s_:128 * (s_ + 1)],
                                               b[:, 0:128]), DV, 1)
                                sF_ev[0] = DV.n
                            return emit
                        pend_T[0] = mk_T()
                    kslot_pe[ks] = PE.n
                if pend_T[0] is not None:
                    pend_T[0]()
                    pend_T[0] = None
                vslot_free = [0] * 4
                vd_at = [0] * 4

                def val_dma(g):
                    sl = g % 4
                    if vslot_free[sl]:
                        sc.wait("sync", PE, vslot_free[sl])
                    src = bass.AP(vals_c.tensor, g * 128 * D, [[D, 128], [1, D]])
                    sc.inc("sync", lambda e, sl_=sl, s=src: e.dma_start(
                        vbuf[:, sl_, :], s), VD, 16)
                    vd_at[sl] = VD.n

                val_dma(0)
                val_dma(1)
                for g in range(32):
                    slot = g % 4
                    if g + 2 < 32:
                        val_dma(g + 2)
                    sc.wait("tensor", VD, vd_at[slot])
                    if g == 0:
                        sc.wait("tensor", DV)
                    for sub in range(4):
                        for dh in range(2):
                            pi = sub if dh == 0 else None
                            st, sp = (g == 0), (g == 31)
                            if dh == 0:
                                outap = pval[:, 512 * sub:512 * (sub + 1)]
                            else:
                                outap = (pfA if sub == 0 else pfB)[:, 0:512] if sub < 2 else \
                                        (pfT0 if sub == 2 else pfT1)[:, 0:512]
                            lastf = (sub == 3 and dh == 1)
                            f = (lambda e, g_=g, sl=slot, s_=sub, d_=dh, o=outap,
                                 s0=st, s1=sp: e.matmul(
                                o, sF[:, g_, 128 * s_:128 * (s_ + 1)],
                                vbuf[:, sl, 512 * d_:512 * (d_ + 1)],
                                start=s0, stop=s1))
                            if lastf:
                                sc.inc("tensor", f, PE, 1)
                            else:
                                sc.emit("tensor", f)
                    vslot_free[slot] = PE.n
                sc.wait("vector", PE)
                for sub in range(4):
                    t1 = t5 * 4 + sub
                    for dh in range(2):
                        if dh == 0:
                            outap = pval[:, 512 * sub:512 * (sub + 1)]
                        else:
                            outap = (pfA if sub == 0 else pfB)[:, 0:512] if sub < 2 else \
                                    (pfT0 if sub == 2 else pfT1)[:, 0:512]
                        sc.inc("vector", lambda e, t=t1, d=dh, o=outap: e.scalar_tensor_tensor(
                            xres[:, t, 512 * d:512 * (d + 1)], o, rsccol[:, :],
                            xres[:, t, 512 * d:512 * (d + 1)],
                            op0=ALU.mult, op1=ALU.add), DV, 1)
                sc.wait("tensor", DV)

    if upto >= 3:
        ffn_pass("_a")
    sc.barrier()

    if tap == "ph3":
        TP = sc.chan("TP")
        for t in range(8):
            sc.inc("sync", lambda e, tt=t: e.dma_start(
                dbg_h[0:128, 1024 * tt:1024 * (tt + 1)], xres[:, tt, :]), TP, 16)
        sc.wait("sync", TP)

    if upto >= 4:
      with ExitStack() as ph4_es:
        _s = ph4_es.enter_context
        wqkvb = _s(nc.sbuf_tensor("wqkvb", [128, 2, NCH, 128], BF16))
        bqkvvbc = _s(nc.sbuf_tensor("bqkvvbc", [128, NE * DH], BF16))
        xnt3b = _s(nc.sbuf_tensor("xnt3b", [128, D], F32))
        xnt4b = _s(nc.sbuf_tensor("xnt4b", [128, D], F32))
        bobbc = _s(nc.sbuf_tensor("bobbc", [128, D], BF16))
        qkv_w = _s(nc.sbuf_tensor("qkv_w", [128, 24, WSZ], BF16))
        Qhb = _s(nc.sbuf_tensor("Qhb", [65, TOKF], BF16))
        Khb = _s(nc.sbuf_tensor("Khb", [65, TOKF], BF16))
        vhatb = _s(nc.sbuf_tensor("vhatb", [128, 32, 65], BF16))
        vals_w = _s(nc.sbuf_tensor("vals_w", [128, NCH, WSZ], BF16))
        ostb = _s(nc.sbuf_tensor("ostb", [64, TOK], BF16))
        ebufb = _s(nc.sbuf_tensor("ebufb", [128, 2, 1536], BF16))
        astb = _s(nc.sbuf_tensor("astb", [64, 512], BF16))
        wotb = _s(nc.sbuf_tensor("wotb", [128, NCH, 512], BF16))
        bgA = _s(nc.psum_tensor("bgA", [128, 1536], F32))
        bgB = _s(nc.psum_tensor("bgB", [128, 1536], F32))
        pattb = _s(nc.psum_tensor("pattb", [128, 512], F32))
        prbcb = _s(nc.psum_tensor("prbcb", [128, 512], F32))
        WDB = sc.chan("WDB")
        ODB = sc.chan("ODB")
        SDB = sc.chan("SDB")
        EXB = sc.chan("EXB")
        AVB = sc.chan("AVB")

        sc.wait("tensor", DV)
        xnquad4 = (xntok, xntok2, xnt3b, xnt4b)
        for j in range(0, NT128, 4):
            ln_pair(tuple(xres[:, j + i, :] for i in range(4)),
                    tuple(j + i for i in range(4)),
                    [bgA[:, 0:512], bgA[:, 512:1024],
                     bgB[:, 0:512], bgB[:, 512:1024]], None, xnbufs=xnquad4)

        B4D = sc.chan("B4D")
        sc.inc("gpsimd", lambda e: e.dma_start(
            bobbc[:, 0:D], bass.AP(bob.tensor, 0, [[0, 128], [1, D]])), B4D, 16)
        sc.inc("gpsimd", lambda e: e.dma_start(
            bqkvvbc[:, :], bass.AP(bqkv.tensor, 128, [[0, 128], [192, NE], [1, 64]])),
            B4D, 16)
        sc.wait("vector", B4D)
        sc.inc("vector", lambda e: e.memset(Khb[64:65, :], 1.0), DV, 1)
        sc.inc("vector", lambda e: e.memset(Qhb[64:65, :], 0.0), DV, 1)

        vslot = {(192 * c + 128) // 128: (c, (192 * c + 128) % 128) for c in range(NE)}
        kc_groups = [(0, 3), (3, 6), (6, 9), (9, 12), (12, 15), (15, 18), (18, 21),
                     (21, 24), (24, 27), (27, 30), (30, 32)]
        win_pe_done = 0
        slot_pe = [0, 0]
        wdb_at = [0, 0]

        def qkv_dma(jr):
            s = jr % 2
            if slot_pe[s]:
                sc.wait("sync", PE, slot_pe[s])
            src = bass.AP(wqkvT_c.tensor, jr * NCH * 128 * 128,
                          [[128, 128], [128 * 128, NCH], [1, 128]])
            sc.inc("sync", lambda e, sr=src, sl=s: e.dma_start(
                bass.AP(wqkvb, sl * NCH * 128,
                        [[2 * NCH * 128, 128], [128, NCH], [1, 128]]), sr), WDB, 16)
            wdb_at[s] = WDB.n

        qkv_dma(0)
        for w in range(NW):
            wc0 = WSZ * w
            sc.wait("tensor", SA)
            sc.wait("tensor", DV)
            sc.inc("vector", lambda e: e.memset(vhatb[:, :, 64:65], 1.0), DV, 1)
            qkv_ev = [0, 0]
            vh_ev = [0, 0]
            for jr in range(24):
                s = jr % 2
                if jr + 1 < 24:
                    qkv_dma(jr + 1)
                elif w + 1 < NW:
                    qkv_dma(0)
                sc.wait("tensor", WDB, wdb_at[s])
                bank = bgA[:, 512:1024] if (jr % 2 == 0) else bgB[:, 512:1024]
                if qkv_ev[s]:
                    sc.wait("tensor", DV, qkv_ev[s])
                if win_pe_done and jr == 0:
                    sc.wait("vector", PE, win_pe_done)
                mmacc(bank[:, 0:WSZ],
                      [(wqkvb[:, s, c, :], xnF[:, c, wc0:wc0 + WSZ]) for c in range(NCH)])
                sc.wait("vector", PE)
                sc.inc("vector", lambda e, j=jr, b=bank: e.tensor_scalar(
                    qkv_w[:, j, :], b[:, 0:WSZ], bqkvcol[:, j:j + 1], None,
                    op0=ALU.add), DV, 1)
                qkv_ev[s] = DV.n
                if jr in vslot:
                    cv, voff = vslot[jr]
                    for ah in range(2):
                        vb = pattb if ah == 0 else prbcb
                        if vh_ev[ah]:
                            sc.wait("tensor", DV, vh_ev[ah])
                        mmacc(vb[:, 0:64],
                              [(xnF[:, c, wc0 + 128 * ah:wc0 + 128 * ah + 128],
                                wqkvb[:, s, c, voff:voff + 64]) for c in range(NCH)])
                        sc.wait("vector", PE)
                        sc.inc("vector", lambda e, t=2 * cv + ah, b=vb, c_=cv: e.tensor_tensor(
                            vhatb[:, t, 0:64], b[:, 0:64],
                            bqkvvbc[:, 64 * c_:64 * c_ + 64], op=ALU.add), DV, 1)
                        vh_ev[ah] = DV.n
                slot_pe[s] = PE.n

            for kind, dstt in ((0, Qhb), (1, Khb)):
                for c in range(NE):
                    Bidx = 3 * c + kind
                    jc, base = Bidx // 2, 64 * (Bidx % 2)
                    if base == 0:
                        sc.inc("vector", lambda e, j=jc, c_=c, d=dstt: e.tensor_copy(
                            d[0:64, WSZ * c_:WSZ * (c_ + 1)], qkv_w[0:64, j, :]), DV, 1)
                if kind == 0:
                    jc0, cstart = 1, 1
                else:
                    jc0, cstart = 2, 0 if False else 2
                if kind == 0:
                    in_off = 64 * (24 * WSZ) + 1 * WSZ
                    out_off = WSZ * 1
                else:
                    in_off = 64 * (24 * WSZ) + 0 * WSZ
                    out_off = WSZ * 0
                sc.wait("gpsimd", DV)
                sc.inc("gpsimd", lambda e, io=in_off, oo=out_off, d=dstt: e.dma_start(
                    bass.AP(d, oo, [[TOKF, 64], [2 * WSZ, 8], [1, WSZ]]),
                    bass.AP(qkv_w, io, [[24 * WSZ, 64], [3 * WSZ, 8], [1, WSZ]])), SDB, 16)

            sc.wait("vector", SDB)
            nslots = [bgA[0:1, 0:1536], bgB[0:1, 0:1536],
                      pattb[0:1, 0:512], prbcb[0:1, 0:512]]
            for (mat, slot) in ((Khb, 2), (Qhb, 3)):
                sc.inc("vector", lambda e, m=mat: e.tensor_tensor(
                    bar[0:64, 0:TOKF], m[0:64, :], m[0:64, :], op=ALU.mult), DV, 1)
                sc.wait("tensor", DV)
                for kt in range(8):
                    if kt < 3:
                        dst = bgA[0:1, 512 * kt:512 * (kt + 1)]
                    elif kt < 6:
                        dst = bgB[0:1, 512 * (kt - 3):512 * (kt - 2)]
                    elif kt == 6:
                        dst = pattb[0:1, 0:512]
                    else:
                        dst = prbcb[0:1, 0:512]
                    sc.inc("tensor", lambda e, k=kt, d=dst: e.matmul(
                        d, ones_bf[0:64, 0:1],
                        bar[0:64, 512 * k:512 * (k + 1)], start=True, stop=True), PE, 1)
                sc.wait("vector", PE)
                for i, ns in enumerate(nslots):
                    sc.inc("vector", lambda e, s=ns, ii=i: e.reduce_max(
                        mrow[0:1, 8 + ii:9 + ii], s, axis=AX), DV, 1)
                sc.bounce("vector", DV, "scalar", SA)
                sc.inc("vector", lambda e, s=slot: e.reduce_max(
                    mrow[0:1, s:s + 1], mrow[0:1, 8:12], axis=AX), DV, 1)
                sc.wait("tensor", DV)
            sc.wait("scalar", DV)
            for slot in (2, 3):
                sc.inc("scalar", lambda e, s=slot: e.activation(
                    mrow[0:1, s:s + 1], mrow[0:1, s:s + 1], ACTF.Sqrt), SA, 1)
            sc.wait("vector", SA)
            sc.inc("vector", lambda e: e.scalar_tensor_tensor(
                mrow[0:1, 4:5], mrow[0:1, 2:3], -0.125,
                mrow[0:1, 3:4], op0=ALU.mult, op1=ALU.mult), DV, 1)
            sc.bounce("vector", DV, "scalar", SA)
            sc.wait("vector", EXB)
            sc.inc("vector", lambda e: e.tensor_scalar(
                mrow[0:1, 0:1], mrow[0:1, 4:5], 68.0, 0.0,
                op0=ALU.add, op1=ALU.min), DV, 1)
            sc.wait("tensor", DV)
            sc.inc("tensor", lambda e: e.matmul(
                prbcb[:, 0:1], ones_bf[0:1, 0:128], mrow[0:1, 0:1],
                start=True, stop=True), PE, 1)
            sc.wait("vector", PE)
            sc.inc("vector", lambda e: e.tensor_copy(negKb[:, :], prbcb[:, 0:1]), DV, 1)
            sc.wait("tensor", SDB)
            sc.wait("tensor", DV)
            sc.wait("tensor", SA)
            sc.wait("scalar", DV)
            def mk_normb(qt):
                def emit():
                    sc.wait("tensor", DV)
                    sc.inc("tensor", lambda e: e.matmul(
                        prbcb[0:64, 0:512], ones_bf[64:65, 0:64], rrow[64:65, 0:512],
                        start=True, stop=True), PE, 1)
                    sc.wait("vector", PE)
                    if win_pe_done:
                        sc.wait("vector", PE, win_pe_done)
                    sc.inc("vector", lambda e, j=qt: e.tensor_tensor(
                        vals_w[0:64, j, :], astb[:, 0:256], prbcb[0:64, 0:256],
                        op=ALU.mult), DV, 1)
                    if qt == 0:
                        sc.wait("vector", ODB)
                    sc.inc("vector", lambda e, j=qt: e.tensor_tensor(
                        ostb[0:64, WSZ * j:WSZ * (j + 1)], astb[:, 256:512],
                        prbcb[0:64, 256:512], op=ALU.mult), DV, 1)
                return emit

            pendingb = None
            for qt in range(8):
                q_ap = Qhb[0:65, 512 * qt:512 * (qt + 1)]

                def av0wb():
                    sc.wait("tensor", SA)
                    sc.wait("tensor", DV)
                attn_stream(Khb, q_ap, vhatb, ebufb, bgA, bgB, pattb, EXB, AVB,
                            av0_waits=av0wb, exp_kw={"bias": negKb[:, :]},
                            mid_emit=pendingb)
                sc.wait("vector", AVB)
                sc.inc("vector", lambda e: e.reciprocal(xntok[64:65, 0:512],
                                                        pattb[64:65, 0:512]), DV, 1)
                sc.inc("vector", lambda e: e.tensor_copy(rrow[64:65, 0:512],
                                                         xntok[64:65, 0:512]), DV, 1)
                sc.inc("vector", lambda e: e.tensor_copy(astb[:, :],
                                                         pattb[0:64, 0:512]), DV, 1)
                pendingb = mk_normb(qt)
            pendingb()
            sc.wait("gpsimd", DV)
            sc.inc("gpsimd", lambda e: e.dma_start(
                bass.AP(vals_w, 64 * NCH * WSZ, [[NCH * WSZ, 64], [WSZ, NCH], [1, WSZ]]),
                bass.AP(ostb, 0, [[TOK, 64], [WSZ, NCH], [1, WSZ]])), ODB, 16)
            sc.wait("tensor", ODB)
            sc.wait("tensor", DV)
            i = 0
            for jh in range(2):
                sc.wait("sync", PE)
                src = bass.AP(woTb_c.tensor, jh * 512, [[D, 128], [128 * D, NCH], [1, 512]])
                sc.inc("sync", lambda e, s=src: e.dma_start(
                    bass.AP(wotb, 0, [[NCH * 512, 128], [512, NCH], [1, 512]]), s), WDB, 16)
                sc.wait("tensor", WDB)
                for tt in range(2):
                    gt = 2 * w + tt
                    bank = bgA[:, 0:512] if (i % 2 == 0) else bgB[:, 0:512]
                    sc.wait("tensor", DV)
                    mmacc(bank,
                          [(vals_w[:, c, 128 * tt:128 * (tt + 1)],
                            wotb[:, c, :]) for c in range(NCH)])
                    sc.wait("vector", PE)
                    sc.inc("vector", lambda e, t=gt, j=jh, b=bank: e.tensor_tensor(
                        xres[:, t, 512 * j:512 * (j + 1)], xres[:, t, 512 * j:512 * (j + 1)],
                        b, op=ALU.add), DV, 1)
                    i += 1
            win_pe_done = PE.n
        for tt in range(NT128):
            sc.inc("vector", lambda e, t=tt: e.tensor_tensor(
                xres[:, t, :], xres[:, t, :], bobbc[:, :], op=ALU.add), DV, 1)

    sc.barrier()

    if tap == "ph4":
        TP = sc.chan("TP")
        for t in range(8):
            sc.inc("sync", lambda e, tt=t: e.dma_start(
                dbg_h[0:128, 1024 * tt:1024 * (tt + 1)], xres[:, tt, :]), TP, 16)
        sc.wait("sync", TP)

    if upto >= 5:
        ffn_pass("_b")
    sc.barrier()

    sc.wait("sync", DV)
    ST = sc.chan("ST")
    for j in range(NT128):
        dst = bass.AP(out_h.tensor, j * 128 * D, [[D, 128], [1, D]])
        sc.inc("sync", lambda e, jj=j, d=dst: e.dma_start(d, xres[:, jj, :]), ST, 16)
    sc.wait("sync", ST)

    sc.replay()
    stack.close()
    return nc


WEIGHT_MAP = [
    ("Wq", "L0_attn_a_Wq"), ("bq", "L0_attn_a_bq"), ("Wk", "L0_attn_a_Wk"),
    ("bk", "L0_attn_a_bk"), ("Wv", "L0_attn_a_Wv"), ("bv", "L0_attn_a_bv"),
    ("Wo", "L0_attn_a_Wo"), ("bo", "L0_attn_a_bo"), ("Ws1", "L0_attn_a_Ws1"),
    ("bs1", "L0_attn_a_bs1"), ("Ws2", "L0_attn_a_Ws2"),
    ("Wqkv", "L0_attn_b_Wqkv"), ("bqkv", "L0_attn_b_bqkv"),
    ("Wob", "L0_attn_b_Wo"), ("bob", "L0_attn_b_bo"),
    ("esel", "L0_ffn_expert_sel"), ("keys", "L0_ffn_keys"),
    ("values", "L0_ffn_values"), ("rsc", "L0_ffn_route_scale"),
]


def make_in_maps(inputs):
    x = np.ascontiguousarray(np.asarray(inputs["L0_attn_a_x"], dtype=np.float32))
    in_maps = []
    for c in range(8):
        b, h = c // 2, c % 2
        m = {
            "x_own": np.ascontiguousarray(x[b, 2048 * h:2048 * (h + 1)]),
            "x_sib": np.ascontiguousarray(x[b, 2048 * (1 - h):2048 * (2 - h)]),
        }
        for dn, kn in WEIGHT_MAP:
            m[dn] = np.ascontiguousarray(np.asarray(inputs[kn], dtype=np.float32))
        in_maps.append(m)
    return in_maps


def kernel(**inputs):
    nc = build_program()
    in_maps = make_in_maps(inputs)
    res = run_bass_kernel_spmd(nc, in_maps, core_ids=list(range(8)))
    out = np.empty((B, S, D), dtype=np.float32)
    for c in range(8):
        b, h = c // 2, c % 2
        out[b, 2048 * h:2048 * (h + 1)] = res.results[c]["out"]
    return out
